# version 18
# baseline (speedup 1.0000x reference)
import numpy as np
import ml_dtypes
import concourse.bass as bass
import concourse.mybir as mybir
from concourse.bass_utils import run_bass_kernel_spmd

F32 = mybir.dt.float32
BF16 = mybir.dt.bfloat16
AF = mybir.ActivationFunctionType
ALU = mybir.AluOpType
AX = mybir.AxisListType

NCORE = 8
S = 8192
D = 4096
DC = 32
H = 32
DH = 128
OWN = 1024
HALO = 256
SH = OWN + 2 * HALO
NT = SH // 128
NT_OWN = OWN // 128
SCALE = 1.0 / float(np.sqrt(DH))
EPS = 1e-5
NSEL = 7580.0
DEBUG = False
QCOLS = [(0, 256), (0, 512), (256, 512), (512, 512), (768, 256)]


class SemCtr:

    def __init__(self, handle, marks, name, final_pass):
        self.h = handle
        self.v = 0
        self.marks = marks
        self.name = name
        self.final_pass = final_pass

    def inc(self, inst, n=1):
        inst.then_inc(self.h, n)
        self.v += n

    def mark(self, key):
        k = (self.name, key)
        if self.final_pass:
            assert self.marks.get(k) == self.v, (k, self.marks.get(k), self.v)
        else:
            self.marks[k] = self.v

    def wait_mark(self, eng, key):
        v = self.marks.get((self.name, key), 0)
        if v > 0:
            eng.wait_ge(self.h, v)

    def wait_now(self, eng):
        if self.v > 0:
            eng.wait_ge(self.h, self.v)


def r3(t, inner, n_outer=None):
    return t.ap().rearrange("p (a b) -> p a b", b=inner)


def build_program(marks, final_pass):
    nc = bass.Bass("TRN2", target_bir_lowering=False, debug=False)

    x_halo = nc.dram_tensor("x_halo", [SH, D], F32, kind="ExternalInput")
    wq_t = nc.dram_tensor("wq_t", [DC, 128, DC * 128], F32, kind="ExternalInput")
    wo_t = nc.dram_tensor("wo_t", [DC, 128, DC * 128], F32, kind="ExternalInput")
    wf_t = nc.dram_tensor("wf_t", [DC, 128, DC * 128], F32, kind="ExternalInput")
    wk_t = nc.dram_tensor("wk_t", [128, DC * 128], F32, kind="ExternalInput")
    wv_t = nc.dram_tensor("wv_t", [128, DC * 128], F32, kind="ExternalInput")
    a_t = nc.dram_tensor("a_t", [128, DC * 128], F32, kind="ExternalInput")
    b_t = nc.dram_tensor("b_t", [128, DC * 128], F32, kind="ExternalInput")
    rw_t = nc.dram_tensor("rw_t", [128, DC * 8], F32, kind="ExternalInput")
    rb_in = nc.dram_tensor("rb_in", [8, 1], F32, kind="ExternalInput")
    bias_t = nc.dram_tensor("bias_t", [128, DC], F32, kind="ExternalInput")
    cos_in = nc.dram_tensor("cos_in", [128, SH], F32, kind="ExternalInput")
    sin_in = nc.dram_tensor("sin_in", [128, SH], F32, kind="ExternalInput")
    alpha_in = nc.dram_tensor("alpha_in", [128, OWN], BF16, kind="ExternalInput")
    alm1_in = nc.dram_tensor("alm1_in", [128, OWN], BF16, kind="ExternalInput")
    kmask_in = nc.dram_tensor("kmask_in", [128, 256], BF16, kind="ExternalInput")
    wmask_in = nc.dram_tensor("wmask_in", [128, OWN], BF16, kind="ExternalInput")
    ident_in = nc.dram_tensor("ident_in", [128, 128], BF16, kind="ExternalInput")
    ones_in = nc.dram_tensor("ones_in", [1, 128], BF16, kind="ExternalInput")
    eye8_in = nc.dram_tensor("eye8_in", [8, 8], F32, kind="ExternalInput")
    sel2_in = nc.dram_tensor("sel2_in", [128, 8], F32, kind="ExternalInput")
    out = nc.dram_tensor("out", [OWN, D], F32, kind="ExternalOutput")
    cc_in = nc.dram_tensor("cc_in", [128, DC], F32, kind="Internal")
    cc_out = nc.dram_tensor("cc_out", [128, DC], F32, kind="Internal")
    if DEBUG:
        d_xnt = nc.dram_tensor("d_xnt", [128, DC * OWN], BF16, kind="ExternalOutput")
        d_xnh = nc.dram_tensor("d_xnh", [128, DC * 2 * HALO], BF16, kind="ExternalOutput")
        d_kr = nc.dram_tensor("d_kr", [128, SH], BF16, kind="ExternalOutput")
        d_vtok = nc.dram_tensor("d_vtok", [128, NT * 128], BF16, kind="ExternalOutput")
        d_fin = nc.dram_tensor("d_fin", [128, H * OWN], BF16, kind="ExternalOutput")
        d_ao = nc.dram_tensor("d_ao", [128, NT_OWN * D], BF16, kind="ExternalOutput")
        d_xn3t = nc.dram_tensor("d_xn3t", [128, DC * OWN], BF16, kind="ExternalOutput")
        d_rt = nc.dram_tensor("d_rt", [128, DC], F32, kind="ExternalOutput")
        d_qs = nc.dram_tensor("d_qs", [128, DC], F32, kind="ExternalOutput")
        d_rtg = nc.dram_tensor("d_rtg", [1, 8], F32, kind="ExternalOutput")
        d_srt = nc.dram_tensor("d_srt", [128, 1], F32, kind="ExternalOutput")
        d_tsc = nc.dram_tensor("d_tsc", [128, OWN], BF16, kind="ExternalOutput")
        d_ffh0 = nc.dram_tensor("d_ffh0", [128, 4 * D], BF16, kind="ExternalOutput")
        d_ffh1 = nc.dram_tensor("d_ffh1", [128, 4 * D], BF16, kind="ExternalOutput")
        d_xstg = nc.dram_tensor("d_xstg", [128, D], F32, kind="ExternalOutput")
        d_xntok = nc.dram_tensor("d_xntok", [128, D], BF16, kind="ExternalOutput")
        d_stat = nc.dram_tensor("d_stat", [128, 8], F32, kind="ExternalOutput")

    import contextlib
    with contextlib.ExitStack() as es:
        actT = es.enter_context(nc.sbuf_tensor("actT", [128, DC * OWN], BF16))
        kr = es.enter_context(nc.sbuf_tensor("kr", [128, SH], BF16))
        vtok = es.enter_context(nc.sbuf_tensor("vtok", [128, NT * 128], BF16))
        ident = es.enter_context(nc.sbuf_tensor("ident", [128, 128], BF16))
        ones_r = es.enter_context(nc.sbuf_tensor("ones_r", [1, 128], BF16))
        ones_f = es.enter_context(nc.sbuf_tensor("ones_f", [1, 128], F32))
        eye8 = es.enter_context(nc.sbuf_tensor("eye8", [8, 8], F32))
        sel2 = es.enter_context(nc.sbuf_tensor("sel2", [128, 8], F32))
        rwsb = es.enter_context(nc.sbuf_tensor("rwsb", [128, DC * 8], F32))
        rbsb = es.enter_context(nc.sbuf_tensor("rbsb", [8, 1], F32))
        biassb = es.enter_context(nc.sbuf_tensor("biassb", [128, DC], F32))
        sum_b = es.enter_context(nc.sbuf_tensor("sum_b", [128, 1], F32))
        ssq_b = es.enter_context(nc.sbuf_tensor("ssq_b", [128, 1], F32))
        negm_b = es.enter_context(nc.sbuf_tensor("negm_b", [128, 1], F32))
        var_b = es.enter_context(nc.sbuf_tensor("var_b", [128, 1], F32))
        std_b = es.enter_context(nc.sbuf_tensor("std_b", [128, 1], F32))
        rstd_b = es.enter_context(nc.sbuf_tensor("rstd_b", [128, 1], F32))
        tmp1_b = es.enter_context(nc.sbuf_tensor("tmp1_b", [128, 1], F32))
        m2c = es.enter_context(nc.sbuf_tensor("m2c", [128, NT_OWN], F32))
        r2c = es.enter_context(nc.sbuf_tensor("r2c", [128, NT_OWN], F32))
        router_sb = es.enter_context(nc.sbuf_tensor("router_sb", [128, DC], F32))
        qsum_sb = es.enter_context(nc.sbuf_tensor("qsum_sb", [128, DC], F32))
        logit_sb = es.enter_context(nc.sbuf_tensor("logit_sb", [8, 1], F32))
        rtg_sb = es.enter_context(nc.sbuf_tensor("rtg_sb", [1, 8], F32))
        rtmp_sb = es.enter_context(nc.sbuf_tensor("rtmp_sb", [1, 8], F32))
        rmax_sb = es.enter_context(nc.sbuf_tensor("rmax_sb", [1, 1], F32))
        rsum1_sb = es.enter_context(nc.sbuf_tensor("rsum1_sb", [1, 1], F32))
        bc8_sb = es.enter_context(nc.sbuf_tensor("bc8_sb", [128, 8], F32))
        srt_sb = es.enter_context(nc.sbuf_tensor("srt_sb", [128, 1], F32))
        tsc_sb = es.enter_context(nc.sbuf_tensor("tsc_sb", [128, OWN], BF16))
        warm_b = es.enter_context(nc.sbuf_tensor("warm_b", [1, 8], F32))
        _dma_c = es.enter_context(nc.semaphore("dma_c"))
        _dma_x = es.enter_context(nc.semaphore("dma_x"))
        _dma_w = es.enter_context(nc.semaphore("dma_w"))
        _dma_o = es.enter_context(nc.semaphore("dma_o"))
        _s_pe = es.enter_context(nc.semaphore("s_pe"))
        _s_dve = es.enter_context(nc.semaphore("s_dve"))
        _s_act = es.enter_context(nc.semaphore("s_act"))
        _s_pool = es.enter_context(nc.semaphore("s_pool"))
        _s_cc = es.enter_context(nc.semaphore("s_cc"))
        _dx0 = es.enter_context(nc.semaphore("dx0"))
        _dx1 = es.enter_context(nc.semaphore("dx1"))
        _do0 = es.enter_context(nc.semaphore("do0"))
        _do1 = es.enter_context(nc.semaphore("do1"))
        _dw0 = es.enter_context(nc.semaphore("dw0"))
        _dw1 = es.enter_context(nc.semaphore("dw1"))
        _dwm = es.enter_context(nc.semaphore("dwm"))

        dma_c = SemCtr(_dma_c, marks, "dma_c", final_pass)
        dma_x = SemCtr(_dma_x, marks, "dma_x", final_pass)
        dma_w = SemCtr(_dma_w, marks, "dma_w", final_pass)
        dma_o = SemCtr(_dma_o, marks, "dma_o", final_pass)
        pe = SemCtr(_s_pe, marks, "pe", final_pass)
        dve = SemCtr(_s_dve, marks, "dve", final_pass)
        act = SemCtr(_s_act, marks, "act", final_pass)
        pool = SemCtr(_s_pool, marks, "pool", final_pass)
        ccs = SemCtr(_s_cc, marks, "cc", final_pass)
        dxp = [SemCtr(_dx0, marks, "dx0", final_pass), SemCtr(_dx1, marks, "dx1", final_pass)]
        dop = [SemCtr(_do0, marks, "do0", final_pass), SemCtr(_do1, marks, "do1", final_pass)]
        dwp = [SemCtr(_dw0, marks, "dw0", final_pass), SemCtr(_dw1, marks, "dw1", final_pass)]
        dwm = SemCtr(_dwm, marks, "dwm", final_pass)

        with nc.Block("p0") as block:
            @block.sync
            def _(sync):
                for dst, src in [
                    (ident[:], ident_in[:]), (ones_r[:], ones_in[:]),
                    (eye8[:], eye8_in[:]), (sel2[:], sel2_in[:]),
                    (rwsb[:], rw_t[:]), (rbsb[:], rb_in[:]), (biassb[:], bias_t[:]),
                ]:
                    dma_c.inc(sync.dma_start(out=dst, in_=src), 16)
                dma_c.wait_now(sync)

            @block.vector
            def _(vector):
                vector.memset(ones_f[:], 1.0)

            @block.scalar
            def _(scalar):
                dma_c.wait_now(scalar)
                scalar.activation(out=warm_b[:, 0:1], in_=rbsb[0:1, 0:1],
                                  func=AF.Square)
                scalar.activation(out=warm_b[:, 1:2], in_=rbsb[0:1, 0:1],
                                  func=AF.Sqrt)

        es1 = contextlib.ExitStack()
        with es1:
            xnH = es1.enter_context(nc.sbuf_tensor("xnH", [128, DC * 2 * HALO], BF16))
            xstg0 = es1.enter_context(nc.sbuf_tensor("xstg0", [128, D], F32))
            xstg1 = es1.enter_context(nc.sbuf_tensor("xstg1", [128, D], F32))
            scr_act = es1.enter_context(nc.sbuf_tensor("scr_act", [128, D], F32))
            xn_tok0 = es1.enter_context(nc.sbuf_tensor("xn_tok0", [128, D], BF16))
            xn_tok1 = es1.enter_context(nc.sbuf_tensor("xn_tok1", [128, D], BF16))
            cosf = es1.enter_context(nc.sbuf_tensor("cosf", [128, SH], F32))
            sinf = es1.enter_context(nc.sbuf_tensor("sinf", [128, SH], F32))
            xstg = [xstg0, xstg1]
            xn_tok = [xn_tok0, xn_tok1]

            def xnT_dst(t, g):
                if 2 <= t <= 9:
                    return r3(actT, OWN)[:, 4 * g:4 * g + 4,
                                         (t - 2) * 128:(t - 2) * 128 + 128]
                hoff = {0: 0, 1: 128, 10: 256, 11: 384}[t]
                return r3(xnH, 2 * HALO)[:, 4 * g:4 * g + 4, hoff:hoff + 128]

            es2 = contextlib.ExitStack()
            with es2:
                trp0 = es2.enter_context(nc.psum_tensor("trp0", [128, 4 * 1024], BF16))
                trp1 = es2.enter_context(nc.psum_tensor("trp1", [128, 4 * 1024], BF16))
                block = es2.enter_context(nc.Block("p1"))
                trp = [trp0, trp1]

                @block.sync
                def _(sync):
                    dma_c.inc(sync.dma_start(out=cosf[:], in_=cos_in[:]), 16)
                    dma_c.inc(sync.dma_start(out=sinf[:], in_=sin_in[:]), 16)
                    dma_c.mark("cossin")
                    for t in range(NT):
                        if t >= 2:
                            dve.wait_mark(sync, f"p1app{t - 2}")
                        dxp[t % 2].inc(sync.dma_start(
                            out=xstg[t % 2][:],
                            in_=x_halo[t * 128:(t + 1) * 128, :]), 16)
                        dxp[t % 2].mark(f"p1x{t}")
                    if DEBUG:
                        dve.wait_mark(sync, "dbg_t0")
                        dma_o.inc(sync.dma_start(out=d_xstg[:], in_=scr_act[:]), 16)
                        dma_o.inc(sync.dma_start(out=d_stat[:], in_=m2c[:, 0:8]), 16)
                        dma_o.inc(sync.dma_start(out=d_xntok[:], in_=xn_tok[0][:]), 16)
                        dve.wait_mark(sync, f"p1ev{NT - 1}g7")
                        dma_o.inc(sync.dma_start(out=d_xnt[:], in_=actT[:]), 16)
                        dma_o.inc(sync.dma_start(out=d_xnh[:], in_=xnH[:]), 16)

                @block.scalar
                def _(scalar):
                    for t in range(NT):
                        dxp[t % 2].wait_mark(scalar, f"p1x{t}")
                        if t >= 1:
                            dve.wait_mark(scalar, f"p1v{t - 1}")
                        scalar.activation(
                            out=scr_act[:], in_=xstg[t % 2][:], func=AF.Square,
                            accum_out=ssq_b[:])
                        act.inc(scalar.drain())
                        act.mark(f"p1sq{t}")
                        dve.wait_mark(scalar, f"p1var{t}")
                        scalar.activation(
                            out=std_b[:], in_=var_b[:], func=AF.Sqrt)
                        act.inc(scalar.drain())
                        act.mark(f"p1std{t}")

                @block.vector
                def _(vector):
                    for t in range(NT):
                        dxp[t % 2].wait_mark(vector, f"p1x{t}")
                        vector.reduce_sum(out=sum_b[:], in_=xstg[t % 2][:], axis=AX.X)
                        vector.drain()
                        vector.tensor_scalar(
                            out=negm_b[:], in0=sum_b[:], scalar1=-1.0 / D,
                            scalar2=None, op0=ALU.mult)
                        vector.drain()
                        act.wait_mark(vector, f"p1sq{t}")
                        vector.tensor_scalar(
                            out=var_b[:], in0=ssq_b[:], scalar1=1.0 / D,
                            scalar2=EPS, op0=ALU.mult, op1=ALU.add)
                        dve.inc(vector.tensor_tensor(
                            out=tmp1_b[:], in0=negm_b[:], in1=negm_b[:], op=ALU.mult))
                        dve.mark(f"p1v{t}")
                        vector.drain()
                        vector.tensor_tensor(
                            out=var_b[:], in0=var_b[:], in1=tmp1_b[:], op=ALU.subtract)
                        dve.inc(vector.drain())
                        dve.mark(f"p1var{t}")
                        act.wait_mark(vector, f"p1std{t}")
                        vector.reciprocal(out=rstd_b[:], in_=std_b[:])
                        vector.drain()
                        if t >= 2:
                            pe.wait_mark(vector, f"p1tr{t - 2}")
                        dve.inc(vector.tensor_scalar(
                            out=xn_tok[t % 2][:], in0=xstg[t % 2][:],
                            scalar1=negm_b[:], scalar2=rstd_b[:],
                            op0=ALU.add, op1=ALU.mult))
                        dve.mark(f"p1app{t}")
                        if DEBUG and t == 0:
                            vector.tensor_copy(out=m2c[:, 3:4], in_=rstd_b[:])
                            dve.inc(vector.drain())
                            dve.mark("dbg_t0")
                        for g in range(8):
                            pe.wait_mark(vector, f"p1tg{t}g{g}")
                            dve.inc(vector.tensor_copy(
                                out=xnT_dst(t, g),
                                in_=r3(trp[g % 2], 1024)[:, :, 0:128]))
                            dve.mark(f"p1ev{t}g{g}")

                @block.tensor
                def _(tensor):
                    for t in range(NT):
                        dve.wait_mark(tensor, f"p1app{t}")
                        for g in range(8):
                            if 8 * t + g >= 2:
                                pg = 8 * t + g - 2
                                dve.wait_mark(tensor, f"p1ev{pg // 8}g{pg % 8}")
                            for j in range(4):
                                kc = 4 * g + j
                                pe.inc(tensor.transpose(
                                    out=trp[g % 2][:, j * 1024:j * 1024 + 128],
                                    in_=xn_tok[t % 2][:, kc * 128:(kc + 1) * 128],
                                    identity=ident[:]))
                            pe.mark(f"p1tg{t}g{g}")
                        pe.mark(f"p1tr{t}")

            GRP = [(0, 256), (256, 512), (768, 512), (1280, 256)]

            def kv_rhs(kc, gi):
                if gi == 0:
                    return xnH[:, kc * 2 * HALO: kc * 2 * HALO + 256]
                if gi == 1:
                    return actT[:, kc * OWN: kc * OWN + 512]
                if gi == 2:
                    return actT[:, kc * OWN + 512: kc * OWN + 1024]
                return xnH[:, kc * 2 * HALO + 256: kc * 2 * HALO + 512]

            es3 = contextlib.ExitStack()
            with es3:
                wk_sb = es3.enter_context(nc.sbuf_tensor("wk_sb", [128, DC * 128], BF16))
                wv_sb = es3.enter_context(nc.sbuf_tensor("wv_sb", [128, DC * 128], BF16))
                kps2 = es3.enter_context(nc.psum_tensor("kps2", [128, 512], F32))
                vps2 = es3.enter_context(nc.psum_tensor("vps2", [128, 512], F32))
                trp3 = es3.enter_context(nc.psum_tensor("trp3", [128, 4 * 1024], BF16))
                block = es3.enter_context(nc.Block("p2"))
                @block.gpsimd
                def _(gpsimd):
                    dwm.inc(gpsimd.dma_start(out=wk_sb[:], in_=wk_t[:]), 16)
                    dwm.inc(gpsimd.dma_start(out=wv_sb[:], in_=wv_t[:]), 16)
                    dwm.mark("wkv")

                @block.sync
                def _(sync):
                    if DEBUG:
                        dve.wait_mark(sync, "p2kr3")
                        dve.wait_mark(sync, "p2ve3")
                        dma_o.inc(sync.dma_start(out=d_kr[:], in_=kr[:]), 16)
                        dma_o.inc(sync.dma_start(out=d_vtok[:], in_=vtok[:]), 16)

                @block.tensor
                def _(tensor):
                    dwm.wait_mark(tensor, "wkv")
                    dma_c.wait_mark(tensor, "cossin")
                    for gi, (l0, n) in enumerate(GRP):
                        if gi >= 1:
                            dve.wait_mark(tensor, f"p2kr{gi - 1}")
                        for kc in range(DC):
                            pe.inc(tensor.matmul(
                                out=kps2[:, 0:n],
                                lhsT=wk_sb[:, kc * 128:(kc + 1) * 128],
                                rhs=kv_rhs(kc, gi),
                                start=(kc == 0), stop=(kc == DC - 1)))
                        pe.mark(f"p2k{gi}")
                    for gi, (l0, n) in enumerate(GRP):
                        if gi >= 1:
                            dve.wait_mark(tensor, f"p2vs{gi - 1}")
                        for kc in range(DC):
                            pe.inc(tensor.matmul(
                                out=vps2[:, 0:n],
                                lhsT=wv_sb[:, kc * 128:(kc + 1) * 128],
                                rhs=kv_rhs(kc, gi),
                                start=(kc == 0), stop=(kc == DC - 1)))
                        pe.mark(f"p2v{gi}")
                        dve.wait_mark(tensor, f"p2vs{gi}")
                        if gi >= 1:
                            dve.wait_mark(tensor, f"p2ve{gi - 1}")
                        for j in range(n // 128):
                            pe.inc(tensor.transpose(
                                out=trp3[:, j * 1024:j * 1024 + 128],
                                in_=xn_tok0[:, j * 128:(j + 1) * 128],
                                identity=ident[:]))
                        pe.mark(f"p2vt{gi}")

                @block.vector
                def _(vector):
                    for gi, (l0, n) in enumerate(GRP):
                        pe.wait_mark(vector, f"p2k{gi}")
                        vector.tensor_tensor(
                            out=xstg0[:, 0:n], in0=kps2[:, 0:n],
                            in1=cosf[:, l0:l0 + n], op=ALU.mult)
                        vector.tensor_tensor(
                            out=xstg1[0:64, 0:n], in0=kps2[64:128, 0:n],
                            in1=sinf[0:64, l0:l0 + n], op=ALU.mult)
                        vector.tensor_tensor(
                            out=xstg1[64:128, 0:n], in0=kps2[0:64, 0:n],
                            in1=sinf[64:128, l0:l0 + n], op=ALU.mult)
                        dve.inc(vector.tensor_tensor(
                            out=kr[:, l0:l0 + n], in0=xstg0[:, 0:n],
                            in1=xstg1[:, 0:n], op=ALU.add))
                        dve.mark(f"p2kr{gi}")
                    for gi, (l0, n) in enumerate(GRP):
                        pe.wait_mark(vector, f"p2v{gi}")
                        if gi >= 1:
                            pe.wait_mark(vector, f"p2vt{gi - 1}")
                        vector.tensor_copy(
                            out=xn_tok0[:, 0:n], in_=vps2[:, 0:n])
                        dve.inc(vector.drain())
                        dve.mark(f"p2vs{gi}")
                        pe.wait_mark(vector, f"p2vt{gi}")
                        nt_g = n // 128
                        dve.inc(vector.tensor_copy(
                            out=r3(vtok, 128)[:, l0 // 128:l0 // 128 + nt_g, :],
                            in_=r3(trp3, 1024)[:, 0:nt_g, 0:128]))
                        dve.mark(f"p2ve{gi}")

        def prev_hm(h, m):
            return (h, m - 1) if m > 0 else (h - 1, 4)

        es_fin = contextlib.ExitStack()
        fin = es_fin.enter_context(nc.sbuf_tensor("fin", [128, H * OWN], BF16, side="right"))
        es4 = contextlib.ExitStack()
        with es4:
            cosq = es4.enter_context(nc.sbuf_tensor("cosq", [128, OWN], F32))
            sinq = es4.enter_context(nc.sbuf_tensor("sinq", [128, OWN], F32))
            alp = es4.enter_context(nc.sbuf_tensor("alp", [128, OWN], BF16))
            alm = es4.enter_context(nc.sbuf_tensor("alm", [128, OWN], BF16))
            kmask = es4.enter_context(nc.sbuf_tensor("kmask", [128, 256], BF16))
            wqb0 = es4.enter_context(nc.sbuf_tensor("wqb0", [128, D], BF16))
            wqb1 = es4.enter_context(nc.sbuf_tensor("wqb1", [128, D], BF16))
            qr0 = es4.enter_context(nc.sbuf_tensor("qr0", [128, OWN], BF16))
            qr1 = es4.enter_context(nc.sbuf_tensor("qr1", [128, OWN], BF16))
            sg0 = es4.enter_context(nc.sbuf_tensor("sg0", [128, 2048], BF16))
            sg1 = es4.enter_context(nc.sbuf_tensor("sg1", [128, 2048], BF16))
            wout = es4.enter_context(nc.sbuf_tensor("wout", [128, 5 * 512], BF16))
            recip_sb = es4.enter_context(nc.sbuf_tensor("recip_sb", [128, 512], F32))
            rps1 = es4.enter_context(nc.sbuf_tensor("rps1", [128, 512], F32))
            qev = es4.enter_context(nc.sbuf_tensor("qev", [128, 1024], F32))
            qev2 = es4.enter_context(nc.sbuf_tensor("qev2", [128, 1024], F32))
            rps2 = es4.enter_context(nc.sbuf_tensor("rps2", [128, 512], F32))
            bls1 = es4.enter_context(nc.sbuf_tensor("bls1", [128, OWN], F32))
            bls2 = es4.enter_context(nc.sbuf_tensor("bls2", [128, OWN], F32))
            qps = es4.enter_context(nc.psum_tensor("qps", [128, 1024], F32))
            scps = es4.enter_context(nc.psum_tensor("scps", [128, 1024], F32))
            bcps = es4.enter_context(nc.psum_tensor("bcps", [128, 1024], F32))
            avps = es4.enter_context(nc.psum_tensor("avps", [128, 1024], F32))
            block = es4.enter_context(nc.Block("p3"))
            wqb = [wqb0, wqb1]
            qrb = [qr0, qr1]
            sgb = [sg0, sg1]

            @block.sync
            def _(sync):
                dma_c.inc(sync.dma_start(out=cosq[:], in_=cos_in[:, HALO:HALO + OWN]), 16)
                dma_c.inc(sync.dma_start(out=sinq[:], in_=sin_in[:, HALO:HALO + OWN]), 16)
                dma_c.inc(sync.dma_start(out=alp[:], in_=alpha_in[:]), 16)
                dma_c.inc(sync.dma_start(out=alm[:], in_=alm1_in[:]), 16)
                dma_c.inc(sync.dma_start(out=kmask[:], in_=kmask_in[:]), 16)
                dma_c.mark("p3c")
                if DEBUG:
                    pool.wait_mark(sync, f"blend{H - 1}")
                    dma_o.inc(sync.dma_start(out=d_fin[:], in_=fin[:]), 16)

            @block.gpsimd
            def _(gpsimd):
                dma_c.wait_mark(gpsimd, "p3c")

                def rope_pool(hh):
                    for g in range(2):
                        dve.wait_mark(gpsimd, f"qev{hh}g{g}")
                        gpsimd.tensor_tensor(
                            out=rps1[:], in0=qev[:, g * 512:(g + 1) * 512],
                            in1=cosq[:, g * 512:(g + 1) * 512], op=ALU.mult)
                        gpsimd.tensor_tensor(
                            out=rps2[:], in0=qev2[:, g * 512:(g + 1) * 512],
                            in1=sinq[:, g * 512:(g + 1) * 512], op=ALU.mult)
                        if hh >= 1 and g == 0:
                            pe.wait_mark(gpsimd, f"scm{hh - 1}m4p1")
                        gpsimd.tensor_tensor(
                            out=qrb[hh % 2][:, g * 512:(g + 1) * 512],
                            in0=rps1[:], in1=rps2[:], op=ALU.add)
                        pool.inc(gpsimd.drain())
                        pool.mark(f"rope{hh}g{g}")

                for h in range(-2, H):
                    hh = h + 2
                    if hh < H:
                        if hh >= 2:
                            pe.wait_mark(gpsimd, f"qmm{hh - 2}")
                        dwp[hh % 2].inc(gpsimd.dma_start(
                            out=wqb[hh % 2][:], in_=wq_t[hh, :, :]), 16)
                        dwp[hh % 2].mark(f"wq{hh}")
                    if h + 1 >= 0 and h + 1 < H:
                        rope_pool(h + 1)
                    if h >= 0:
                        dve.wait_mark(gpsimd, f"wev{h}m4")
                        prev = r3(wout, 512)[:, 0:4, 256:512]
                        cur = r3(wout, 512)[:, 1:5, 0:256]
                        alm_v = r3(alm, 256)[:, :, :]
                        alp_v = r3(alp, 256)[:, :, :]
                        gpsimd.tensor_tensor(
                            out=r3(bls1, 256)[:, :, :], in0=prev, in1=alm_v, op=ALU.mult)
                        gpsimd.tensor_tensor(
                            out=r3(bls2, 256)[:, :, :], in0=cur, in1=alp_v, op=ALU.mult)
                        pool.inc(gpsimd.tensor_tensor(
                            out=fin[:, h * OWN:(h + 1) * OWN],
                            in0=bls1[:], in1=bls2[:], op=ALU.add))
                        pool.mark(f"blend{h}")

            @block.tensor
            def _(tensor):
                dma_c.wait_mark(tensor, "p3c")

                def emit_q(hh, g, kc):
                    if kc == 0:
                        dwp[hh % 2].wait_mark(tensor, f"wq{hh}")
                        if hh >= 1:
                            dve.wait_mark(tensor, f"qev{hh - 1}g{g}")
                    pe.inc(tensor.matmul(
                        out=qps[:, g * 512:(g + 1) * 512],
                        lhsT=wqb[hh % 2][:, kc * 128:(kc + 1) * 128],
                        rhs=actT[:, kc * OWN + g * 512: kc * OWN + (g + 1) * 512],
                        start=(kc == 0), stop=(kc == DC - 1)))
                    if kc == DC - 1:
                        pe.mark(f"q{hh}g{g}")
                        if g == 1:
                            pe.mark(f"qmm{hh}")

                for g in range(2):
                    for kc in range(DC):
                        emit_q(0, g, kc)

                for h in range(H):
                    fillers = ([(g, kc) for g in range(2) for kc in range(DC)]
                               if h + 1 < H else [])
                    fpos = [0]

                    def fill(n):
                        while n > 0 and fpos[0] < len(fillers):
                            g, kc = fillers[fpos[0]]
                            emit_q(h + 1, g, kc)
                            fpos[0] += 1
                            n -= 1

                    pool.wait_mark(tensor, f"rope{h}g1")
                    for m in range(5):
                        qlo, qn = QCOLS[m]
                        koff = 256 * m
                        sg = sgb[m % 2]
                        for p in range(2):
                            if not (h == 0 and m == 0 and p == 0):
                                if p == 1:
                                    act.wait_mark(tensor, f"sg{h}m{m}p0")
                                else:
                                    ph, pm = prev_hm(h, m)
                                    act.wait_mark(tensor, f"sg{ph}m{pm}p1")
                            for i in (2 * p, 2 * p + 1):
                                pe.inc(tensor.matmul(
                                    out=scps[:, (i - 2 * p) * 512:(i - 2 * p) * 512 + qn],
                                    lhsT=kr[:, koff + i * 128: koff + (i + 1) * 128],
                                    rhs=qrb[h % 2][:, qlo:qlo + qn],
                                    start=True, stop=True))
                            pe.mark(f"scm{h}m{m}p{p}")
                            fill(8)
                            act.wait_mark(tensor, f"sg{h}m{m}p{p}")
                            if p == 0 and (5 * h + m >= 2):
                                pk = 5 * h + m - 2
                                dve.wait_mark(tensor, f"recip{pk // 5}m{pk % 5}")
                            for i in (2 * p, 2 * p + 1):
                                masked = (m == 4 and i >= 2)
                                pe.inc(tensor.matmul(
                                    out=bcps[:, (m % 2) * 512:(m % 2) * 512 + qn],
                                    lhsT=kmask[:, 128:256] if masked else kmask[:, 0:128],
                                    rhs=sg[:, i * 512: i * 512 + qn],
                                    start=(i == 0), stop=(i == 3)))
                        pe.mark(f"bc{h}m{m}")
                        if 5 * h + m >= 2:
                            pk = 5 * h + m - 2
                            dve.wait_mark(tensor, f"wev{pk // 5}m{pk % 5}")
                        for i in range(4):
                            tt = 2 * m + i
                            pe.inc(tensor.matmul(
                                out=avps[:, (m % 2) * 512:(m % 2) * 512 + qn],
                                lhsT=vtok[:, tt * 128:(tt + 1) * 128],
                                rhs=sg[:, i * 512: i * 512 + qn],
                                start=(i == 0), stop=(i == 3)))
                        pe.mark(f"av{h}m{m}")
                    fill(len(fillers))

            @block.scalar
            def _(scalar):
                scalar.activation(out=warm_b[:, 2:3], in_=rbsb[0:1, 0:1],
                                  func=AF.Sigmoid)
                for h in range(H):
                    for m in range(5):
                        qlo, qn = QCOLS[m]
                        sg = sgb[m % 2]
                        for p in range(2):
                            pe.wait_mark(scalar, f"scm{h}m{m}p{p}")
                            if p == 0 and not (h == 0 and m <= 1):
                                ph, pm = (h, m - 2) if m >= 2 else (h - 1, m + 3)
                                pe.wait_mark(scalar, f"av{ph}m{pm}")
                            act.inc(scalar.activation(
                                out=r3(sg, 512)[:, 2 * p:2 * p + 2, 0:qn],
                                in_=r3(scps, 512)[:, 0:2, 0:qn],
                                func=AF.Sigmoid, scale=SCALE))
                            act.mark(f"sg{h}m{m}p{p}")

            @block.vector
            def _(vector):
                dma_c.wait_mark(vector, "p3c")

                def emit_qev(hh):
                    for g in range(2):
                        pe.wait_mark(vector, f"q{hh}g{g}")
                        vector.tensor_copy(
                            out=qev[:, g * 512:(g + 1) * 512],
                            in_=qps[:, g * 512:(g + 1) * 512])
                        vector.tensor_copy(
                            out=qev2[0:64, g * 512:(g + 1) * 512],
                            in_=qps[64:128, g * 512:(g + 1) * 512])
                        dve.inc(vector.tensor_copy(
                            out=qev2[64:128, g * 512:(g + 1) * 512],
                            in_=qps[0:64, g * 512:(g + 1) * 512]))
                        dve.mark(f"qev{hh}g{g}")

                emit_qev(0)
                for h in range(H):
                    for m in range(5):
                        qlo, qn = QCOLS[m]
                        c0 = 256 if m == 0 else 0
                        pe.wait_mark(vector, f"bc{h}m{m}")
                        dve.inc(vector.reciprocal(
                            out=recip_sb[:, 0:qn],
                            in_=bcps[:, (m % 2) * 512:(m % 2) * 512 + qn]))
                        dve.mark(f"recip{h}m{m}")
                        pe.wait_mark(vector, f"av{h}m{m}")
                        if m == 0 and h >= 1:
                            pool.wait_mark(vector, f"blend{h - 1}")
                        dve.inc(vector.tensor_tensor(
                            out=wout[:, m * 512 + c0: m * 512 + c0 + qn],
                            in0=avps[:, (m % 2) * 512:(m % 2) * 512 + qn],
                            in1=recip_sb[:, 0:qn], op=ALU.mult))
                        dve.mark(f"wev{h}m{m}")
                        if m == 3 and h + 1 < H:
                            emit_qev(h + 1)

        with nc.sbuf_tensor("ao_tok", [128, NT_OWN * D], BF16) as ao_tok:
            stage4 = kr

            es5 = contextlib.ExitStack()
            with es5:
                wob = es5.enter_context(nc.sbuf_tensor("wob", [128, 2 * 1024], BF16))
                wops = es5.enter_context(nc.psum_tensor("wops", [128, 1024], F32))
                trp4 = es5.enter_context(nc.psum_tensor("trp4", [128, 4 * 1024], BF16))
                block = es5.enter_context(nc.Block("p4"))
                @block.sync
                def _(sync):
                    if DEBUG:
                        dve.wait_mark(sync, f"p4ev{DC - 1}g1")
                        dma_o.inc(sync.dma_start(out=d_ao[:], in_=ao_tok[:]), 16)

                @block.gpsimd
                def _(gpsimd):
                    for oc in range(DC):
                        for q4 in range(4):
                            k = oc * 4 + q4
                            if k >= 2:
                                pe.wait_mark(gpsimd, f"woq{k - 2}")
                            dwp[k % 2].inc(gpsimd.dma_start(
                                out=wob[:, (k % 2) * 1024:(k % 2 + 1) * 1024],
                                in_=wo_t[oc, :, q4 * 1024:(q4 + 1) * 1024]), 16)
                            dwp[k % 2].mark(f"wo{k}")

                @block.tensor
                def _(tensor):
                    for oc in range(DC):
                        for q4 in range(4):
                            k = oc * 4 + q4
                            dwp[k % 2].wait_mark(tensor, f"wo{k}")
                            if q4 == 0 and oc >= 1:
                                act.wait_mark(tensor, f"woev{oc - 1}g1")
                            for f8 in range(8):
                                fc = 8 * q4 + f8
                                for g in range(2):
                                    pe.inc(tensor.matmul(
                                        out=wops[:, g * 512:(g + 1) * 512],
                                        lhsT=wob[:, (k % 2) * 1024 + f8 * 128:
                                                  (k % 2) * 1024 + (f8 + 1) * 128],
                                        rhs=fin[:, fc * OWN + g * 512:
                                                fc * OWN + (g + 1) * 512],
                                        start=(fc == 0), stop=(fc == DC - 1)))
                            pe.mark(f"woq{k}")
                        pe.mark(f"womm{oc}")
                        for g in range(2):
                            act.wait_mark(tensor, f"woev{oc}g{g}")
                            if 2 * oc + g >= 1:
                                pk = 2 * oc + g - 1
                                dve.wait_mark(tensor, f"p4ev{pk // 2}g{pk % 2}")
                            for j in range(4):
                                pe.inc(tensor.transpose(
                                    out=trp4[:, j * 1024:j * 1024 + 128],
                                    in_=stage4[:, g * 512 + j * 128:
                                               g * 512 + (j + 1) * 128],
                                    identity=ident[:]))
                            pe.mark(f"wotr{oc}g{g}")

                @block.scalar
                def _(scalar):
                    scalar.activation(out=warm_b[:, 3:4], in_=rbsb[0:1, 0:1],
                                      func=AF.Sqrt)
                    for oc in range(DC):
                        pe.wait_mark(scalar, f"womm{oc}")
                        for g in range(2):
                            if oc >= 1:
                                pe.wait_mark(scalar, f"wotr{oc - 1}g{g}")
                            scalar.copy(
                                out=stage4[:, g * 512:(g + 1) * 512],
                                in_=wops[:, g * 512:(g + 1) * 512])
                            act.inc(scalar.drain())
                            act.mark(f"woev{oc}g{g}")

                @block.vector
                def _(vector):
                    for oc in range(DC):
                        for g in range(2):
                            pe.wait_mark(vector, f"wotr{oc}g{g}")
                            dve.inc(vector.tensor_copy(
                                out=r3(ao_tok, D)[:, 4 * g:4 * g + 4,
                                                  oc * 128:(oc + 1) * 128],
                                in_=r3(trp4, 1024)[:, :, 0:128]))
                            dve.mark(f"p4ev{oc}g{g}")

            es_fin.close()

            es6 = contextlib.ExitStack()
            with es6:
                xstg5 = es6.enter_context(nc.sbuf_tensor("xstg5", [128, D], F32))
                xm5 = es6.enter_context(nc.sbuf_tensor("xm5", [128, D], F32))
                scr5 = es6.enter_context(nc.sbuf_tensor("scr5", [128, D], F32))
                xn3s = es6.enter_context(nc.sbuf_tensor("xn3s", [128, D], BF16))
                trp5 = es6.enter_context(nc.psum_tensor("trp5", [128, 4 * 1024], BF16))
                block = es6.enter_context(nc.Block("p5"))
                @block.sync
                def _(sync):
                    for t in range(NT_OWN):
                        if t >= 1:
                            dve.wait_mark(sync, f"p5xm{t - 1}")
                        dxp[0].inc(sync.dma_start(
                            out=xstg5[:],
                            in_=x_halo[(t + 2) * 128:(t + 3) * 128, :]), 16)
                        dxp[0].mark(f"p5x{t}")
                    if t == NT_OWN - 1 and DEBUG:
                        dve.wait_mark(sync, f"p5ev{NT_OWN - 1}g7")
                        dma_o.inc(sync.dma_start(out=d_xn3t[:], in_=actT[:]), 16)

                @block.scalar
                def _(scalar):
                    for t in range(NT_OWN):
                        ao_v = ao_tok[:, t * D:(t + 1) * D]
                        if t >= 1:
                            dve.wait_mark(scalar, f"p5v3{t - 1}")
                        scalar.activation(
                            out=scr5[:], in_=ao_v, func=AF.Square, accum_out=ssq_b[:])
                        act.inc(scalar.drain())
                        act.mark(f"p5sq2{t}")
                        dve.wait_mark(scalar, f"p5var2{t}")
                        scalar.activation(
                            out=std_b[:], in_=var_b[:], func=AF.Sqrt)
                        act.inc(scalar.drain())
                        act.mark(f"p5std2{t}")
                        dve.wait_mark(scalar, f"p5xm{t}")
                        scalar.activation(
                            out=scr5[:], in_=xm5[:], func=AF.Square, accum_out=ssq_b[:])
                        act.inc(scalar.drain())
                        act.mark(f"p5sq3{t}")
                        dve.wait_mark(scalar, f"p5var3{t}")
                        scalar.activation(
                            out=std_b[:], in_=var_b[:], func=AF.Sqrt)
                        act.inc(scalar.drain())
                        act.mark(f"p5std3{t}")

                @block.vector
                def _(vector):
                    for t in range(NT_OWN):
                        ao_v = ao_tok[:, t * D:(t + 1) * D]
                        vector.reduce_sum(out=sum_b[:], in_=ao_v, axis=AX.X)
                        vector.drain()
                        vector.tensor_scalar(
                            out=m2c[:, t:t + 1], in0=sum_b[:], scalar1=-1.0 / D,
                            scalar2=None, op0=ALU.mult)
                        vector.drain()
                        act.wait_mark(vector, f"p5sq2{t}")
                        vector.tensor_scalar(
                            out=var_b[:], in0=ssq_b[:], scalar1=1.0 / D,
                            scalar2=EPS, op0=ALU.mult, op1=ALU.add)
                        vector.tensor_tensor(
                            out=tmp1_b[:], in0=m2c[:, t:t + 1], in1=m2c[:, t:t + 1],
                            op=ALU.mult)
                        vector.drain()
                        vector.tensor_tensor(
                            out=var_b[:], in0=var_b[:], in1=tmp1_b[:], op=ALU.subtract)
                        dve.inc(vector.drain())
                        dve.mark(f"p5var2{t}")
                        act.wait_mark(vector, f"p5std2{t}")
                        vector.reciprocal(out=r2c[:, t:t + 1], in_=std_b[:])
                        vector.drain()
                        vector.tensor_scalar(
                            out=xm5[:], in0=ao_v, scalar1=m2c[:, t:t + 1],
                            scalar2=r2c[:, t:t + 1], op0=ALU.add, op1=ALU.mult)
                        dxp[0].wait_mark(vector, f"p5x{t}")
                        dve.inc(vector.tensor_tensor(
                            out=xm5[:], in0=xm5[:], in1=xstg5[:], op=ALU.add))
                        dve.mark(f"p5xm{t}")
                        vector.reduce_sum(out=sum_b[:], in_=xm5[:], axis=AX.X)
                        vector.drain()
                        vector.tensor_scalar(
                            out=negm_b[:], in0=sum_b[:], scalar1=-1.0 / D,
                            scalar2=None, op0=ALU.mult)
                        vector.drain()
                        act.wait_mark(vector, f"p5sq3{t}")
                        vector.tensor_scalar(
                            out=var_b[:], in0=ssq_b[:], scalar1=1.0 / D,
                            scalar2=EPS, op0=ALU.mult, op1=ALU.add)
                        vector.tensor_tensor(
                            out=tmp1_b[:], in0=negm_b[:], in1=negm_b[:], op=ALU.mult)
                        vector.drain()
                        vector.tensor_tensor(
                            out=var_b[:], in0=var_b[:], in1=tmp1_b[:], op=ALU.subtract)
                        dve.inc(vector.drain())
                        dve.mark(f"p5var3{t}")
                        dve.mark(f"p5v3{t}")
                        act.wait_mark(vector, f"p5std3{t}")
                        vector.reciprocal(out=rstd_b[:], in_=std_b[:])
                        vector.drain()
                        if t >= 1:
                            pe.wait_mark(vector, f"p5tr{t - 1}")
                        dve.inc(vector.tensor_scalar(
                            out=xn3s[:], in0=xm5[:], scalar1=negm_b[:],
                            scalar2=rstd_b[:], op0=ALU.add, op1=ALU.mult))
                        dve.mark(f"p5app{t}")
                        for g in range(8):
                            pe.wait_mark(vector, f"p5tg{t}g{g}")
                            dve.inc(vector.tensor_copy(
                                out=r3(actT, OWN)[:, 4 * g:4 * g + 4,
                                                  t * 128:(t + 1) * 128],
                                in_=r3(trp5, 1024)[:, :, 0:128]))
                            dve.mark(f"p5ev{t}g{g}")

                @block.tensor
                def _(tensor):
                    for t in range(NT_OWN):
                        dve.wait_mark(tensor, f"p5app{t}")
                        for g in range(8):
                            if 8 * t + g >= 1:
                                pg = 8 * t + g - 1
                                dve.wait_mark(tensor, f"p5ev{pg // 8}g{pg % 8}")
                            for j in range(4):
                                kc = 4 * g + j
                                pe.inc(tensor.transpose(
                                    out=trp5[:, j * 1024:j * 1024 + 128],
                                    in_=xn3s[:, kc * 128:(kc + 1) * 128],
                                    identity=ident[:]))
                            pe.mark(f"p5tg{t}g{g}")
                        pe.mark(f"p5tr{t}")

            es7 = contextlib.ExitStack()
            with es7:
                wmask = es7.enter_context(nc.sbuf_tensor("wmask", [128, OWN], BF16))
                mskscr = es7.enter_context(nc.sbuf_tensor("mskscr", [128, OWN], BF16))
                a_sb = es7.enter_context(nc.sbuf_tensor("a_sb", [128, DC * 128], BF16))
                traw = es7.enter_context(nc.sbuf_tensor("traw", [128, OWN], F32))
                lgps = es7.enter_context(nc.psum_tensor("lgps", [8, 1], F32))
                ltps = es7.enter_context(nc.psum_tensor("ltps", [1, 8], F32))
                bc8ps = es7.enter_context(nc.psum_tensor("bc8ps", [128, 8], F32))
                tps = es7.enter_context(nc.psum_tensor("tps", [128, 1024], F32))
                block = es7.enter_context(nc.Block("p6"))
                @block.sync
                def _(sync):
                    dma_c.inc(sync.dma_start(out=wmask[:], in_=wmask_in[:]), 16)
                    dma_c.mark("p6c")
                    if DEBUG:
                        dve.wait_mark(sync, "router_partial")
                        dma_o.inc(sync.dma_start(out=d_rt[:], in_=router_sb[:]), 16)
                        dve.wait_mark(sync, "tsc")
                        dma_o.inc(sync.dma_start(out=d_qs[:], in_=qsum_sb[:]), 16)
                        dma_o.inc(sync.dma_start(out=d_rtg[:], in_=rtg_sb[:]), 16)
                        dma_o.inc(sync.dma_start(out=d_srt[:], in_=srt_sb[:]), 16)
                        dma_o.inc(sync.dma_start(out=d_tsc[:], in_=tsc_sb[:]), 16)

                @block.gpsimd
                def _(gpsimd):
                    dwm.inc(gpsimd.dma_start(out=a_sb[:], in_=a_t[:]), 16)
                    dwm.mark("a_sb")
                    dve.wait_mark(gpsimd, "router_partial")
                    dwm.inc(gpsimd.dma_start(out=cc_in[:], in_=router_sb[:]), 16)
                    dwm.wait_now(gpsimd)
                    ccs.inc(gpsimd.collective_compute(
                        "AllReduce", ALU.add,
                        replica_groups=[list(range(NCORE))],
                        ins=[cc_in.ap().opt()],
                        outs=[cc_out.ap().opt()]), 1)
                    ccs.wait_now(gpsimd)
                    dwm.inc(gpsimd.dma_start(out=qsum_sb[:], in_=cc_out[:]), 16)
                    dwm.mark("qsum")

                @block.tensor
                def _(tensor):
                    dwm.wait_mark(tensor, "qsum")
                    for kc in range(DC):
                        pe.inc(tensor.matmul(
                            out=lgps[:], lhsT=rwsb[:, kc * 8:(kc + 1) * 8],
                            rhs=qsum_sb[:, kc:kc + 1],
                            start=(kc == 0), stop=(kc == DC - 1)))
                    pe.mark("logits")
                    dve.wait_mark(tensor, "lg_sb")
                    pe.inc(tensor.matmul(
                        out=ltps[:], lhsT=logit_sb[:], rhs=eye8[:],
                        start=True, stop=True))
                    pe.mark("logitsT")
                    dve.wait_mark(tensor, "routing")
                    pe.inc(tensor.matmul(
                        out=bc8ps[:], lhsT=ones_f[:], rhs=rtg_sb[:],
                        start=True, stop=True))
                    pe.mark("bc8")
                    dwm.wait_mark(tensor, "a_sb")
                    for g in range(2):
                        for kc in range(DC):
                            pe.inc(tensor.matmul(
                                out=tps[:, g * 512:(g + 1) * 512],
                                lhsT=a_sb[:, kc * 128:(kc + 1) * 128],
                                rhs=actT[:, kc * OWN + g * 512: kc * OWN + (g + 1) * 512],
                                start=(kc == 0), stop=(kc == DC - 1)))
                    pe.mark("tmm")

                @block.vector
                def _(vector):
                    dma_c.wait_mark(vector, "p6c")
                    for kc in range(DC):
                        vector.scalar_tensor_tensor(
                            out=mskscr[:], in0=actT[:, kc * OWN:(kc + 1) * OWN],
                            scalar=0.0, in1=wmask[:], op0=ALU.add, op1=ALU.mult,
                            accum_out=router_sb[:, kc:kc + 1])
                    dve.inc(vector.drain(), DC)
                    dve.mark("router_partial")
                    pe.wait_mark(vector, "logits")
                    vector.tensor_scalar(
                        out=logit_sb[:], in0=lgps[:], scalar1=1.0 / NSEL,
                        scalar2=rbsb[:], op0=ALU.mult, op1=ALU.add)
                    dve.inc(vector.drain())
                    dve.mark("lg_sb")
                    pe.wait_mark(vector, "logitsT")
                    vector.tensor_copy(out=rtmp_sb[:], in_=ltps[:])
                    vector.drain()
                    vector.reduce_max(out=rmax_sb[:], in_=rtmp_sb[:], axis=AX.X)
                    vector.drain()
                    vector.tensor_scalar(
                        out=rmax_sb[:], in0=rmax_sb[:], scalar1=-1.0,
                        scalar2=None, op0=ALU.mult)
                    dve.inc(vector.drain())
                    dve.mark("sftm_pre")
                    act.wait_mark(vector, "sftm_exp")
                    vector.reduce_sum(out=rsum1_sb[:], in_=rtmp_sb[:], axis=AX.X)
                    vector.drain()
                    vector.reciprocal(out=rsum1_sb[:], in_=rsum1_sb[:])
                    vector.drain()
                    vector.tensor_scalar(
                        out=rtg_sb[:], in0=rtmp_sb[:], scalar1=rsum1_sb[:],
                        scalar2=None, op0=ALU.mult)
                    dve.inc(vector.drain())
                    dve.mark("routing")
                    pe.wait_mark(vector, "bc8")
                    vector.tensor_tensor(
                        out=bc8_sb[:], in0=bc8ps[:], in1=sel2[:], op=ALU.mult)
                    vector.drain()
                    vector.tensor_reduce(
                        out=srt_sb[:], in_=bc8_sb[:], axis=AX.X, op=ALU.add)
                    vector.drain()
                    pe.wait_mark(vector, "tmm")
                    vector.tensor_copy(out=traw[:], in_=tps[:])
                    vector.drain()
                    dve.inc(vector.tensor_scalar(
                        out=tsc_sb[:], in0=traw[:], scalar1=srt_sb[:],
                        scalar2=None, op0=ALU.mult))
                    dve.mark("tsc")

                @block.scalar
                def _(scalar):
                    scalar.activation(out=warm_b[:, 4:5], in_=rbsb[0:1, 0:1],
                                      func=AF.Exp)
                    dve.wait_mark(scalar, "sftm_pre")
                    scalar.activation(
                        out=rtmp_sb[:], in_=rtmp_sb[:], func=AF.Exp,
                        bias=rmax_sb[:])
                    act.inc(scalar.drain())
                    act.mark("sftm_exp")

            es8 = contextlib.ExitStack()
            with es8:
                ffh = es8.enter_context(nc.sbuf_tensor("ffh", [128, 4 * D], BF16))
                wfb = es8.enter_context(nc.sbuf_tensor("wfb", [128, 4 * 1024], BF16))
                b_sb = es8.enter_context(nc.sbuf_tensor("b_sb", [128, DC * 128], BF16))
                xb9 = es8.enter_context(nc.sbuf_tensor("xb9", [128, 1024], F32))
                s9a0 = es8.enter_context(nc.sbuf_tensor("s9a0", [128, 1024], F32))
                s9a1 = es8.enter_context(nc.sbuf_tensor("s9a1", [128, 1024], F32))
                s9b = es8.enter_context(nc.sbuf_tensor("s9b", [128, 1024], F32))
                junk9 = es8.enter_context(nc.sbuf_tensor("junk9", [128, 1024], BF16))
                st9 = es8.enter_context(nc.sbuf_tensor("st9", [128, 4], F32))
                ffps = es8.enter_context(nc.psum_tensor("ffps", [128, 1024], F32))
                trp8 = es8.enter_context(nc.psum_tensor("trp8", [128, 4 * 1024], BF16))
                s9a = [s9a0, s9a1]
                stage8 = kr

                for hf in range(2):
                    with nc.Block(f"p8h{hf}") as block:
                        @block.sync
                        def _(sync, hf=hf):
                            if DEBUG:
                                dve.wait_mark(sync, f"p8ev{hf}o{DC - 1}")
                                dma_o.inc(sync.dma_start(
                                    out=(d_ffh0 if hf == 0 else d_ffh1)[:],
                                    in_=ffh[:]), 16)

                        @block.gpsimd
                        def _(gpsimd, hf=hf):
                            if hf == 0:
                                dwm.inc(gpsimd.dma_start(out=b_sb[:], in_=b_t[:]), 16)
                                dwm.mark("b_sb")
                            for oc in range(DC):
                                for q4 in range(4):
                                    k = oc * 4 + q4
                                    if k >= 4:
                                        pe.wait_mark(gpsimd, f"f{hf}q{k - 4}")
                                    dwp[k % 2].inc(gpsimd.dma_start(
                                        out=wfb[:, (k % 4) * 1024:(k % 4 + 1) * 1024],
                                        in_=wf_t[oc, :, q4 * 1024:(q4 + 1) * 1024]), 16)
                                    dwp[k % 2].mark(f"wf{hf}o{oc}q{q4}")

                        @block.tensor
                        def _(tensor, hf=hf):
                            dwm.wait_mark(tensor, "b_sb")
                            for oc in range(DC):
                                fo = (oc % 2) * 512
                                for q4 in range(4):
                                    k = oc * 4 + q4
                                    dwp[k % 2].wait_mark(tensor, f"wf{hf}o{oc}q{q4}")
                                    if q4 == 0 and (hf * DC + oc >= 2):
                                        po = (hf * DC + oc - 2) % DC
                                        ph = (hf * DC + oc - 2) // DC
                                        act.wait_mark(tensor, f"ffev{ph}o{po}")
                                    for f8 in range(8):
                                        kc = 8 * q4 + f8
                                        pe.inc(tensor.matmul(
                                            out=ffps[:, fo:fo + 512],
                                            lhsT=wfb[:, (k % 4) * 1024 + f8 * 128:
                                                      (k % 4) * 1024 + (f8 + 1) * 128],
                                            rhs=actT[:, kc * OWN + hf * 512:
                                                     kc * OWN + hf * 512 + 512],
                                            start=(kc == 0), stop=False))
                                    pe.mark(f"f{hf}q{k}")
                                pe.inc(tensor.matmul(
                                    out=ffps[:, fo:fo + 512],
                                    lhsT=b_sb[:, oc * 128:(oc + 1) * 128],
                                    rhs=tsc_sb[:, hf * 512: hf * 512 + 512],
                                    start=False, stop=True))
                                pe.mark(f"ffmm{hf}o{oc}")
                                act.wait_mark(tensor, f"ffev{hf}o{oc}")
                                if hf + oc >= 1:
                                    po = oc - 1 if oc >= 1 else DC - 1
                                    ph = hf if oc >= 1 else hf - 1
                                    dve.wait_mark(tensor, f"p8ev{ph}o{po}")
                                for j in range(4):
                                    pe.inc(tensor.transpose(
                                        out=trp8[:, j * 1024:j * 1024 + 128],
                                        in_=stage8[:, j * 128:(j + 1) * 128],
                                        identity=ident[:]))
                                pe.mark(f"fftr{hf}o{oc}")

                        @block.scalar
                        def _(scalar, hf=hf):
                            if hf == 0:
                                scalar.activation(out=warm_b[:, 5:6],
                                                  in_=rbsb[0:1, 0:1], func=AF.Sqrt)
                            for oc in range(DC):
                                pe.wait_mark(scalar, f"ffmm{hf}o{oc}")
                                if hf + oc >= 1:
                                    po = oc - 1 if oc >= 1 else DC - 1
                                    ph = hf if oc >= 1 else hf - 1
                                    pe.wait_mark(scalar, f"fftr{ph}o{po}")
                                scalar.activation(
                                    out=stage8[:, 0:512],
                                    in_=ffps[:, (oc % 2) * 512:(oc % 2) * 512 + 512],
                                    func=AF.Identity, bias=biassb[:, oc:oc + 1])
                                act.inc(scalar.drain())
                                act.mark(f"ffev{hf}o{oc}")

                        @block.vector
                        def _(vector, hf=hf):
                            for oc in range(DC):
                                pe.wait_mark(vector, f"fftr{hf}o{oc}")
                                dve.inc(vector.tensor_copy(
                                    out=r3(ffh, D)[:, :, oc * 128:(oc + 1) * 128],
                                    in_=r3(trp8, 1024)[:, :, 0:128]))
                                dve.mark(f"p8ev{hf}o{oc}")

                    with nc.Block(f"p9h{hf}") as block:
                        @block.sync
                        def _(sync, hf=hf):
                            for k in range(17):
                                tl, cb = divmod(k, 4)
                                T = 4 * hf + tl
                                if k < 16:
                                    if k >= 1:
                                        dve.wait_mark(sync, f"p9xm{hf}k{k - 1}")
                                    dxp[0].inc(sync.dma_start(
                                        out=xb9[:],
                                        in_=x_halo[(T + 2) * 128:(T + 3) * 128,
                                                   cb * 1024:(cb + 1) * 1024]), 16)
                                    dxp[0].mark(f"p9x{hf}t{tl}c{cb}")
                                if k >= 1:
                                    kx = k - 1
                                    ptl, pcb = divmod(kx, 4)
                                    PT = 4 * hf + ptl
                                    dve.wait_mark(sync, f"p9f{hf}k{kx}")
                                    dop[kx % 2].inc(sync.dma_start(
                                        out=out[PT * 128:(PT + 1) * 128,
                                                pcb * 1024:(pcb + 1) * 1024],
                                        in_=s9a[kx % 2][:]), 16)
                                    dop[kx % 2].mark(f"p9o{hf}k{kx}")

                        @block.scalar
                        def _(scalar, hf=hf):
                            for tl in range(4):
                                ff_t3 = r3(ffh, 1024)[:, 4 * tl:4 * tl + 4, :]
                                if hf + tl >= 1:
                                    dve.wait_mark(scalar, f"p9st{hf}t{tl - 1}"
                                                  if tl >= 1 else f"p9st{hf - 1}t3")
                                for cb in range(4):
                                    scalar.activation(
                                        out=junk9[:], in_=ffh[:, (4 * tl + cb) * 1024:
                                                             (4 * tl + cb + 1) * 1024],
                                        func=AF.Square, accum_out=st9[:, cb:cb + 1])
                                act.inc(scalar.drain(), 4)
                                act.mark(f"p9sq{hf}t{tl}")
                                dve.wait_mark(scalar, f"p9var{hf}t{tl}")
                                scalar.activation(
                                    out=std_b[:], in_=var_b[:], func=AF.Sqrt)
                                act.inc(scalar.drain())
                                act.mark(f"p9std{hf}t{tl}")

                        @block.vector
                        def _(vector, hf=hf):
                            for tl in range(4):
                                T = 4 * hf + tl
                                ff_v = ffh[:, tl * D:(tl + 1) * D]
                                vector.reduce_sum(out=sum_b[:], in_=ff_v, axis=AX.X)
                                vector.drain()
                                vector.tensor_scalar(
                                    out=negm_b[:], in0=sum_b[:], scalar1=-1.0 / D,
                                    scalar2=None, op0=ALU.mult)
                                vector.drain()
                                act.wait_mark(vector, f"p9sq{hf}t{tl}")
                                vector.tensor_reduce(
                                    out=ssq_b[:], in_=st9[:], axis=AX.X, op=ALU.add)
                                vector.drain()
                                vector.tensor_scalar(
                                    out=var_b[:], in0=ssq_b[:], scalar1=1.0 / D,
                                    scalar2=EPS, op0=ALU.mult, op1=ALU.add)
                                vector.tensor_tensor(
                                    out=tmp1_b[:], in0=negm_b[:], in1=negm_b[:],
                                    op=ALU.mult)
                                vector.drain()
                                vector.tensor_tensor(
                                    out=var_b[:], in0=var_b[:], in1=tmp1_b[:],
                                    op=ALU.subtract)
                                dve.inc(vector.drain())
                                dve.mark(f"p9var{hf}t{tl}")
                                dve.mark(f"p9st{hf}t{tl}")
                                act.wait_mark(vector, f"p9std{hf}t{tl}")
                                vector.reciprocal(out=rstd_b[:], in_=std_b[:])
                                vector.drain()
                                for cb in range(4):
                                    kx = 4 * tl + cb
                                    sa = s9a[kx % 2]
                                    if kx >= 2:
                                        dop[kx % 2].wait_mark(vector, f"p9o{hf}k{kx - 2}")
                                    vector.tensor_scalar(
                                        out=sa[:],
                                        in0=ao_tok[:, T * D + cb * 1024:
                                                   T * D + (cb + 1) * 1024],
                                        scalar1=m2c[:, T:T + 1], scalar2=r2c[:, T:T + 1],
                                        op0=ALU.add, op1=ALU.mult)
                                    dxp[0].wait_mark(vector, f"p9x{hf}t{tl}c{cb}")
                                    dve.inc(vector.tensor_tensor(
                                        out=sa[:], in0=sa[:], in1=xb9[:], op=ALU.add))
                                    dve.mark(f"p9xm{hf}k{kx}")
                                    vector.tensor_scalar(
                                        out=s9b[:],
                                        in0=ffh[:, tl * D + cb * 1024:
                                                tl * D + (cb + 1) * 1024],
                                        scalar1=negm_b[:], scalar2=rstd_b[:],
                                        op0=ALU.add, op1=ALU.mult)
                                    vector.tensor_tensor(
                                        out=sa[:], in0=sa[:], in1=s9b[:], op=ALU.add)
                                    dve.inc(vector.drain())
                                    dve.mark(f"p9f{hf}k{kx}")


        with nc.Block("fin") as block:
            @block.sync
            def _(sync):
                dop[0].wait_now(sync)
                dop[1].wait_now(sync)
                if DEBUG:
                    dma_o.wait_now(sync)

    return nc



def _host_prep(L0_attn_x, L0_attn_Wq, L0_attn_Wk, L0_attn_Wv, L0_attn_Wo,
               L0_ffn_W, L0_ffn_b, L0_ffn_A, L0_ffn_B,
               L0_ffn_router_W, L0_ffn_router_b):
    f32 = np.float32
    bf16 = ml_dtypes.bfloat16

    def tile_w(W):
        return np.ascontiguousarray(
            W.reshape(DC, 128, DC, 128).transpose(0, 3, 2, 1).reshape(DC, 128, DC * 128))

    def tile_small(Wsm):
        return np.ascontiguousarray(
            Wsm.reshape(128, DC, 128).transpose(2, 1, 0).reshape(128, DC * 128))

    wq = tile_w(np.asarray(L0_attn_Wq, f32))
    wo = tile_w(np.asarray(L0_attn_Wo, f32))
    wf = tile_w(np.asarray(L0_ffn_W, f32))
    wk = tile_small(np.asarray(L0_attn_Wk, f32))
    wv = tile_small(np.asarray(L0_attn_Wv, f32))
    a_cat = np.asarray(L0_ffn_A, f32).reshape(128, D)
    a_tl = tile_small(a_cat)
    b_cat = np.asarray(L0_ffn_B, f32).transpose(1, 0, 2).reshape(D, 128)
    b_tl = np.ascontiguousarray(
        b_cat.reshape(DC, 128, 128).transpose(2, 0, 1).reshape(128, DC * 128))
    rw = np.ascontiguousarray(
        np.asarray(L0_ffn_router_W, f32).T.reshape(DC, 128, 8).transpose(1, 0, 2)
        .reshape(128, DC * 8))
    rb = np.ascontiguousarray(np.asarray(L0_ffn_router_b, f32).reshape(8, 1))
    bias = np.ascontiguousarray(np.asarray(L0_ffn_b, f32).reshape(DC, 128).T)

    inv_freq = (1.0 / 10000.0 ** (np.arange(0, DH, 2, dtype=f32) / DH)).astype(f32)
    t_pos = np.arange(S, dtype=f32)
    f = np.outer(t_pos, inv_freq).astype(f32)
    cosg = np.zeros((S, DH), f32)
    sing = np.zeros((S, DH), f32)
    cosg[:, 0::2] = np.cos(f)
    cosg[:, 1::2] = np.cos(f)
    sing[:, 0::2] = np.sin(f)
    sing[:, 1::2] = np.sin(f)

    alpha_blk = np.linspace(0.0, 1.0, 256, dtype=f32)

    xf = np.asarray(L0_attn_x, f32).reshape(S, D)
    xpad = np.pad(xf, ((HALO, HALO), (0, 0)))

    ident = np.eye(128, dtype=bf16)
    ones_r = np.ones((1, 128), dtype=bf16)
    eye8 = np.eye(8, dtype=f32)
    sel2 = np.zeros((128, 8), f32)
    for p in range(128):
        sel2[p, p // 16] = 2.0

    in_maps = []
    for c in range(NCORE):
        g0 = c * OWN
        xh = np.ascontiguousarray(xpad[g0:g0 + SH])
        pos = np.clip(np.arange(g0 - HALO, g0 - HALO + SH), 0, S - 1)
        cos_c = np.ascontiguousarray(cosg[pos].T)
        sin_c = cosg[pos].T * 0 + sing[pos].T
        sin_c[0:64, :] *= -1.0
        sin_c = np.ascontiguousarray(sin_c.astype(f32))
        al = np.tile(alpha_blk, 4).astype(f32)
        if c == 0:
            al[0:256] = 1.0
        alm = (np.float32(1.0) - al).astype(f32)
        al_t = np.ascontiguousarray(np.broadcast_to(al, (128, OWN))).astype(bf16)
        alm_t = np.ascontiguousarray(np.broadcast_to(alm, (128, OWN))).astype(bf16)
        km = np.ones((128, 256), np.float32)
        if c == NCORE - 1:
            km[:, 128:256] = 0.0
        km = km.astype(bf16)
        wm = np.zeros(OWN, np.float32)
        gidx = np.arange(g0, g0 + OWN)
        wm[(gidx >= 611) & (gidx <= S - 2)] = 1.0
        wm_t = np.ascontiguousarray(np.broadcast_to(wm, (128, OWN))).astype(bf16)
        in_maps.append({
            "x_halo": xh, "wq_t": wq, "wo_t": wo, "wf_t": wf,
            "wk_t": wk, "wv_t": wv, "a_t": a_tl, "b_t": b_tl,
            "rw_t": rw, "rb_in": rb, "bias_t": bias,
            "cos_in": cos_c, "sin_in": sin_c,
            "alpha_in": al_t, "alm1_in": alm_t,
            "kmask_in": km, "wmask_in": wm_t,
            "ident_in": ident, "ones_in": ones_r, "eye8_in": eye8,
            "sel2_in": sel2,
        })
    return in_maps


def kernel(L0_attn_x, L0_attn_Wq, L0_attn_Wk, L0_attn_Wv, L0_attn_Wo,
           L0_ffn_W, L0_ffn_b, L0_ffn_A, L0_ffn_B,
           L0_ffn_router_W, L0_ffn_router_b):
    in_maps = _host_prep(L0_attn_x, L0_attn_Wq, L0_attn_Wk, L0_attn_Wv,
                         L0_attn_Wo, L0_ffn_W, L0_ffn_b, L0_ffn_A, L0_ffn_B,
                         L0_ffn_router_W, L0_ffn_router_b)
    marks = {}
    build_program(marks, final_pass=False)
    nc = build_program(marks, final_pass=True)
    res = run_bass_kernel_spmd(nc, in_maps, core_ids=list(range(NCORE)))
    full = np.concatenate(
        [np.asarray(res.results[c]["out"], dtype=np.float32) for c in range(NCORE)],
        axis=0)
    return full.reshape(1, S, D)


# version 19
# speedup vs baseline: 1.0332x; 1.0332x over previous
import numpy as np
import ml_dtypes
import concourse.bass as bass
import concourse.mybir as mybir
from concourse.bass_utils import run_bass_kernel_spmd

F32 = mybir.dt.float32
BF16 = mybir.dt.bfloat16
AF = mybir.ActivationFunctionType
ALU = mybir.AluOpType
AX = mybir.AxisListType

NCORE = 8
S = 8192
D = 4096
DC = 32
H = 32
DH = 128
OWN = 1024
HALO = 256
SH = OWN + 2 * HALO
NT = SH // 128
NT_OWN = OWN // 128
SCALE = 1.0 / float(np.sqrt(DH))
EPS = 1e-5
NSEL = 7580.0
DEBUG = False
QCOLS = [(0, 256), (0, 512), (256, 512), (512, 512), (768, 256)]


class SemCtr:

    def __init__(self, handle, marks, name, final_pass):
        self.h = handle
        self.v = 0
        self.marks = marks
        self.name = name
        self.final_pass = final_pass

    def inc(self, inst, n=1):
        inst.then_inc(self.h, n)
        self.v += n

    def mark(self, key):
        k = (self.name, key)
        if self.final_pass:
            assert self.marks.get(k) == self.v, (k, self.marks.get(k), self.v)
        else:
            self.marks[k] = self.v

    def wait_mark(self, eng, key):
        v = self.marks.get((self.name, key), 0)
        if v > 0:
            eng.wait_ge(self.h, v)

    def wait_now(self, eng):
        if self.v > 0:
            eng.wait_ge(self.h, self.v)


def r3(t, inner, n_outer=None):
    return t.ap().rearrange("p (a b) -> p a b", b=inner)


def build_program(marks, final_pass):
    nc = bass.Bass("TRN2", target_bir_lowering=False, debug=False)

    x_halo = nc.dram_tensor("x_halo", [SH, D], F32, kind="ExternalInput")
    wq_t = nc.dram_tensor("wq_t", [DC, 128, DC * 128], F32, kind="ExternalInput")
    wo_t = nc.dram_tensor("wo_t", [DC, 128, DC * 128], F32, kind="ExternalInput")
    wf_t = nc.dram_tensor("wf_t", [DC, 128, DC * 128], F32, kind="ExternalInput")
    wk_t = nc.dram_tensor("wk_t", [128, DC * 128], F32, kind="ExternalInput")
    wv_t = nc.dram_tensor("wv_t", [128, DC * 128], F32, kind="ExternalInput")
    a_t = nc.dram_tensor("a_t", [128, DC * 128], F32, kind="ExternalInput")
    b_t = nc.dram_tensor("b_t", [128, DC * 128], F32, kind="ExternalInput")
    rw_t = nc.dram_tensor("rw_t", [128, DC * 8], F32, kind="ExternalInput")
    rb_in = nc.dram_tensor("rb_in", [8, 1], F32, kind="ExternalInput")
    bias_t = nc.dram_tensor("bias_t", [128, DC], F32, kind="ExternalInput")
    cos_in = nc.dram_tensor("cos_in", [128, SH], F32, kind="ExternalInput")
    sin_in = nc.dram_tensor("sin_in", [128, SH], F32, kind="ExternalInput")
    alpha_in = nc.dram_tensor("alpha_in", [128, OWN], BF16, kind="ExternalInput")
    alm1_in = nc.dram_tensor("alm1_in", [128, OWN], BF16, kind="ExternalInput")
    kmask_in = nc.dram_tensor("kmask_in", [128, 256], BF16, kind="ExternalInput")
    wmask_in = nc.dram_tensor("wmask_in", [128, OWN], BF16, kind="ExternalInput")
    ident_in = nc.dram_tensor("ident_in", [128, 128], BF16, kind="ExternalInput")
    ones_in = nc.dram_tensor("ones_in", [1, 128], BF16, kind="ExternalInput")
    eye8_in = nc.dram_tensor("eye8_in", [8, 8], F32, kind="ExternalInput")
    sel2_in = nc.dram_tensor("sel2_in", [128, 8], F32, kind="ExternalInput")
    out = nc.dram_tensor("out", [OWN, D], F32, kind="ExternalOutput")
    cc_in = nc.dram_tensor("cc_in", [128, DC], F32, kind="Internal")
    cc_out = nc.dram_tensor("cc_out", [128, DC], F32, kind="Internal")
    if DEBUG:
        d_xnt = nc.dram_tensor("d_xnt", [128, DC * OWN], BF16, kind="ExternalOutput")
        d_xnh = nc.dram_tensor("d_xnh", [128, DC * 2 * HALO], BF16, kind="ExternalOutput")
        d_kr = nc.dram_tensor("d_kr", [128, SH], BF16, kind="ExternalOutput")
        d_vtok = nc.dram_tensor("d_vtok", [128, NT * 128], BF16, kind="ExternalOutput")
        d_fin = nc.dram_tensor("d_fin", [128, H * OWN], BF16, kind="ExternalOutput")
        d_ao = nc.dram_tensor("d_ao", [128, NT_OWN * D], BF16, kind="ExternalOutput")
        d_xn3t = nc.dram_tensor("d_xn3t", [128, DC * OWN], BF16, kind="ExternalOutput")
        d_rt = nc.dram_tensor("d_rt", [128, DC], F32, kind="ExternalOutput")
        d_qs = nc.dram_tensor("d_qs", [128, DC], F32, kind="ExternalOutput")
        d_rtg = nc.dram_tensor("d_rtg", [1, 8], F32, kind="ExternalOutput")
        d_srt = nc.dram_tensor("d_srt", [128, 1], F32, kind="ExternalOutput")
        d_tsc = nc.dram_tensor("d_tsc", [128, OWN], BF16, kind="ExternalOutput")
        d_ffh0 = nc.dram_tensor("d_ffh0", [128, 4 * D], BF16, kind="ExternalOutput")
        d_ffh1 = nc.dram_tensor("d_ffh1", [128, 4 * D], BF16, kind="ExternalOutput")
        d_xstg = nc.dram_tensor("d_xstg", [128, D], F32, kind="ExternalOutput")
        d_xntok = nc.dram_tensor("d_xntok", [128, D], BF16, kind="ExternalOutput")
        d_stat = nc.dram_tensor("d_stat", [128, 8], F32, kind="ExternalOutput")

    import contextlib
    with contextlib.ExitStack() as es:
        actT = es.enter_context(nc.sbuf_tensor("actT", [128, DC * OWN], BF16))
        kr = es.enter_context(nc.sbuf_tensor("kr", [128, SH], BF16))
        vtok = es.enter_context(nc.sbuf_tensor("vtok", [128, NT * 128], BF16))
        ident = es.enter_context(nc.sbuf_tensor("ident", [128, 128], BF16))
        ones_r = es.enter_context(nc.sbuf_tensor("ones_r", [1, 128], BF16))
        ones_f = es.enter_context(nc.sbuf_tensor("ones_f", [1, 128], F32))
        eye8 = es.enter_context(nc.sbuf_tensor("eye8", [8, 8], F32))
        sel2 = es.enter_context(nc.sbuf_tensor("sel2", [128, 8], F32))
        rwsb = es.enter_context(nc.sbuf_tensor("rwsb", [128, DC * 8], F32))
        rbsb = es.enter_context(nc.sbuf_tensor("rbsb", [8, 1], F32))
        biassb = es.enter_context(nc.sbuf_tensor("biassb", [128, DC], F32))
        sum_b = es.enter_context(nc.sbuf_tensor("sum_b", [128, 1], F32))
        ssq_b = es.enter_context(nc.sbuf_tensor("ssq_b", [128, 1], F32))
        negm_b = es.enter_context(nc.sbuf_tensor("negm_b", [128, 1], F32))
        var_b = es.enter_context(nc.sbuf_tensor("var_b", [128, 1], F32))
        std_b = es.enter_context(nc.sbuf_tensor("std_b", [128, 1], F32))
        rstd_b = es.enter_context(nc.sbuf_tensor("rstd_b", [128, 1], F32))
        tmp1_b = es.enter_context(nc.sbuf_tensor("tmp1_b", [128, 1], F32))
        m2c = es.enter_context(nc.sbuf_tensor("m2c", [128, NT_OWN], F32))
        r2c = es.enter_context(nc.sbuf_tensor("r2c", [128, NT_OWN], F32))
        router_sb = es.enter_context(nc.sbuf_tensor("router_sb", [128, DC], F32))
        qsum_sb = es.enter_context(nc.sbuf_tensor("qsum_sb", [128, DC], F32))
        logit_sb = es.enter_context(nc.sbuf_tensor("logit_sb", [8, 1], F32))
        rtg_sb = es.enter_context(nc.sbuf_tensor("rtg_sb", [1, 8], F32))
        rtmp_sb = es.enter_context(nc.sbuf_tensor("rtmp_sb", [1, 8], F32))
        rmax_sb = es.enter_context(nc.sbuf_tensor("rmax_sb", [1, 1], F32))
        rsum1_sb = es.enter_context(nc.sbuf_tensor("rsum1_sb", [1, 1], F32))
        bc8_sb = es.enter_context(nc.sbuf_tensor("bc8_sb", [128, 8], F32))
        srt_sb = es.enter_context(nc.sbuf_tensor("srt_sb", [128, 1], F32))
        tsc_sb = es.enter_context(nc.sbuf_tensor("tsc_sb", [128, OWN], BF16))
        warm_b = es.enter_context(nc.sbuf_tensor("warm_b", [1, 8], F32))
        _dma_c = es.enter_context(nc.semaphore("dma_c"))
        _dma_x = es.enter_context(nc.semaphore("dma_x"))
        _dma_w = es.enter_context(nc.semaphore("dma_w"))
        _dma_o = es.enter_context(nc.semaphore("dma_o"))
        _s_pe = es.enter_context(nc.semaphore("s_pe"))
        _s_dve = es.enter_context(nc.semaphore("s_dve"))
        _s_act = es.enter_context(nc.semaphore("s_act"))
        _s_pool = es.enter_context(nc.semaphore("s_pool"))
        _s_cc = es.enter_context(nc.semaphore("s_cc"))
        _dx0 = es.enter_context(nc.semaphore("dx0"))
        _dx1 = es.enter_context(nc.semaphore("dx1"))
        _do0 = es.enter_context(nc.semaphore("do0"))
        _do1 = es.enter_context(nc.semaphore("do1"))
        _dw0 = es.enter_context(nc.semaphore("dw0"))
        _dw1 = es.enter_context(nc.semaphore("dw1"))
        _dwm = es.enter_context(nc.semaphore("dwm"))

        dma_c = SemCtr(_dma_c, marks, "dma_c", final_pass)
        dma_x = SemCtr(_dma_x, marks, "dma_x", final_pass)
        dma_w = SemCtr(_dma_w, marks, "dma_w", final_pass)
        dma_o = SemCtr(_dma_o, marks, "dma_o", final_pass)
        pe = SemCtr(_s_pe, marks, "pe", final_pass)
        dve = SemCtr(_s_dve, marks, "dve", final_pass)
        act = SemCtr(_s_act, marks, "act", final_pass)
        pool = SemCtr(_s_pool, marks, "pool", final_pass)
        ccs = SemCtr(_s_cc, marks, "cc", final_pass)
        dxp = [SemCtr(_dx0, marks, "dx0", final_pass), SemCtr(_dx1, marks, "dx1", final_pass)]
        dop = [SemCtr(_do0, marks, "do0", final_pass), SemCtr(_do1, marks, "do1", final_pass)]
        dwp = [SemCtr(_dw0, marks, "dw0", final_pass), SemCtr(_dw1, marks, "dw1", final_pass)]
        dwm = SemCtr(_dwm, marks, "dwm", final_pass)

        with nc.Block("p0") as block:
            @block.sync
            def _(sync):
                for dst, src in [
                    (ident[:], ident_in[:]), (ones_r[:], ones_in[:]),
                    (eye8[:], eye8_in[:]), (sel2[:], sel2_in[:]),
                    (rwsb[:], rw_t[:]), (rbsb[:], rb_in[:]), (biassb[:], bias_t[:]),
                ]:
                    dma_c.inc(sync.dma_start(out=dst, in_=src), 16)
                dma_c.wait_now(sync)

            @block.vector
            def _(vector):
                vector.memset(ones_f[:], 1.0)

            @block.scalar
            def _(scalar):
                dma_c.wait_now(scalar)
                scalar.activation(out=warm_b[:, 0:1], in_=rbsb[0:1, 0:1],
                                  func=AF.Square)
                scalar.activation(out=warm_b[:, 1:2], in_=rbsb[0:1, 0:1],
                                  func=AF.Sqrt)

        es1 = contextlib.ExitStack()
        with es1:
            xnH = es1.enter_context(nc.sbuf_tensor("xnH", [128, DC * 2 * HALO], BF16))
            xstg0 = es1.enter_context(nc.sbuf_tensor("xstg0", [128, D], F32))
            xstg1 = es1.enter_context(nc.sbuf_tensor("xstg1", [128, D], F32))
            scr_act = es1.enter_context(nc.sbuf_tensor("scr_act", [128, D], F32))
            xn_tok0 = es1.enter_context(nc.sbuf_tensor("xn_tok0", [128, D], BF16))
            xn_tok1 = es1.enter_context(nc.sbuf_tensor("xn_tok1", [128, D], BF16))
            cosf = es1.enter_context(nc.sbuf_tensor("cosf", [128, SH], F32))
            sinf = es1.enter_context(nc.sbuf_tensor("sinf", [128, SH], F32))
            xstg = [xstg0, xstg1]
            xn_tok = [xn_tok0, xn_tok1]

            def xnT_dst(t, g):
                if 2 <= t <= 9:
                    return r3(actT, OWN)[:, 4 * g:4 * g + 4,
                                         (t - 2) * 128:(t - 2) * 128 + 128]
                hoff = {0: 0, 1: 128, 10: 256, 11: 384}[t]
                return r3(xnH, 2 * HALO)[:, 4 * g:4 * g + 4, hoff:hoff + 128]

            es2 = contextlib.ExitStack()
            with es2:
                trp0 = es2.enter_context(nc.psum_tensor("trp0", [128, 4 * 1024], BF16))
                trp1 = es2.enter_context(nc.psum_tensor("trp1", [128, 4 * 1024], BF16))
                block = es2.enter_context(nc.Block("p1"))
                trp = [trp0, trp1]

                @block.sync
                def _(sync):
                    dma_c.inc(sync.dma_start(out=cosf[:], in_=cos_in[:]), 16)
                    dma_c.inc(sync.dma_start(out=sinf[:], in_=sin_in[:]), 16)
                    dma_c.mark("cossin")
                    for t in range(NT):
                        if t >= 2:
                            dve.wait_mark(sync, f"p1app{t - 2}")
                        dxp[t % 2].inc(sync.dma_start(
                            out=xstg[t % 2][:],
                            in_=x_halo[t * 128:(t + 1) * 128, :]), 16)
                        dxp[t % 2].mark(f"p1x{t}")
                    if DEBUG:
                        dve.wait_mark(sync, "dbg_t0")
                        dma_o.inc(sync.dma_start(out=d_xstg[:], in_=scr_act[:]), 16)
                        dma_o.inc(sync.dma_start(out=d_stat[:], in_=m2c[:, 0:8]), 16)
                        dma_o.inc(sync.dma_start(out=d_xntok[:], in_=xn_tok[0][:]), 16)
                        dve.wait_mark(sync, f"p1ev{NT - 1}g7")
                        dma_o.inc(sync.dma_start(out=d_xnt[:], in_=actT[:]), 16)
                        dma_o.inc(sync.dma_start(out=d_xnh[:], in_=xnH[:]), 16)

                @block.scalar
                def _(scalar):
                    for t in range(NT):
                        dxp[t % 2].wait_mark(scalar, f"p1x{t}")
                        if t >= 1:
                            dve.wait_mark(scalar, f"p1v{t - 1}")
                        scalar.activation(
                            out=scr_act[:], in_=xstg[t % 2][:], func=AF.Square,
                            accum_out=ssq_b[:])
                        act.inc(scalar.drain())
                        act.mark(f"p1sq{t}")
                        dve.wait_mark(scalar, f"p1var{t}")
                        scalar.activation(
                            out=std_b[:], in_=var_b[:], func=AF.Sqrt)
                        act.inc(scalar.drain())
                        act.mark(f"p1std{t}")

                @block.vector
                def _(vector):
                    for t in range(NT):
                        dxp[t % 2].wait_mark(vector, f"p1x{t}")
                        vector.reduce_sum(out=sum_b[:], in_=xstg[t % 2][:], axis=AX.X)
                        vector.drain()
                        vector.tensor_scalar(
                            out=negm_b[:], in0=sum_b[:], scalar1=-1.0 / D,
                            scalar2=None, op0=ALU.mult)
                        vector.drain()
                        act.wait_mark(vector, f"p1sq{t}")
                        vector.tensor_scalar(
                            out=var_b[:], in0=ssq_b[:], scalar1=1.0 / D,
                            scalar2=EPS, op0=ALU.mult, op1=ALU.add)
                        dve.inc(vector.tensor_tensor(
                            out=tmp1_b[:], in0=negm_b[:], in1=negm_b[:], op=ALU.mult))
                        dve.mark(f"p1v{t}")
                        vector.drain()
                        vector.tensor_tensor(
                            out=var_b[:], in0=var_b[:], in1=tmp1_b[:], op=ALU.subtract)
                        dve.inc(vector.drain())
                        dve.mark(f"p1var{t}")
                        act.wait_mark(vector, f"p1std{t}")
                        vector.reciprocal(out=rstd_b[:], in_=std_b[:])
                        vector.drain()
                        if t >= 2:
                            pe.wait_mark(vector, f"p1tr{t - 2}")
                        dve.inc(vector.tensor_scalar(
                            out=xn_tok[t % 2][:], in0=xstg[t % 2][:],
                            scalar1=negm_b[:], scalar2=rstd_b[:],
                            op0=ALU.add, op1=ALU.mult))
                        dve.mark(f"p1app{t}")
                        if DEBUG and t == 0:
                            vector.tensor_copy(out=m2c[:, 3:4], in_=rstd_b[:])
                            dve.inc(vector.drain())
                            dve.mark("dbg_t0")
                        for g in range(8):
                            pe.wait_mark(vector, f"p1tg{t}g{g}")
                            dve.inc(vector.tensor_copy(
                                out=xnT_dst(t, g),
                                in_=r3(trp[g % 2], 1024)[:, :, 0:128]))
                            dve.mark(f"p1ev{t}g{g}")

                @block.tensor
                def _(tensor):
                    for t in range(NT):
                        dve.wait_mark(tensor, f"p1app{t}")
                        for g in range(8):
                            if 8 * t + g >= 2:
                                pg = 8 * t + g - 2
                                dve.wait_mark(tensor, f"p1ev{pg // 8}g{pg % 8}")
                            for j in range(4):
                                kc = 4 * g + j
                                pe.inc(tensor.transpose(
                                    out=trp[g % 2][:, j * 1024:j * 1024 + 128],
                                    in_=xn_tok[t % 2][:, kc * 128:(kc + 1) * 128],
                                    identity=ident[:]))
                            pe.mark(f"p1tg{t}g{g}")
                        pe.mark(f"p1tr{t}")

            GRP = [(0, 256), (256, 512), (768, 512), (1280, 256)]

            def kv_rhs(kc, gi):
                if gi == 0:
                    return xnH[:, kc * 2 * HALO: kc * 2 * HALO + 256]
                if gi == 1:
                    return actT[:, kc * OWN: kc * OWN + 512]
                if gi == 2:
                    return actT[:, kc * OWN + 512: kc * OWN + 1024]
                return xnH[:, kc * 2 * HALO + 256: kc * 2 * HALO + 512]

            es3 = contextlib.ExitStack()
            with es3:
                wk_sb = es3.enter_context(nc.sbuf_tensor("wk_sb", [128, DC * 128], BF16))
                wv_sb = es3.enter_context(nc.sbuf_tensor("wv_sb", [128, DC * 128], BF16))
                kps2 = es3.enter_context(nc.psum_tensor("kps2", [128, 512], F32))
                vps2 = es3.enter_context(nc.psum_tensor("vps2", [128, 512], F32))
                trp3 = es3.enter_context(nc.psum_tensor("trp3", [128, 4 * 1024], BF16))
                block = es3.enter_context(nc.Block("p2"))
                @block.gpsimd
                def _(gpsimd):
                    dwm.inc(gpsimd.dma_start(out=wk_sb[:], in_=wk_t[:]), 16)
                    dwm.inc(gpsimd.dma_start(out=wv_sb[:], in_=wv_t[:]), 16)
                    dwm.mark("wkv")

                @block.sync
                def _(sync):
                    if DEBUG:
                        dve.wait_mark(sync, "p2kr3")
                        dve.wait_mark(sync, "p2ve3")
                        dma_o.inc(sync.dma_start(out=d_kr[:], in_=kr[:]), 16)
                        dma_o.inc(sync.dma_start(out=d_vtok[:], in_=vtok[:]), 16)

                @block.tensor
                def _(tensor):
                    dwm.wait_mark(tensor, "wkv")
                    dma_c.wait_mark(tensor, "cossin")
                    for gi, (l0, n) in enumerate(GRP):
                        if gi >= 1:
                            dve.wait_mark(tensor, f"p2kr{gi - 1}")
                        for kc in range(DC):
                            pe.inc(tensor.matmul(
                                out=kps2[:, 0:n],
                                lhsT=wk_sb[:, kc * 128:(kc + 1) * 128],
                                rhs=kv_rhs(kc, gi),
                                start=(kc == 0), stop=(kc == DC - 1)))
                        pe.mark(f"p2k{gi}")
                    for gi, (l0, n) in enumerate(GRP):
                        if gi >= 1:
                            dve.wait_mark(tensor, f"p2vs{gi - 1}")
                        for kc in range(DC):
                            pe.inc(tensor.matmul(
                                out=vps2[:, 0:n],
                                lhsT=wv_sb[:, kc * 128:(kc + 1) * 128],
                                rhs=kv_rhs(kc, gi),
                                start=(kc == 0), stop=(kc == DC - 1)))
                        pe.mark(f"p2v{gi}")
                        dve.wait_mark(tensor, f"p2vs{gi}")
                        if gi >= 1:
                            dve.wait_mark(tensor, f"p2ve{gi - 1}")
                        for j in range(n // 128):
                            pe.inc(tensor.transpose(
                                out=trp3[:, j * 1024:j * 1024 + 128],
                                in_=xn_tok0[:, j * 128:(j + 1) * 128],
                                identity=ident[:]))
                        pe.mark(f"p2vt{gi}")

                @block.vector
                def _(vector):
                    for gi, (l0, n) in enumerate(GRP):
                        pe.wait_mark(vector, f"p2k{gi}")
                        vector.tensor_tensor(
                            out=xstg0[:, 0:n], in0=kps2[:, 0:n],
                            in1=cosf[:, l0:l0 + n], op=ALU.mult)
                        vector.tensor_tensor(
                            out=xstg1[0:64, 0:n], in0=kps2[64:128, 0:n],
                            in1=sinf[0:64, l0:l0 + n], op=ALU.mult)
                        vector.tensor_tensor(
                            out=xstg1[64:128, 0:n], in0=kps2[0:64, 0:n],
                            in1=sinf[64:128, l0:l0 + n], op=ALU.mult)
                        dve.inc(vector.tensor_tensor(
                            out=kr[:, l0:l0 + n], in0=xstg0[:, 0:n],
                            in1=xstg1[:, 0:n], op=ALU.add))
                        dve.mark(f"p2kr{gi}")
                    for gi, (l0, n) in enumerate(GRP):
                        pe.wait_mark(vector, f"p2v{gi}")
                        if gi >= 1:
                            pe.wait_mark(vector, f"p2vt{gi - 1}")
                        vector.tensor_copy(
                            out=xn_tok0[:, 0:n], in_=vps2[:, 0:n])
                        dve.inc(vector.drain())
                        dve.mark(f"p2vs{gi}")
                        pe.wait_mark(vector, f"p2vt{gi}")
                        nt_g = n // 128
                        dve.inc(vector.tensor_copy(
                            out=r3(vtok, 128)[:, l0 // 128:l0 // 128 + nt_g, :],
                            in_=r3(trp3, 1024)[:, 0:nt_g, 0:128]))
                        dve.mark(f"p2ve{gi}")

        def prev_hm(h, m):
            return (h, m - 1) if m > 0 else (h - 1, 4)

        es_fin = contextlib.ExitStack()
        fin = es_fin.enter_context(nc.sbuf_tensor("fin", [128, H * OWN], BF16, side="right"))
        es4 = contextlib.ExitStack()
        with es4:
            cosq = es4.enter_context(nc.sbuf_tensor("cosq", [128, OWN], F32))
            sinq = es4.enter_context(nc.sbuf_tensor("sinq", [128, OWN], F32))
            alp = es4.enter_context(nc.sbuf_tensor("alp", [128, OWN], BF16))
            alm = es4.enter_context(nc.sbuf_tensor("alm", [128, OWN], BF16))
            kmask = es4.enter_context(nc.sbuf_tensor("kmask", [128, 256], BF16))
            wqb0 = es4.enter_context(nc.sbuf_tensor("wqb0", [128, D], BF16))
            wqb1 = es4.enter_context(nc.sbuf_tensor("wqb1", [128, D], BF16))
            qr0 = es4.enter_context(nc.sbuf_tensor("qr0", [128, OWN], BF16))
            qr1 = es4.enter_context(nc.sbuf_tensor("qr1", [128, OWN], BF16))
            sg0 = es4.enter_context(nc.sbuf_tensor("sg0", [128, 2048], BF16))
            sg1 = es4.enter_context(nc.sbuf_tensor("sg1", [128, 2048], BF16))
            wout = es4.enter_context(nc.sbuf_tensor("wout", [128, 5 * 512], BF16))
            recip_sb = es4.enter_context(nc.sbuf_tensor("recip_sb", [128, 512], F32))
            rps1 = es4.enter_context(nc.sbuf_tensor("rps1", [128, 512], F32))
            qev = es4.enter_context(nc.sbuf_tensor("qev", [128, 1024], F32))
            qev2 = es4.enter_context(nc.sbuf_tensor("qev2", [128, 1024], F32))
            rps2 = es4.enter_context(nc.sbuf_tensor("rps2", [128, 512], F32))
            bls1 = es4.enter_context(nc.sbuf_tensor("bls1", [128, OWN], F32))
            bls2 = es4.enter_context(nc.sbuf_tensor("bls2", [128, OWN], F32))
            qps = es4.enter_context(nc.psum_tensor("qps", [128, 1024], F32))
            scps = es4.enter_context(nc.psum_tensor("scps", [128, 1024], F32))
            bcps = es4.enter_context(nc.psum_tensor("bcps", [128, 1024], F32))
            avps = es4.enter_context(nc.psum_tensor("avps", [128, 1024], F32))
            block = es4.enter_context(nc.Block("p3"))
            wqb = [wqb0, wqb1]
            qrb = [qr0, qr1]
            sgb = [sg0, sg1]

            @block.sync
            def _(sync):
                dma_c.inc(sync.dma_start(out=cosq[:], in_=cos_in[:, HALO:HALO + OWN]), 16)
                dma_c.inc(sync.dma_start(out=sinq[:], in_=sin_in[:, HALO:HALO + OWN]), 16)
                dma_c.inc(sync.dma_start(out=alp[:], in_=alpha_in[:]), 16)
                dma_c.inc(sync.dma_start(out=alm[:], in_=alm1_in[:]), 16)
                dma_c.inc(sync.dma_start(out=kmask[:], in_=kmask_in[:]), 16)
                dma_c.mark("p3c")
                if DEBUG:
                    pool.wait_mark(sync, f"blend{H - 1}")
                    dma_o.inc(sync.dma_start(out=d_fin[:], in_=fin[:]), 16)

            @block.gpsimd
            def _(gpsimd):
                dma_c.wait_mark(gpsimd, "p3c")

                def rope_pool(hh, g):
                    dve.wait_mark(gpsimd, f"qev{hh}g{g}")
                    gpsimd.tensor_tensor(
                        out=rps1[:], in0=qev[:, g * 512:(g + 1) * 512],
                        in1=cosq[:, g * 512:(g + 1) * 512], op=ALU.mult)
                    gpsimd.tensor_tensor(
                        out=rps2[:], in0=qev2[:, g * 512:(g + 1) * 512],
                        in1=sinq[:, g * 512:(g + 1) * 512], op=ALU.mult)
                    if hh >= 1 and g == 0:
                        pe.wait_mark(gpsimd, f"scm{hh - 1}m4p1")
                    pool.inc(gpsimd.tensor_tensor(
                        out=qrb[hh % 2][:, g * 512:(g + 1) * 512],
                        in0=rps1[:], in1=rps2[:], op=ALU.add))
                    pool.mark(f"rope{hh}g{g}")

                for h in range(-2, H):
                    hh = h + 2
                    if hh < H:
                        if hh >= 2:
                            pe.wait_mark(gpsimd, f"qmm{hh - 2}")
                        dwp[hh % 2].inc(gpsimd.dma_start(
                            out=wqb[hh % 2][:], in_=wq_t[hh, :, :]), 16)
                        dwp[hh % 2].mark(f"wq{hh}")
                    if 0 <= h + 1 < H:
                        rope_pool(h + 1, 0)
                        rope_pool(h + 1, 1)
                    if h >= 0:
                        dve.wait_mark(gpsimd, f"wev{h}m4")
                        prev = r3(wout, 512)[:, 0:4, 256:512]
                        cur = r3(wout, 512)[:, 1:5, 0:256]
                        alm_v = r3(alm, 256)[:, :, :]
                        alp_v = r3(alp, 256)[:, :, :]
                        gpsimd.tensor_tensor(
                            out=r3(bls1, 256)[:, :, :], in0=prev, in1=alm_v, op=ALU.mult)
                        gpsimd.tensor_tensor(
                            out=r3(bls2, 256)[:, :, :], in0=cur, in1=alp_v, op=ALU.mult)
                        pool.inc(gpsimd.tensor_tensor(
                            out=fin[:, h * OWN:(h + 1) * OWN],
                            in0=bls1[:], in1=bls2[:], op=ALU.add))
                        pool.mark(f"blend{h}")

            @block.tensor
            def _(tensor):
                dma_c.wait_mark(tensor, "p3c")

                def emit_q(hh, g, kc):
                    if kc == 0:
                        dwp[hh % 2].wait_mark(tensor, f"wq{hh}")
                        if hh >= 1:
                            dve.wait_mark(tensor, f"qev{hh - 1}g{g}")
                    pe.inc(tensor.matmul(
                        out=qps[:, g * 512:(g + 1) * 512],
                        lhsT=wqb[hh % 2][:, kc * 128:(kc + 1) * 128],
                        rhs=actT[:, kc * OWN + g * 512: kc * OWN + (g + 1) * 512],
                        start=(kc == 0), stop=(kc == DC - 1)))
                    if kc == DC - 1:
                        pe.mark(f"q{hh}g{g}")
                        if g == 1:
                            pe.mark(f"qmm{hh}")

                for g in range(2):
                    for kc in range(DC):
                        emit_q(0, g, kc)

                for h in range(H):
                    fillers = ([(g, kc) for g in range(2) for kc in range(DC)]
                               if h + 1 < H else [])
                    fpos = [0]

                    def fill(n):
                        while n > 0 and fpos[0] < len(fillers):
                            g, kc = fillers[fpos[0]]
                            emit_q(h + 1, g, kc)
                            fpos[0] += 1
                            n -= 1

                    pool.wait_mark(tensor, f"rope{h}g1")
                    for m in range(5):
                        qlo, qn = QCOLS[m]
                        koff = 256 * m
                        sg = sgb[m % 2]
                        for p in range(2):
                            if not (h == 0 and m == 0 and p == 0):
                                if p == 1:
                                    act.wait_mark(tensor, f"sg{h}m{m}p0")
                                else:
                                    ph, pm = prev_hm(h, m)
                                    act.wait_mark(tensor, f"sg{ph}m{pm}p1")
                            for i in (2 * p, 2 * p + 1):
                                pe.inc(tensor.matmul(
                                    out=scps[:, (i - 2 * p) * 512:(i - 2 * p) * 512 + qn],
                                    lhsT=kr[:, koff + i * 128: koff + (i + 1) * 128],
                                    rhs=qrb[h % 2][:, qlo:qlo + qn],
                                    start=True, stop=True))
                            pe.mark(f"scm{h}m{m}p{p}")
                            fill(8)
                            act.wait_mark(tensor, f"sg{h}m{m}p{p}")
                            if p == 0 and (5 * h + m >= 2):
                                pk = 5 * h + m - 2
                                dve.wait_mark(tensor, f"recip{pk // 5}m{pk % 5}")
                            for i in (2 * p, 2 * p + 1):
                                masked = (m == 4 and i >= 2)
                                pe.inc(tensor.matmul(
                                    out=bcps[:, (m % 2) * 512:(m % 2) * 512 + qn],
                                    lhsT=kmask[:, 128:256] if masked else kmask[:, 0:128],
                                    rhs=sg[:, i * 512: i * 512 + qn],
                                    start=(i == 0), stop=(i == 3)))
                        pe.mark(f"bc{h}m{m}")
                        if 5 * h + m >= 2:
                            pk = 5 * h + m - 2
                            dve.wait_mark(tensor, f"wev{pk // 5}m{pk % 5}")
                        for i in range(4):
                            tt = 2 * m + i
                            pe.inc(tensor.matmul(
                                out=avps[:, (m % 2) * 512:(m % 2) * 512 + qn],
                                lhsT=vtok[:, tt * 128:(tt + 1) * 128],
                                rhs=sg[:, i * 512: i * 512 + qn],
                                start=(i == 0), stop=(i == 3)))
                        pe.mark(f"av{h}m{m}")
                    fill(len(fillers))

            @block.scalar
            def _(scalar):
                scalar.activation(out=warm_b[:, 2:3], in_=rbsb[0:1, 0:1],
                                  func=AF.Sigmoid)
                for h in range(H):
                    for m in range(5):
                        qlo, qn = QCOLS[m]
                        sg = sgb[m % 2]
                        for p in range(2):
                            pe.wait_mark(scalar, f"scm{h}m{m}p{p}")
                            if p == 0 and not (h == 0 and m <= 1):
                                ph, pm = (h, m - 2) if m >= 2 else (h - 1, m + 3)
                                pe.wait_mark(scalar, f"av{ph}m{pm}")
                            act.inc(scalar.activation(
                                out=r3(sg, 512)[:, 2 * p:2 * p + 2, 0:qn],
                                in_=r3(scps, 512)[:, 0:2, 0:qn],
                                func=AF.Sigmoid, scale=SCALE))
                            act.mark(f"sg{h}m{m}p{p}")

            @block.vector
            def _(vector):
                dma_c.wait_mark(vector, "p3c")

                def emit_qev(hh, g):
                    pe.wait_mark(vector, f"q{hh}g{g}")
                    vector.tensor_copy(
                        out=qev[:, g * 512:(g + 1) * 512],
                        in_=qps[:, g * 512:(g + 1) * 512])
                    vector.tensor_copy(
                        out=qev2[0:64, g * 512:(g + 1) * 512],
                        in_=qps[64:128, g * 512:(g + 1) * 512])
                    dve.inc(vector.tensor_copy(
                        out=qev2[64:128, g * 512:(g + 1) * 512],
                        in_=qps[0:64, g * 512:(g + 1) * 512]))
                    dve.mark(f"qev{hh}g{g}")

                emit_qev(0, 0)
                emit_qev(0, 1)
                for h in range(H):
                    for m in range(5):
                        qlo, qn = QCOLS[m]
                        c0 = 256 if m == 0 else 0
                        pe.wait_mark(vector, f"bc{h}m{m}")
                        dve.inc(vector.reciprocal(
                            out=recip_sb[:, 0:qn],
                            in_=bcps[:, (m % 2) * 512:(m % 2) * 512 + qn]))
                        dve.mark(f"recip{h}m{m}")
                        pe.wait_mark(vector, f"av{h}m{m}")
                        if m == 0 and h >= 1:
                            pool.wait_mark(vector, f"blend{h - 1}")
                        dve.inc(vector.tensor_tensor(
                            out=wout[:, m * 512 + c0: m * 512 + c0 + qn],
                            in0=avps[:, (m % 2) * 512:(m % 2) * 512 + qn],
                            in1=recip_sb[:, 0:qn], op=ALU.mult))
                        dve.mark(f"wev{h}m{m}")
                        if m == 1 and h + 1 < H:
                            emit_qev(h + 1, 0)
                        if m == 3 and h + 1 < H:
                            emit_qev(h + 1, 1)

        with nc.sbuf_tensor("ao_tok", [128, NT_OWN * D], BF16) as ao_tok:
            stage4 = kr

            es5 = contextlib.ExitStack()
            with es5:
                wob = es5.enter_context(nc.sbuf_tensor("wob", [128, 2 * 1024], BF16))
                wops = es5.enter_context(nc.psum_tensor("wops", [128, 1024], F32))
                trp4 = es5.enter_context(nc.psum_tensor("trp4", [128, 4 * 1024], BF16))
                block = es5.enter_context(nc.Block("p4"))
                @block.sync
                def _(sync):
                    if DEBUG:
                        dve.wait_mark(sync, f"p4ev{DC - 1}g1")
                        dma_o.inc(sync.dma_start(out=d_ao[:], in_=ao_tok[:]), 16)

                @block.gpsimd
                def _(gpsimd):
                    for oc in range(DC):
                        for q4 in range(4):
                            k = oc * 4 + q4
                            if k >= 2:
                                pe.wait_mark(gpsimd, f"woq{k - 2}")
                            dwp[k % 2].inc(gpsimd.dma_start(
                                out=wob[:, (k % 2) * 1024:(k % 2 + 1) * 1024],
                                in_=wo_t[oc, :, q4 * 1024:(q4 + 1) * 1024]), 16)
                            dwp[k % 2].mark(f"wo{k}")

                @block.tensor
                def _(tensor):
                    for oc in range(DC):
                        for q4 in range(4):
                            k = oc * 4 + q4
                            dwp[k % 2].wait_mark(tensor, f"wo{k}")
                            if q4 == 0 and oc >= 1:
                                act.wait_mark(tensor, f"woev{oc - 1}g1")
                            for f8 in range(8):
                                fc = 8 * q4 + f8
                                for g in range(2):
                                    pe.inc(tensor.matmul(
                                        out=wops[:, g * 512:(g + 1) * 512],
                                        lhsT=wob[:, (k % 2) * 1024 + f8 * 128:
                                                  (k % 2) * 1024 + (f8 + 1) * 128],
                                        rhs=fin[:, fc * OWN + g * 512:
                                                fc * OWN + (g + 1) * 512],
                                        start=(fc == 0), stop=(fc == DC - 1)))
                            pe.mark(f"woq{k}")
                        pe.mark(f"womm{oc}")
                        for g in range(2):
                            act.wait_mark(tensor, f"woev{oc}g{g}")
                            if 2 * oc + g >= 1:
                                pk = 2 * oc + g - 1
                                dve.wait_mark(tensor, f"p4ev{pk // 2}g{pk % 2}")
                            for j in range(4):
                                pe.inc(tensor.transpose(
                                    out=trp4[:, j * 1024:j * 1024 + 128],
                                    in_=stage4[:, g * 512 + j * 128:
                                               g * 512 + (j + 1) * 128],
                                    identity=ident[:]))
                            pe.mark(f"wotr{oc}g{g}")

                @block.scalar
                def _(scalar):
                    scalar.activation(out=warm_b[:, 3:4], in_=rbsb[0:1, 0:1],
                                      func=AF.Sqrt)
                    for oc in range(DC):
                        pe.wait_mark(scalar, f"womm{oc}")
                        for g in range(2):
                            if oc >= 1:
                                pe.wait_mark(scalar, f"wotr{oc - 1}g{g}")
                            scalar.copy(
                                out=stage4[:, g * 512:(g + 1) * 512],
                                in_=wops[:, g * 512:(g + 1) * 512])
                            act.inc(scalar.drain())
                            act.mark(f"woev{oc}g{g}")

                @block.vector
                def _(vector):
                    for oc in range(DC):
                        for g in range(2):
                            pe.wait_mark(vector, f"wotr{oc}g{g}")
                            dve.inc(vector.tensor_copy(
                                out=r3(ao_tok, D)[:, 4 * g:4 * g + 4,
                                                  oc * 128:(oc + 1) * 128],
                                in_=r3(trp4, 1024)[:, :, 0:128]))
                            dve.mark(f"p4ev{oc}g{g}")

            es_fin.close()

            es6 = contextlib.ExitStack()
            with es6:
                xstg5 = es6.enter_context(nc.sbuf_tensor("xstg5", [128, D], F32))
                xm5 = es6.enter_context(nc.sbuf_tensor("xm5", [128, D], F32))
                scr5 = es6.enter_context(nc.sbuf_tensor("scr5", [128, D], F32))
                xn3s = es6.enter_context(nc.sbuf_tensor("xn3s", [128, D], BF16))
                trp5 = es6.enter_context(nc.psum_tensor("trp5", [128, 4 * 1024], BF16))
                block = es6.enter_context(nc.Block("p5"))
                @block.sync
                def _(sync):
                    for t in range(NT_OWN):
                        if t >= 1:
                            dve.wait_mark(sync, f"p5xm{t - 1}")
                        dxp[0].inc(sync.dma_start(
                            out=xstg5[:],
                            in_=x_halo[(t + 2) * 128:(t + 3) * 128, :]), 16)
                        dxp[0].mark(f"p5x{t}")
                    if t == NT_OWN - 1 and DEBUG:
                        dve.wait_mark(sync, f"p5ev{NT_OWN - 1}g7")
                        dma_o.inc(sync.dma_start(out=d_xn3t[:], in_=actT[:]), 16)

                @block.scalar
                def _(scalar):
                    for t in range(NT_OWN):
                        ao_v = ao_tok[:, t * D:(t + 1) * D]
                        if t >= 1:
                            dve.wait_mark(scalar, f"p5v3{t - 1}")
                        scalar.activation(
                            out=scr5[:], in_=ao_v, func=AF.Square, accum_out=ssq_b[:])
                        act.inc(scalar.drain())
                        act.mark(f"p5sq2{t}")
                        dve.wait_mark(scalar, f"p5var2{t}")
                        scalar.activation(
                            out=std_b[:], in_=var_b[:], func=AF.Sqrt)
                        act.inc(scalar.drain())
                        act.mark(f"p5std2{t}")
                        dve.wait_mark(scalar, f"p5xm{t}")
                        scalar.activation(
                            out=scr5[:], in_=xm5[:], func=AF.Square, accum_out=ssq_b[:])
                        act.inc(scalar.drain())
                        act.mark(f"p5sq3{t}")
                        dve.wait_mark(scalar, f"p5var3{t}")
                        scalar.activation(
                            out=std_b[:], in_=var_b[:], func=AF.Sqrt)
                        act.inc(scalar.drain())
                        act.mark(f"p5std3{t}")

                @block.vector
                def _(vector):
                    for t in range(NT_OWN):
                        ao_v = ao_tok[:, t * D:(t + 1) * D]
                        vector.reduce_sum(out=sum_b[:], in_=ao_v, axis=AX.X)
                        vector.drain()
                        vector.tensor_scalar(
                            out=m2c[:, t:t + 1], in0=sum_b[:], scalar1=-1.0 / D,
                            scalar2=None, op0=ALU.mult)
                        vector.drain()
                        act.wait_mark(vector, f"p5sq2{t}")
                        vector.tensor_scalar(
                            out=var_b[:], in0=ssq_b[:], scalar1=1.0 / D,
                            scalar2=EPS, op0=ALU.mult, op1=ALU.add)
                        vector.tensor_tensor(
                            out=tmp1_b[:], in0=m2c[:, t:t + 1], in1=m2c[:, t:t + 1],
                            op=ALU.mult)
                        vector.drain()
                        vector.tensor_tensor(
                            out=var_b[:], in0=var_b[:], in1=tmp1_b[:], op=ALU.subtract)
                        dve.inc(vector.drain())
                        dve.mark(f"p5var2{t}")
                        act.wait_mark(vector, f"p5std2{t}")
                        vector.reciprocal(out=r2c[:, t:t + 1], in_=std_b[:])
                        vector.drain()
                        vector.tensor_scalar(
                            out=xm5[:], in0=ao_v, scalar1=m2c[:, t:t + 1],
                            scalar2=r2c[:, t:t + 1], op0=ALU.add, op1=ALU.mult)
                        dxp[0].wait_mark(vector, f"p5x{t}")
                        dve.inc(vector.tensor_tensor(
                            out=xm5[:], in0=xm5[:], in1=xstg5[:], op=ALU.add))
                        dve.mark(f"p5xm{t}")
                        vector.reduce_sum(out=sum_b[:], in_=xm5[:], axis=AX.X)
                        vector.drain()
                        vector.tensor_scalar(
                            out=negm_b[:], in0=sum_b[:], scalar1=-1.0 / D,
                            scalar2=None, op0=ALU.mult)
                        vector.drain()
                        act.wait_mark(vector, f"p5sq3{t}")
                        vector.tensor_scalar(
                            out=var_b[:], in0=ssq_b[:], scalar1=1.0 / D,
                            scalar2=EPS, op0=ALU.mult, op1=ALU.add)
                        vector.tensor_tensor(
                            out=tmp1_b[:], in0=negm_b[:], in1=negm_b[:], op=ALU.mult)
                        vector.drain()
                        vector.tensor_tensor(
                            out=var_b[:], in0=var_b[:], in1=tmp1_b[:], op=ALU.subtract)
                        dve.inc(vector.drain())
                        dve.mark(f"p5var3{t}")
                        dve.mark(f"p5v3{t}")
                        act.wait_mark(vector, f"p5std3{t}")
                        vector.reciprocal(out=rstd_b[:], in_=std_b[:])
                        vector.drain()
                        if t >= 1:
                            pe.wait_mark(vector, f"p5tr{t - 1}")
                        dve.inc(vector.tensor_scalar(
                            out=xn3s[:], in0=xm5[:], scalar1=negm_b[:],
                            scalar2=rstd_b[:], op0=ALU.add, op1=ALU.mult))
                        dve.mark(f"p5app{t}")
                        for g in range(8):
                            pe.wait_mark(vector, f"p5tg{t}g{g}")
                            dve.inc(vector.tensor_copy(
                                out=r3(actT, OWN)[:, 4 * g:4 * g + 4,
                                                  t * 128:(t + 1) * 128],
                                in_=r3(trp5, 1024)[:, :, 0:128]))
                            dve.mark(f"p5ev{t}g{g}")

                @block.tensor
                def _(tensor):
                    for t in range(NT_OWN):
                        dve.wait_mark(tensor, f"p5app{t}")
                        for g in range(8):
                            if 8 * t + g >= 1:
                                pg = 8 * t + g - 1
                                dve.wait_mark(tensor, f"p5ev{pg // 8}g{pg % 8}")
                            for j in range(4):
                                kc = 4 * g + j
                                pe.inc(tensor.transpose(
                                    out=trp5[:, j * 1024:j * 1024 + 128],
                                    in_=xn3s[:, kc * 128:(kc + 1) * 128],
                                    identity=ident[:]))
                            pe.mark(f"p5tg{t}g{g}")
                        pe.mark(f"p5tr{t}")

            es7 = contextlib.ExitStack()
            with es7:
                wmask = es7.enter_context(nc.sbuf_tensor("wmask", [128, OWN], BF16))
                mskscr = es7.enter_context(nc.sbuf_tensor("mskscr", [128, OWN], BF16))
                a_sb = es7.enter_context(nc.sbuf_tensor("a_sb", [128, DC * 128], BF16))
                traw = es7.enter_context(nc.sbuf_tensor("traw", [128, OWN], F32))
                lgps = es7.enter_context(nc.psum_tensor("lgps", [8, 1], F32))
                ltps = es7.enter_context(nc.psum_tensor("ltps", [1, 8], F32))
                bc8ps = es7.enter_context(nc.psum_tensor("bc8ps", [128, 8], F32))
                tps = es7.enter_context(nc.psum_tensor("tps", [128, 1024], F32))
                block = es7.enter_context(nc.Block("p6"))
                @block.sync
                def _(sync):
                    dma_c.inc(sync.dma_start(out=wmask[:], in_=wmask_in[:]), 16)
                    dma_c.mark("p6c")
                    if DEBUG:
                        dve.wait_mark(sync, "router_partial")
                        dma_o.inc(sync.dma_start(out=d_rt[:], in_=router_sb[:]), 16)
                        dve.wait_mark(sync, "tsc")
                        dma_o.inc(sync.dma_start(out=d_qs[:], in_=qsum_sb[:]), 16)
                        dma_o.inc(sync.dma_start(out=d_rtg[:], in_=rtg_sb[:]), 16)
                        dma_o.inc(sync.dma_start(out=d_srt[:], in_=srt_sb[:]), 16)
                        dma_o.inc(sync.dma_start(out=d_tsc[:], in_=tsc_sb[:]), 16)

                @block.gpsimd
                def _(gpsimd):
                    dwm.inc(gpsimd.dma_start(out=a_sb[:], in_=a_t[:]), 16)
                    dwm.mark("a_sb")
                    dve.wait_mark(gpsimd, "router_partial")
                    dwm.inc(gpsimd.dma_start(out=cc_in[:], in_=router_sb[:]), 16)
                    dwm.wait_now(gpsimd)
                    ccs.inc(gpsimd.collective_compute(
                        "AllReduce", ALU.add,
                        replica_groups=[list(range(NCORE))],
                        ins=[cc_in.ap().opt()],
                        outs=[cc_out.ap().opt()]), 1)
                    ccs.wait_now(gpsimd)
                    dwm.inc(gpsimd.dma_start(out=qsum_sb[:], in_=cc_out[:]), 16)
                    dwm.mark("qsum")

                @block.tensor
                def _(tensor):
                    dwm.wait_mark(tensor, "qsum")
                    for kc in range(DC):
                        pe.inc(tensor.matmul(
                            out=lgps[:], lhsT=rwsb[:, kc * 8:(kc + 1) * 8],
                            rhs=qsum_sb[:, kc:kc + 1],
                            start=(kc == 0), stop=(kc == DC - 1)))
                    pe.mark("logits")
                    dve.wait_mark(tensor, "lg_sb")
                    pe.inc(tensor.matmul(
                        out=ltps[:], lhsT=logit_sb[:], rhs=eye8[:],
                        start=True, stop=True))
                    pe.mark("logitsT")
                    dve.wait_mark(tensor, "routing")
                    pe.inc(tensor.matmul(
                        out=bc8ps[:], lhsT=ones_f[:], rhs=rtg_sb[:],
                        start=True, stop=True))
                    pe.mark("bc8")
                    dwm.wait_mark(tensor, "a_sb")
                    for g in range(2):
                        for kc in range(DC):
                            pe.inc(tensor.matmul(
                                out=tps[:, g * 512:(g + 1) * 512],
                                lhsT=a_sb[:, kc * 128:(kc + 1) * 128],
                                rhs=actT[:, kc * OWN + g * 512: kc * OWN + (g + 1) * 512],
                                start=(kc == 0), stop=(kc == DC - 1)))
                    pe.mark("tmm")

                @block.vector
                def _(vector):
                    dma_c.wait_mark(vector, "p6c")
                    for kc in range(DC):
                        vector.scalar_tensor_tensor(
                            out=mskscr[:], in0=actT[:, kc * OWN:(kc + 1) * OWN],
                            scalar=0.0, in1=wmask[:], op0=ALU.add, op1=ALU.mult,
                            accum_out=router_sb[:, kc:kc + 1])
                    dve.inc(vector.drain(), DC)
                    dve.mark("router_partial")
                    pe.wait_mark(vector, "logits")
                    vector.tensor_scalar(
                        out=logit_sb[:], in0=lgps[:], scalar1=1.0 / NSEL,
                        scalar2=rbsb[:], op0=ALU.mult, op1=ALU.add)
                    dve.inc(vector.drain())
                    dve.mark("lg_sb")
                    pe.wait_mark(vector, "logitsT")
                    vector.tensor_copy(out=rtmp_sb[:], in_=ltps[:])
                    vector.drain()
                    vector.reduce_max(out=rmax_sb[:], in_=rtmp_sb[:], axis=AX.X)
                    vector.drain()
                    vector.tensor_scalar(
                        out=rmax_sb[:], in0=rmax_sb[:], scalar1=-1.0,
                        scalar2=None, op0=ALU.mult)
                    dve.inc(vector.drain())
                    dve.mark("sftm_pre")
                    act.wait_mark(vector, "sftm_exp")
                    vector.reduce_sum(out=rsum1_sb[:], in_=rtmp_sb[:], axis=AX.X)
                    vector.drain()
                    vector.reciprocal(out=rsum1_sb[:], in_=rsum1_sb[:])
                    vector.drain()
                    vector.tensor_scalar(
                        out=rtg_sb[:], in0=rtmp_sb[:], scalar1=rsum1_sb[:],
                        scalar2=None, op0=ALU.mult)
                    dve.inc(vector.drain())
                    dve.mark("routing")
                    pe.wait_mark(vector, "bc8")
                    vector.tensor_tensor(
                        out=bc8_sb[:], in0=bc8ps[:], in1=sel2[:], op=ALU.mult)
                    vector.drain()
                    vector.tensor_reduce(
                        out=srt_sb[:], in_=bc8_sb[:], axis=AX.X, op=ALU.add)
                    vector.drain()
                    pe.wait_mark(vector, "tmm")
                    vector.tensor_copy(out=traw[:], in_=tps[:])
                    vector.drain()
                    dve.inc(vector.tensor_scalar(
                        out=tsc_sb[:], in0=traw[:], scalar1=srt_sb[:],
                        scalar2=None, op0=ALU.mult))
                    dve.mark("tsc")

                @block.scalar
                def _(scalar):
                    scalar.activation(out=warm_b[:, 4:5], in_=rbsb[0:1, 0:1],
                                      func=AF.Exp)
                    dve.wait_mark(scalar, "sftm_pre")
                    scalar.activation(
                        out=rtmp_sb[:], in_=rtmp_sb[:], func=AF.Exp,
                        bias=rmax_sb[:])
                    act.inc(scalar.drain())
                    act.mark("sftm_exp")

            es8 = contextlib.ExitStack()
            with es8:
                ffh = es8.enter_context(nc.sbuf_tensor("ffh", [128, 4 * D], BF16))
                wfb = es8.enter_context(nc.sbuf_tensor("wfb", [128, 4 * 1024], BF16))
                b_sb = es8.enter_context(nc.sbuf_tensor("b_sb", [128, DC * 128], BF16))
                xb9 = es8.enter_context(nc.sbuf_tensor("xb9", [128, 1024], F32))
                s9a0 = es8.enter_context(nc.sbuf_tensor("s9a0", [128, 1024], F32))
                s9a1 = es8.enter_context(nc.sbuf_tensor("s9a1", [128, 1024], F32))
                s9b = es8.enter_context(nc.sbuf_tensor("s9b", [128, 1024], F32))
                junk9 = es8.enter_context(nc.sbuf_tensor("junk9", [128, 1024], BF16))
                st9 = es8.enter_context(nc.sbuf_tensor("st9", [128, 4], F32))
                ffps = es8.enter_context(nc.psum_tensor("ffps", [128, 1024], F32))
                trp8 = es8.enter_context(nc.psum_tensor("trp8", [128, 4 * 1024], BF16))
                s9a = [s9a0, s9a1]
                stage8 = kr

                for hf in range(2):
                    with nc.Block(f"p8h{hf}") as block:
                        @block.sync
                        def _(sync, hf=hf):
                            if DEBUG:
                                dve.wait_mark(sync, f"p8ev{hf}o{DC - 1}")
                                dma_o.inc(sync.dma_start(
                                    out=(d_ffh0 if hf == 0 else d_ffh1)[:],
                                    in_=ffh[:]), 16)

                        @block.gpsimd
                        def _(gpsimd, hf=hf):
                            if hf == 0:
                                dwm.inc(gpsimd.dma_start(out=b_sb[:], in_=b_t[:]), 16)
                                dwm.mark("b_sb")
                            for oc in range(DC):
                                for q4 in range(4):
                                    k = oc * 4 + q4
                                    if k >= 4:
                                        pe.wait_mark(gpsimd, f"f{hf}q{k - 4}")
                                    dwp[k % 2].inc(gpsimd.dma_start(
                                        out=wfb[:, (k % 4) * 1024:(k % 4 + 1) * 1024],
                                        in_=wf_t[oc, :, q4 * 1024:(q4 + 1) * 1024]), 16)
                                    dwp[k % 2].mark(f"wf{hf}o{oc}q{q4}")

                        @block.tensor
                        def _(tensor, hf=hf):
                            dwm.wait_mark(tensor, "b_sb")
                            for oc in range(DC):
                                fo = (oc % 2) * 512
                                for q4 in range(4):
                                    k = oc * 4 + q4
                                    dwp[k % 2].wait_mark(tensor, f"wf{hf}o{oc}q{q4}")
                                    if q4 == 0 and (hf * DC + oc >= 2):
                                        po = (hf * DC + oc - 2) % DC
                                        ph = (hf * DC + oc - 2) // DC
                                        act.wait_mark(tensor, f"ffev{ph}o{po}")
                                    for f8 in range(8):
                                        kc = 8 * q4 + f8
                                        pe.inc(tensor.matmul(
                                            out=ffps[:, fo:fo + 512],
                                            lhsT=wfb[:, (k % 4) * 1024 + f8 * 128:
                                                      (k % 4) * 1024 + (f8 + 1) * 128],
                                            rhs=actT[:, kc * OWN + hf * 512:
                                                     kc * OWN + hf * 512 + 512],
                                            start=(kc == 0), stop=False))
                                    pe.mark(f"f{hf}q{k}")
                                pe.inc(tensor.matmul(
                                    out=ffps[:, fo:fo + 512],
                                    lhsT=b_sb[:, oc * 128:(oc + 1) * 128],
                                    rhs=tsc_sb[:, hf * 512: hf * 512 + 512],
                                    start=False, stop=True))
                                pe.mark(f"ffmm{hf}o{oc}")
                                act.wait_mark(tensor, f"ffev{hf}o{oc}")
                                if hf + oc >= 1:
                                    po = oc - 1 if oc >= 1 else DC - 1
                                    ph = hf if oc >= 1 else hf - 1
                                    dve.wait_mark(tensor, f"p8ev{ph}o{po}")
                                for j in range(4):
                                    pe.inc(tensor.transpose(
                                        out=trp8[:, j * 1024:j * 1024 + 128],
                                        in_=stage8[:, j * 128:(j + 1) * 128],
                                        identity=ident[:]))
                                pe.mark(f"fftr{hf}o{oc}")

                        @block.scalar
                        def _(scalar, hf=hf):
                            if hf == 0:
                                scalar.activation(out=warm_b[:, 5:6],
                                                  in_=rbsb[0:1, 0:1], func=AF.Sqrt)
                            for oc in range(DC):
                                pe.wait_mark(scalar, f"ffmm{hf}o{oc}")
                                if hf + oc >= 1:
                                    po = oc - 1 if oc >= 1 else DC - 1
                                    ph = hf if oc >= 1 else hf - 1
                                    pe.wait_mark(scalar, f"fftr{ph}o{po}")
                                scalar.activation(
                                    out=stage8[:, 0:512],
                                    in_=ffps[:, (oc % 2) * 512:(oc % 2) * 512 + 512],
                                    func=AF.Identity, bias=biassb[:, oc:oc + 1])
                                act.inc(scalar.drain())
                                act.mark(f"ffev{hf}o{oc}")

                        @block.vector
                        def _(vector, hf=hf):
                            for oc in range(DC):
                                pe.wait_mark(vector, f"fftr{hf}o{oc}")
                                dve.inc(vector.tensor_copy(
                                    out=r3(ffh, D)[:, :, oc * 128:(oc + 1) * 128],
                                    in_=r3(trp8, 1024)[:, :, 0:128]))
                                dve.mark(f"p8ev{hf}o{oc}")

                    with nc.Block(f"p9h{hf}") as block:
                        @block.sync
                        def _(sync, hf=hf):
                            for k in range(17):
                                tl, cb = divmod(k, 4)
                                T = 4 * hf + tl
                                if k < 16:
                                    if k >= 1:
                                        dve.wait_mark(sync, f"p9xm{hf}k{k - 1}")
                                    dxp[0].inc(sync.dma_start(
                                        out=xb9[:],
                                        in_=x_halo[(T + 2) * 128:(T + 3) * 128,
                                                   cb * 1024:(cb + 1) * 1024]), 16)
                                    dxp[0].mark(f"p9x{hf}t{tl}c{cb}")
                                if k >= 1:
                                    kx = k - 1
                                    ptl, pcb = divmod(kx, 4)
                                    PT = 4 * hf + ptl
                                    dve.wait_mark(sync, f"p9f{hf}k{kx}")
                                    dop[kx % 2].inc(sync.dma_start(
                                        out=out[PT * 128:(PT + 1) * 128,
                                                pcb * 1024:(pcb + 1) * 1024],
                                        in_=s9a[kx % 2][:]), 16)
                                    dop[kx % 2].mark(f"p9o{hf}k{kx}")

                        @block.scalar
                        def _(scalar, hf=hf):
                            for tl in range(4):
                                ff_t3 = r3(ffh, 1024)[:, 4 * tl:4 * tl + 4, :]
                                if hf + tl >= 1:
                                    dve.wait_mark(scalar, f"p9st{hf}t{tl - 1}"
                                                  if tl >= 1 else f"p9st{hf - 1}t3")
                                for cb in range(4):
                                    scalar.activation(
                                        out=junk9[:], in_=ffh[:, (4 * tl + cb) * 1024:
                                                             (4 * tl + cb + 1) * 1024],
                                        func=AF.Square, accum_out=st9[:, cb:cb + 1])
                                act.inc(scalar.drain(), 4)
                                act.mark(f"p9sq{hf}t{tl}")
                                dve.wait_mark(scalar, f"p9var{hf}t{tl}")
                                scalar.activation(
                                    out=std_b[:], in_=var_b[:], func=AF.Sqrt)
                                act.inc(scalar.drain())
                                act.mark(f"p9std{hf}t{tl}")

                        @block.vector
                        def _(vector, hf=hf):
                            for tl in range(4):
                                T = 4 * hf + tl
                                ff_v = ffh[:, tl * D:(tl + 1) * D]
                                vector.reduce_sum(out=sum_b[:], in_=ff_v, axis=AX.X)
                                vector.drain()
                                vector.tensor_scalar(
                                    out=negm_b[:], in0=sum_b[:], scalar1=-1.0 / D,
                                    scalar2=None, op0=ALU.mult)
                                vector.drain()
                                act.wait_mark(vector, f"p9sq{hf}t{tl}")
                                vector.tensor_reduce(
                                    out=ssq_b[:], in_=st9[:], axis=AX.X, op=ALU.add)
                                vector.drain()
                                vector.tensor_scalar(
                                    out=var_b[:], in0=ssq_b[:], scalar1=1.0 / D,
                                    scalar2=EPS, op0=ALU.mult, op1=ALU.add)
                                vector.tensor_tensor(
                                    out=tmp1_b[:], in0=negm_b[:], in1=negm_b[:],
                                    op=ALU.mult)
                                vector.drain()
                                vector.tensor_tensor(
                                    out=var_b[:], in0=var_b[:], in1=tmp1_b[:],
                                    op=ALU.subtract)
                                dve.inc(vector.drain())
                                dve.mark(f"p9var{hf}t{tl}")
                                dve.mark(f"p9st{hf}t{tl}")
                                act.wait_mark(vector, f"p9std{hf}t{tl}")
                                vector.reciprocal(out=rstd_b[:], in_=std_b[:])
                                vector.drain()
                                for cb in range(4):
                                    kx = 4 * tl + cb
                                    sa = s9a[kx % 2]
                                    if kx >= 2:
                                        dop[kx % 2].wait_mark(vector, f"p9o{hf}k{kx - 2}")
                                    vector.tensor_scalar(
                                        out=sa[:],
                                        in0=ao_tok[:, T * D + cb * 1024:
                                                   T * D + (cb + 1) * 1024],
                                        scalar1=m2c[:, T:T + 1], scalar2=r2c[:, T:T + 1],
                                        op0=ALU.add, op1=ALU.mult)
                                    dxp[0].wait_mark(vector, f"p9x{hf}t{tl}c{cb}")
                                    dve.inc(vector.tensor_tensor(
                                        out=sa[:], in0=sa[:], in1=xb9[:], op=ALU.add))
                                    dve.mark(f"p9xm{hf}k{kx}")
                                    vector.tensor_scalar(
                                        out=s9b[:],
                                        in0=ffh[:, tl * D + cb * 1024:
                                                tl * D + (cb + 1) * 1024],
                                        scalar1=negm_b[:], scalar2=rstd_b[:],
                                        op0=ALU.add, op1=ALU.mult)
                                    vector.tensor_tensor(
                                        out=sa[:], in0=sa[:], in1=s9b[:], op=ALU.add)
                                    dve.inc(vector.drain())
                                    dve.mark(f"p9f{hf}k{kx}")


        with nc.Block("fin") as block:
            @block.sync
            def _(sync):
                dop[0].wait_now(sync)
                dop[1].wait_now(sync)
                if DEBUG:
                    dma_o.wait_now(sync)

    return nc



def _host_prep(L0_attn_x, L0_attn_Wq, L0_attn_Wk, L0_attn_Wv, L0_attn_Wo,
               L0_ffn_W, L0_ffn_b, L0_ffn_A, L0_ffn_B,
               L0_ffn_router_W, L0_ffn_router_b):
    f32 = np.float32
    bf16 = ml_dtypes.bfloat16

    def tile_w(W):
        return np.ascontiguousarray(
            W.reshape(DC, 128, DC, 128).transpose(0, 3, 2, 1).reshape(DC, 128, DC * 128))

    def tile_small(Wsm):
        return np.ascontiguousarray(
            Wsm.reshape(128, DC, 128).transpose(2, 1, 0).reshape(128, DC * 128))

    wq = tile_w(np.asarray(L0_attn_Wq, f32))
    wo = tile_w(np.asarray(L0_attn_Wo, f32))
    wf = tile_w(np.asarray(L0_ffn_W, f32))
    wk = tile_small(np.asarray(L0_attn_Wk, f32))
    wv = tile_small(np.asarray(L0_attn_Wv, f32))
    a_cat = np.asarray(L0_ffn_A, f32).reshape(128, D)
    a_tl = tile_small(a_cat)
    b_cat = np.asarray(L0_ffn_B, f32).transpose(1, 0, 2).reshape(D, 128)
    b_tl = np.ascontiguousarray(
        b_cat.reshape(DC, 128, 128).transpose(2, 0, 1).reshape(128, DC * 128))
    rw = np.ascontiguousarray(
        np.asarray(L0_ffn_router_W, f32).T.reshape(DC, 128, 8).transpose(1, 0, 2)
        .reshape(128, DC * 8))
    rb = np.ascontiguousarray(np.asarray(L0_ffn_router_b, f32).reshape(8, 1))
    bias = np.ascontiguousarray(np.asarray(L0_ffn_b, f32).reshape(DC, 128).T)

    inv_freq = (1.0 / 10000.0 ** (np.arange(0, DH, 2, dtype=f32) / DH)).astype(f32)
    t_pos = np.arange(S, dtype=f32)
    f = np.outer(t_pos, inv_freq).astype(f32)
    cosg = np.zeros((S, DH), f32)
    sing = np.zeros((S, DH), f32)
    cosg[:, 0::2] = np.cos(f)
    cosg[:, 1::2] = np.cos(f)
    sing[:, 0::2] = np.sin(f)
    sing[:, 1::2] = np.sin(f)

    alpha_blk = np.linspace(0.0, 1.0, 256, dtype=f32)

    xf = np.asarray(L0_attn_x, f32).reshape(S, D)
    xpad = np.pad(xf, ((HALO, HALO), (0, 0)))

    ident = np.eye(128, dtype=bf16)
    ones_r = np.ones((1, 128), dtype=bf16)
    eye8 = np.eye(8, dtype=f32)
    sel2 = np.zeros((128, 8), f32)
    for p in range(128):
        sel2[p, p // 16] = 2.0

    in_maps = []
    for c in range(NCORE):
        g0 = c * OWN
        xh = np.ascontiguousarray(xpad[g0:g0 + SH])
        pos = np.clip(np.arange(g0 - HALO, g0 - HALO + SH), 0, S - 1)
        cos_c = np.ascontiguousarray(cosg[pos].T)
        sin_c = cosg[pos].T * 0 + sing[pos].T
        sin_c[0:64, :] *= -1.0
        sin_c = np.ascontiguousarray(sin_c.astype(f32))
        al = np.tile(alpha_blk, 4).astype(f32)
        if c == 0:
            al[0:256] = 1.0
        alm = (np.float32(1.0) - al).astype(f32)
        al_t = np.ascontiguousarray(np.broadcast_to(al, (128, OWN))).astype(bf16)
        alm_t = np.ascontiguousarray(np.broadcast_to(alm, (128, OWN))).astype(bf16)
        km = np.ones((128, 256), np.float32)
        if c == NCORE - 1:
            km[:, 128:256] = 0.0
        km = km.astype(bf16)
        wm = np.zeros(OWN, np.float32)
        gidx = np.arange(g0, g0 + OWN)
        wm[(gidx >= 611) & (gidx <= S - 2)] = 1.0
        wm_t = np.ascontiguousarray(np.broadcast_to(wm, (128, OWN))).astype(bf16)
        in_maps.append({
            "x_halo": xh, "wq_t": wq, "wo_t": wo, "wf_t": wf,
            "wk_t": wk, "wv_t": wv, "a_t": a_tl, "b_t": b_tl,
            "rw_t": rw, "rb_in": rb, "bias_t": bias,
            "cos_in": cos_c, "sin_in": sin_c,
            "alpha_in": al_t, "alm1_in": alm_t,
            "kmask_in": km, "wmask_in": wm_t,
            "ident_in": ident, "ones_in": ones_r, "eye8_in": eye8,
            "sel2_in": sel2,
        })
    return in_maps


def kernel(L0_attn_x, L0_attn_Wq, L0_attn_Wk, L0_attn_Wv, L0_attn_Wo,
           L0_ffn_W, L0_ffn_b, L0_ffn_A, L0_ffn_B,
           L0_ffn_router_W, L0_ffn_router_b):
    in_maps = _host_prep(L0_attn_x, L0_attn_Wq, L0_attn_Wk, L0_attn_Wv,
                         L0_attn_Wo, L0_ffn_W, L0_ffn_b, L0_ffn_A, L0_ffn_B,
                         L0_ffn_router_W, L0_ffn_router_b)
    marks = {}
    build_program(marks, final_pass=False)
    nc = build_program(marks, final_pass=True)
    res = run_bass_kernel_spmd(nc, in_maps, core_ids=list(range(NCORE)))
    full = np.concatenate(
        [np.asarray(res.results[c]["out"], dtype=np.float32) for c in range(NCORE)],
        axis=0)
    return full.reshape(1, S, D)


# version 20 (final)
# speedup vs baseline: 1.0398x; 1.0064x over previous
import numpy as np
import ml_dtypes
import concourse.bass as bass
import concourse.mybir as mybir
from concourse.bass_utils import run_bass_kernel_spmd

F32 = mybir.dt.float32
BF16 = mybir.dt.bfloat16
AF = mybir.ActivationFunctionType
ALU = mybir.AluOpType
AX = mybir.AxisListType

NCORE = 8
S = 8192
D = 4096
DC = 32
H = 32
DH = 128
OWN = 1024
HALO = 256
SH = OWN + 2 * HALO
NT = SH // 128
NT_OWN = OWN // 128
SCALE = 1.0 / float(np.sqrt(DH))
EPS = 1e-5
NSEL = 7580.0
DEBUG = False
QCOLS = [(0, 256), (0, 512), (256, 512), (512, 512), (768, 256)]


class SemCtr:

    def __init__(self, handle, marks, name, final_pass):
        self.h = handle
        self.v = 0
        self.marks = marks
        self.name = name
        self.final_pass = final_pass

    def inc(self, inst, n=1):
        inst.then_inc(self.h, n)
        self.v += n

    def mark(self, key):
        k = (self.name, key)
        if self.final_pass:
            assert self.marks.get(k) == self.v, (k, self.marks.get(k), self.v)
        else:
            self.marks[k] = self.v

    def wait_mark(self, eng, key):
        v = self.marks.get((self.name, key), 0)
        if v > 0:
            eng.wait_ge(self.h, v)

    def wait_now(self, eng):
        if self.v > 0:
            eng.wait_ge(self.h, self.v)


def r3(t, inner, n_outer=None):
    return t.ap().rearrange("p (a b) -> p a b", b=inner)


def build_program(marks, final_pass):
    nc = bass.Bass("TRN2", target_bir_lowering=False, debug=False)

    x_halo = nc.dram_tensor("x_halo", [SH, D], F32, kind="ExternalInput")
    wq_t = nc.dram_tensor("wq_t", [DC, 128, DC * 128], F32, kind="ExternalInput")
    wo_t = nc.dram_tensor("wo_t", [DC, 128, DC * 128], F32, kind="ExternalInput")
    wf_t = nc.dram_tensor("wf_t", [DC, 128, DC * 128], F32, kind="ExternalInput")
    wk_t = nc.dram_tensor("wk_t", [128, DC * 128], F32, kind="ExternalInput")
    wv_t = nc.dram_tensor("wv_t", [128, DC * 128], F32, kind="ExternalInput")
    a_t = nc.dram_tensor("a_t", [128, DC * 128], F32, kind="ExternalInput")
    b_t = nc.dram_tensor("b_t", [128, DC * 128], F32, kind="ExternalInput")
    rw_t = nc.dram_tensor("rw_t", [128, DC * 8], F32, kind="ExternalInput")
    rb_in = nc.dram_tensor("rb_in", [8, 1], F32, kind="ExternalInput")
    bias_t = nc.dram_tensor("bias_t", [128, DC], F32, kind="ExternalInput")
    cos_in = nc.dram_tensor("cos_in", [128, SH], F32, kind="ExternalInput")
    sin_in = nc.dram_tensor("sin_in", [128, SH], F32, kind="ExternalInput")
    alpha_in = nc.dram_tensor("alpha_in", [128, OWN], BF16, kind="ExternalInput")
    alm1_in = nc.dram_tensor("alm1_in", [128, OWN], BF16, kind="ExternalInput")
    kmask_in = nc.dram_tensor("kmask_in", [128, 256], BF16, kind="ExternalInput")
    wmask_in = nc.dram_tensor("wmask_in", [128, OWN], BF16, kind="ExternalInput")
    ident_in = nc.dram_tensor("ident_in", [128, 128], BF16, kind="ExternalInput")
    ones_in = nc.dram_tensor("ones_in", [1, 128], BF16, kind="ExternalInput")
    eye8_in = nc.dram_tensor("eye8_in", [8, 8], F32, kind="ExternalInput")
    sel2_in = nc.dram_tensor("sel2_in", [128, 8], F32, kind="ExternalInput")
    out = nc.dram_tensor("out", [OWN, D], F32, kind="ExternalOutput")
    cc_in = nc.dram_tensor("cc_in", [128, DC], F32, kind="Internal")
    cc_out = nc.dram_tensor("cc_out", [128, DC], F32, kind="Internal")
    if DEBUG:
        d_xnt = nc.dram_tensor("d_xnt", [128, DC * OWN], BF16, kind="ExternalOutput")
        d_xnh = nc.dram_tensor("d_xnh", [128, DC * 2 * HALO], BF16, kind="ExternalOutput")
        d_kr = nc.dram_tensor("d_kr", [128, SH], BF16, kind="ExternalOutput")
        d_vtok = nc.dram_tensor("d_vtok", [128, NT * 128], BF16, kind="ExternalOutput")
        d_fin = nc.dram_tensor("d_fin", [128, H * OWN], BF16, kind="ExternalOutput")
        d_ao = nc.dram_tensor("d_ao", [128, NT_OWN * D], BF16, kind="ExternalOutput")
        d_xn3t = nc.dram_tensor("d_xn3t", [128, DC * OWN], BF16, kind="ExternalOutput")
        d_rt = nc.dram_tensor("d_rt", [128, DC], F32, kind="ExternalOutput")
        d_qs = nc.dram_tensor("d_qs", [128, DC], F32, kind="ExternalOutput")
        d_rtg = nc.dram_tensor("d_rtg", [1, 8], F32, kind="ExternalOutput")
        d_srt = nc.dram_tensor("d_srt", [128, 1], F32, kind="ExternalOutput")
        d_tsc = nc.dram_tensor("d_tsc", [128, OWN], BF16, kind="ExternalOutput")
        d_ffh0 = nc.dram_tensor("d_ffh0", [128, 4 * D], BF16, kind="ExternalOutput")
        d_ffh1 = nc.dram_tensor("d_ffh1", [128, 4 * D], BF16, kind="ExternalOutput")
        d_xstg = nc.dram_tensor("d_xstg", [128, D], F32, kind="ExternalOutput")
        d_xntok = nc.dram_tensor("d_xntok", [128, D], BF16, kind="ExternalOutput")
        d_stat = nc.dram_tensor("d_stat", [128, 8], F32, kind="ExternalOutput")

    import contextlib
    with contextlib.ExitStack() as es:
        actT = es.enter_context(nc.sbuf_tensor("actT", [128, DC * OWN], BF16))
        kr = es.enter_context(nc.sbuf_tensor("kr", [128, SH], BF16))
        vtok = es.enter_context(nc.sbuf_tensor("vtok", [128, NT * 128], BF16))
        ident = es.enter_context(nc.sbuf_tensor("ident", [128, 128], BF16))
        ones_r = es.enter_context(nc.sbuf_tensor("ones_r", [1, 128], BF16))
        ones_f = es.enter_context(nc.sbuf_tensor("ones_f", [1, 128], F32))
        eye8 = es.enter_context(nc.sbuf_tensor("eye8", [8, 8], F32))
        sel2 = es.enter_context(nc.sbuf_tensor("sel2", [128, 8], F32))
        rwsb = es.enter_context(nc.sbuf_tensor("rwsb", [128, DC * 8], F32))
        rbsb = es.enter_context(nc.sbuf_tensor("rbsb", [8, 1], F32))
        biassb = es.enter_context(nc.sbuf_tensor("biassb", [128, DC], F32))
        sum_b = es.enter_context(nc.sbuf_tensor("sum_b", [128, 1], F32))
        ssq_b = es.enter_context(nc.sbuf_tensor("ssq_b", [128, 1], F32))
        negm_b = es.enter_context(nc.sbuf_tensor("negm_b", [128, 1], F32))
        var_b = es.enter_context(nc.sbuf_tensor("var_b", [128, 1], F32))
        std_b = es.enter_context(nc.sbuf_tensor("std_b", [128, 1], F32))
        rstd_b = es.enter_context(nc.sbuf_tensor("rstd_b", [128, 1], F32))
        tmp1_b = es.enter_context(nc.sbuf_tensor("tmp1_b", [128, 1], F32))
        m2c = es.enter_context(nc.sbuf_tensor("m2c", [128, NT_OWN], F32))
        r2c = es.enter_context(nc.sbuf_tensor("r2c", [128, NT_OWN], F32))
        router_sb = es.enter_context(nc.sbuf_tensor("router_sb", [128, DC], F32))
        qsum_sb = es.enter_context(nc.sbuf_tensor("qsum_sb", [128, DC], F32))
        logit_sb = es.enter_context(nc.sbuf_tensor("logit_sb", [8, 1], F32))
        rtg_sb = es.enter_context(nc.sbuf_tensor("rtg_sb", [1, 8], F32))
        rtmp_sb = es.enter_context(nc.sbuf_tensor("rtmp_sb", [1, 8], F32))
        rmax_sb = es.enter_context(nc.sbuf_tensor("rmax_sb", [1, 1], F32))
        rsum1_sb = es.enter_context(nc.sbuf_tensor("rsum1_sb", [1, 1], F32))
        bc8_sb = es.enter_context(nc.sbuf_tensor("bc8_sb", [128, 8], F32))
        srt_sb = es.enter_context(nc.sbuf_tensor("srt_sb", [128, 1], F32))
        tsc_sb = es.enter_context(nc.sbuf_tensor("tsc_sb", [128, OWN], BF16))
        warm_b = es.enter_context(nc.sbuf_tensor("warm_b", [1, 8], F32))
        _dma_c = es.enter_context(nc.semaphore("dma_c"))
        _dma_x = es.enter_context(nc.semaphore("dma_x"))
        _dma_w = es.enter_context(nc.semaphore("dma_w"))
        _dma_o = es.enter_context(nc.semaphore("dma_o"))
        _s_pe = es.enter_context(nc.semaphore("s_pe"))
        _s_dve = es.enter_context(nc.semaphore("s_dve"))
        _s_act = es.enter_context(nc.semaphore("s_act"))
        _s_pool = es.enter_context(nc.semaphore("s_pool"))
        _s_cc = es.enter_context(nc.semaphore("s_cc"))
        _dx0 = es.enter_context(nc.semaphore("dx0"))
        _dx1 = es.enter_context(nc.semaphore("dx1"))
        _do0 = es.enter_context(nc.semaphore("do0"))
        _do1 = es.enter_context(nc.semaphore("do1"))
        _dw0 = es.enter_context(nc.semaphore("dw0"))
        _dw1 = es.enter_context(nc.semaphore("dw1"))
        _dwm = es.enter_context(nc.semaphore("dwm"))

        dma_c = SemCtr(_dma_c, marks, "dma_c", final_pass)
        dma_x = SemCtr(_dma_x, marks, "dma_x", final_pass)
        dma_w = SemCtr(_dma_w, marks, "dma_w", final_pass)
        dma_o = SemCtr(_dma_o, marks, "dma_o", final_pass)
        pe = SemCtr(_s_pe, marks, "pe", final_pass)
        dve = SemCtr(_s_dve, marks, "dve", final_pass)
        act = SemCtr(_s_act, marks, "act", final_pass)
        pool = SemCtr(_s_pool, marks, "pool", final_pass)
        ccs = SemCtr(_s_cc, marks, "cc", final_pass)
        dxp = [SemCtr(_dx0, marks, "dx0", final_pass), SemCtr(_dx1, marks, "dx1", final_pass)]
        dop = [SemCtr(_do0, marks, "do0", final_pass), SemCtr(_do1, marks, "do1", final_pass)]
        dwp = [SemCtr(_dw0, marks, "dw0", final_pass), SemCtr(_dw1, marks, "dw1", final_pass)]
        dwm = SemCtr(_dwm, marks, "dwm", final_pass)

        with nc.Block("p0") as block:
            @block.sync
            def _(sync):
                for dst, src in [
                    (ident[:], ident_in[:]), (ones_r[:], ones_in[:]),
                    (eye8[:], eye8_in[:]), (sel2[:], sel2_in[:]),
                    (rwsb[:], rw_t[:]), (rbsb[:], rb_in[:]), (biassb[:], bias_t[:]),
                ]:
                    dma_c.inc(sync.dma_start(out=dst, in_=src), 16)
                dma_c.wait_now(sync)

            @block.vector
            def _(vector):
                vector.memset(ones_f[:], 1.0)

            @block.scalar
            def _(scalar):
                dma_c.wait_now(scalar)
                scalar.activation(out=warm_b[:, 0:1], in_=rbsb[0:1, 0:1],
                                  func=AF.Square)
                scalar.activation(out=warm_b[:, 1:2], in_=rbsb[0:1, 0:1],
                                  func=AF.Sqrt)

        es1 = contextlib.ExitStack()
        with es1:
            xnH = es1.enter_context(nc.sbuf_tensor("xnH", [128, DC * 2 * HALO], BF16))
            xstg0 = es1.enter_context(nc.sbuf_tensor("xstg0", [128, D], F32))
            xstg1 = es1.enter_context(nc.sbuf_tensor("xstg1", [128, D], F32))
            scr_act = es1.enter_context(nc.sbuf_tensor("scr_act", [128, D], F32))
            xn_tok0 = es1.enter_context(nc.sbuf_tensor("xn_tok0", [128, D], BF16))
            xn_tok1 = es1.enter_context(nc.sbuf_tensor("xn_tok1", [128, D], BF16))
            cosf = es1.enter_context(nc.sbuf_tensor("cosf", [128, SH], F32))
            sinf = es1.enter_context(nc.sbuf_tensor("sinf", [128, SH], F32))
            xstg = [xstg0, xstg1]
            xn_tok = [xn_tok0, xn_tok1]

            def xnT_dst(t, g):
                if 2 <= t <= 9:
                    return r3(actT, OWN)[:, 4 * g:4 * g + 4,
                                         (t - 2) * 128:(t - 2) * 128 + 128]
                hoff = {0: 0, 1: 128, 10: 256, 11: 384}[t]
                return r3(xnH, 2 * HALO)[:, 4 * g:4 * g + 4, hoff:hoff + 128]

            es2 = contextlib.ExitStack()
            with es2:
                trp0 = es2.enter_context(nc.psum_tensor("trp0", [128, 4 * 1024], BF16))
                trp1 = es2.enter_context(nc.psum_tensor("trp1", [128, 4 * 1024], BF16))
                block = es2.enter_context(nc.Block("p1"))
                trp = [trp0, trp1]

                @block.sync
                def _(sync):
                    dma_c.inc(sync.dma_start(out=cosf[:], in_=cos_in[:]), 16)
                    dma_c.inc(sync.dma_start(out=sinf[:], in_=sin_in[:]), 16)
                    dma_c.mark("cossin")
                    for t in range(NT):
                        if t >= 2:
                            dve.wait_mark(sync, f"p1app{t - 2}")
                        dxp[t % 2].inc(sync.dma_start(
                            out=xstg[t % 2][:],
                            in_=x_halo[t * 128:(t + 1) * 128, :]), 16)
                        dxp[t % 2].mark(f"p1x{t}")
                    if DEBUG:
                        dve.wait_mark(sync, "dbg_t0")
                        dma_o.inc(sync.dma_start(out=d_xstg[:], in_=scr_act[:]), 16)
                        dma_o.inc(sync.dma_start(out=d_stat[:], in_=m2c[:, 0:8]), 16)
                        dma_o.inc(sync.dma_start(out=d_xntok[:], in_=xn_tok[0][:]), 16)
                        dve.wait_mark(sync, f"p1ev{NT - 1}g7")
                        dma_o.inc(sync.dma_start(out=d_xnt[:], in_=actT[:]), 16)
                        dma_o.inc(sync.dma_start(out=d_xnh[:], in_=xnH[:]), 16)

                @block.scalar
                def _(scalar):
                    for t in range(NT):
                        dxp[t % 2].wait_mark(scalar, f"p1x{t}")
                        if t >= 1:
                            dve.wait_mark(scalar, f"p1v{t - 1}")
                        scalar.activation(
                            out=scr_act[:], in_=xstg[t % 2][:], func=AF.Square,
                            accum_out=ssq_b[:])
                        act.inc(scalar.drain())
                        act.mark(f"p1sq{t}")
                        dve.wait_mark(scalar, f"p1var{t}")
                        scalar.activation(
                            out=std_b[:], in_=var_b[:], func=AF.Sqrt)
                        act.inc(scalar.drain())
                        act.mark(f"p1std{t}")

                @block.vector
                def _(vector):
                    for t in range(NT):
                        dxp[t % 2].wait_mark(vector, f"p1x{t}")
                        vector.reduce_sum(out=sum_b[:], in_=xstg[t % 2][:], axis=AX.X)
                        vector.drain()
                        vector.tensor_scalar(
                            out=negm_b[:], in0=sum_b[:], scalar1=-1.0 / D,
                            scalar2=None, op0=ALU.mult)
                        vector.drain()
                        act.wait_mark(vector, f"p1sq{t}")
                        vector.tensor_scalar(
                            out=var_b[:], in0=ssq_b[:], scalar1=1.0 / D,
                            scalar2=EPS, op0=ALU.mult, op1=ALU.add)
                        dve.inc(vector.tensor_tensor(
                            out=tmp1_b[:], in0=negm_b[:], in1=negm_b[:], op=ALU.mult))
                        dve.mark(f"p1v{t}")
                        vector.drain()
                        vector.tensor_tensor(
                            out=var_b[:], in0=var_b[:], in1=tmp1_b[:], op=ALU.subtract)
                        dve.inc(vector.drain())
                        dve.mark(f"p1var{t}")
                        act.wait_mark(vector, f"p1std{t}")
                        vector.reciprocal(out=rstd_b[:], in_=std_b[:])
                        vector.drain()
                        if t >= 2:
                            pe.wait_mark(vector, f"p1tr{t - 2}")
                        dve.inc(vector.tensor_scalar(
                            out=xn_tok[t % 2][:], in0=xstg[t % 2][:],
                            scalar1=negm_b[:], scalar2=rstd_b[:],
                            op0=ALU.add, op1=ALU.mult))
                        dve.mark(f"p1app{t}")
                        if DEBUG and t == 0:
                            vector.tensor_copy(out=m2c[:, 3:4], in_=rstd_b[:])
                            dve.inc(vector.drain())
                            dve.mark("dbg_t0")
                        for g in range(8):
                            pe.wait_mark(vector, f"p1tg{t}g{g}")
                            dve.inc(vector.tensor_copy(
                                out=xnT_dst(t, g),
                                in_=r3(trp[g % 2], 1024)[:, :, 0:128]))
                            dve.mark(f"p1ev{t}g{g}")

                @block.tensor
                def _(tensor):
                    for t in range(NT):
                        dve.wait_mark(tensor, f"p1app{t}")
                        for g in range(8):
                            if 8 * t + g >= 2:
                                pg = 8 * t + g - 2
                                dve.wait_mark(tensor, f"p1ev{pg // 8}g{pg % 8}")
                            for j in range(4):
                                kc = 4 * g + j
                                pe.inc(tensor.transpose(
                                    out=trp[g % 2][:, j * 1024:j * 1024 + 128],
                                    in_=xn_tok[t % 2][:, kc * 128:(kc + 1) * 128],
                                    identity=ident[:]))
                            pe.mark(f"p1tg{t}g{g}")
                        pe.mark(f"p1tr{t}")

            GRP = [(0, 256), (256, 512), (768, 512), (1280, 256)]

            def kv_rhs(kc, gi):
                if gi == 0:
                    return xnH[:, kc * 2 * HALO: kc * 2 * HALO + 256]
                if gi == 1:
                    return actT[:, kc * OWN: kc * OWN + 512]
                if gi == 2:
                    return actT[:, kc * OWN + 512: kc * OWN + 1024]
                return xnH[:, kc * 2 * HALO + 256: kc * 2 * HALO + 512]

            es3 = contextlib.ExitStack()
            with es3:
                wk_sb = es3.enter_context(nc.sbuf_tensor("wk_sb", [128, DC * 128], BF16))
                wv_sb = es3.enter_context(nc.sbuf_tensor("wv_sb", [128, DC * 128], BF16))
                kps2 = es3.enter_context(nc.psum_tensor("kps2", [128, 512], F32))
                vps2 = es3.enter_context(nc.psum_tensor("vps2", [128, 512], F32))
                trp3 = es3.enter_context(nc.psum_tensor("trp3", [128, 4 * 1024], BF16))
                block = es3.enter_context(nc.Block("p2"))
                @block.gpsimd
                def _(gpsimd):
                    dwm.inc(gpsimd.dma_start(out=wk_sb[:], in_=wk_t[:]), 16)
                    dwm.inc(gpsimd.dma_start(out=wv_sb[:], in_=wv_t[:]), 16)
                    dwm.mark("wkv")

                @block.sync
                def _(sync):
                    if DEBUG:
                        dve.wait_mark(sync, "p2kr3")
                        dve.wait_mark(sync, "p2ve3")
                        dma_o.inc(sync.dma_start(out=d_kr[:], in_=kr[:]), 16)
                        dma_o.inc(sync.dma_start(out=d_vtok[:], in_=vtok[:]), 16)

                @block.tensor
                def _(tensor):
                    dwm.wait_mark(tensor, "wkv")
                    dma_c.wait_mark(tensor, "cossin")
                    for gi, (l0, n) in enumerate(GRP):
                        if gi >= 1:
                            dve.wait_mark(tensor, f"p2kr{gi - 1}")
                        for kc in range(DC):
                            pe.inc(tensor.matmul(
                                out=kps2[:, 0:n],
                                lhsT=wk_sb[:, kc * 128:(kc + 1) * 128],
                                rhs=kv_rhs(kc, gi),
                                start=(kc == 0), stop=(kc == DC - 1)))
                        pe.mark(f"p2k{gi}")
                    for gi, (l0, n) in enumerate(GRP):
                        if gi >= 1:
                            dve.wait_mark(tensor, f"p2vs{gi - 1}")
                        for kc in range(DC):
                            pe.inc(tensor.matmul(
                                out=vps2[:, 0:n],
                                lhsT=wv_sb[:, kc * 128:(kc + 1) * 128],
                                rhs=kv_rhs(kc, gi),
                                start=(kc == 0), stop=(kc == DC - 1)))
                        pe.mark(f"p2v{gi}")
                        dve.wait_mark(tensor, f"p2vs{gi}")
                        if gi >= 1:
                            dve.wait_mark(tensor, f"p2ve{gi - 1}")
                        for j in range(n // 128):
                            pe.inc(tensor.transpose(
                                out=trp3[:, j * 1024:j * 1024 + 128],
                                in_=xn_tok0[:, j * 128:(j + 1) * 128],
                                identity=ident[:]))
                        pe.mark(f"p2vt{gi}")

                @block.vector
                def _(vector):
                    for gi, (l0, n) in enumerate(GRP):
                        pe.wait_mark(vector, f"p2k{gi}")
                        vector.tensor_tensor(
                            out=xstg0[:, 0:n], in0=kps2[:, 0:n],
                            in1=cosf[:, l0:l0 + n], op=ALU.mult)
                        vector.tensor_tensor(
                            out=xstg1[0:64, 0:n], in0=kps2[64:128, 0:n],
                            in1=sinf[0:64, l0:l0 + n], op=ALU.mult)
                        vector.tensor_tensor(
                            out=xstg1[64:128, 0:n], in0=kps2[0:64, 0:n],
                            in1=sinf[64:128, l0:l0 + n], op=ALU.mult)
                        dve.inc(vector.tensor_tensor(
                            out=kr[:, l0:l0 + n], in0=xstg0[:, 0:n],
                            in1=xstg1[:, 0:n], op=ALU.add))
                        dve.mark(f"p2kr{gi}")
                    for gi, (l0, n) in enumerate(GRP):
                        pe.wait_mark(vector, f"p2v{gi}")
                        if gi >= 1:
                            pe.wait_mark(vector, f"p2vt{gi - 1}")
                        vector.tensor_copy(
                            out=xn_tok0[:, 0:n], in_=vps2[:, 0:n])
                        dve.inc(vector.drain())
                        dve.mark(f"p2vs{gi}")
                        pe.wait_mark(vector, f"p2vt{gi}")
                        nt_g = n // 128
                        dve.inc(vector.tensor_copy(
                            out=r3(vtok, 128)[:, l0 // 128:l0 // 128 + nt_g, :],
                            in_=r3(trp3, 1024)[:, 0:nt_g, 0:128]))
                        dve.mark(f"p2ve{gi}")

        def prev_hm(h, m):
            return (h, m - 1) if m > 0 else (h - 1, 4)

        es_fin = contextlib.ExitStack()
        fin = es_fin.enter_context(nc.sbuf_tensor("fin", [128, H * OWN], BF16, side="right"))
        es4 = contextlib.ExitStack()
        with es4:
            cosq = es4.enter_context(nc.sbuf_tensor("cosq", [128, OWN], F32))
            sinq = es4.enter_context(nc.sbuf_tensor("sinq", [128, OWN], F32))
            alp = es4.enter_context(nc.sbuf_tensor("alp", [128, OWN], BF16))
            alm = es4.enter_context(nc.sbuf_tensor("alm", [128, OWN], BF16))
            kmask = es4.enter_context(nc.sbuf_tensor("kmask", [128, 256], BF16))
            wqb0 = es4.enter_context(nc.sbuf_tensor("wqb0", [128, D], BF16))
            wqb1 = es4.enter_context(nc.sbuf_tensor("wqb1", [128, D], BF16))
            qr0 = es4.enter_context(nc.sbuf_tensor("qr0", [128, OWN], BF16))
            qr1 = es4.enter_context(nc.sbuf_tensor("qr1", [128, OWN], BF16))
            sg0 = es4.enter_context(nc.sbuf_tensor("sg0", [128, 2048], BF16))
            sg1 = es4.enter_context(nc.sbuf_tensor("sg1", [128, 2048], BF16))
            wout = es4.enter_context(nc.sbuf_tensor("wout", [128, 5 * 512], BF16))
            recip_sb = es4.enter_context(nc.sbuf_tensor("recip_sb", [128, 512], F32))
            rps1 = es4.enter_context(nc.sbuf_tensor("rps1", [128, 512], F32))
            qev = es4.enter_context(nc.sbuf_tensor("qev", [128, 1024], F32))
            qev2 = es4.enter_context(nc.sbuf_tensor("qev2", [128, 1024], F32))
            rps2 = es4.enter_context(nc.sbuf_tensor("rps2", [128, 512], F32))
            bls1 = es4.enter_context(nc.sbuf_tensor("bls1", [128, OWN], F32))
            bls2 = es4.enter_context(nc.sbuf_tensor("bls2", [128, OWN], F32))
            qps = es4.enter_context(nc.psum_tensor("qps", [128, 1024], F32))
            scps = es4.enter_context(nc.psum_tensor("scps", [128, 1024], F32))
            bcps = es4.enter_context(nc.psum_tensor("bcps", [128, 1024], F32))
            avps = es4.enter_context(nc.psum_tensor("avps", [128, 1024], F32))
            block = es4.enter_context(nc.Block("p3"))
            wqb = [wqb0, wqb1]
            qrb = [qr0, qr1]
            sgb = [sg0, sg1]

            @block.sync
            def _(sync):
                dma_c.inc(sync.dma_start(out=cosq[:], in_=cos_in[:, HALO:HALO + OWN]), 16)
                dma_c.inc(sync.dma_start(out=sinq[:], in_=sin_in[:, HALO:HALO + OWN]), 16)
                dma_c.inc(sync.dma_start(out=alp[:], in_=alpha_in[:]), 16)
                dma_c.inc(sync.dma_start(out=alm[:], in_=alm1_in[:]), 16)
                dma_c.inc(sync.dma_start(out=kmask[:], in_=kmask_in[:]), 16)
                dma_c.mark("p3c")
                if DEBUG:
                    pool.wait_mark(sync, f"blend{H - 1}")
                    dma_o.inc(sync.dma_start(out=d_fin[:], in_=fin[:]), 16)

            @block.gpsimd
            def _(gpsimd):
                dma_c.wait_mark(gpsimd, "p3c")

                def rope_pool(hh, g):
                    dve.wait_mark(gpsimd, f"qev{hh}g{g}")
                    gpsimd.tensor_tensor(
                        out=rps1[:], in0=qev[:, g * 512:(g + 1) * 512],
                        in1=cosq[:, g * 512:(g + 1) * 512], op=ALU.mult)
                    gpsimd.tensor_tensor(
                        out=rps2[:], in0=qev2[:, g * 512:(g + 1) * 512],
                        in1=sinq[:, g * 512:(g + 1) * 512], op=ALU.mult)
                    if hh >= 1 and g == 0:
                        pe.wait_mark(gpsimd, f"scm{hh - 1}m4p1")
                    pool.inc(gpsimd.tensor_tensor(
                        out=qrb[hh % 2][:, g * 512:(g + 1) * 512],
                        in0=rps1[:], in1=rps2[:], op=ALU.add))
                    pool.mark(f"rope{hh}g{g}")

                for h in range(-2, H):
                    hh = h + 2
                    if hh < H:
                        if hh >= 2:
                            pe.wait_mark(gpsimd, f"qmm{hh - 2}")
                        dwp[hh % 2].inc(gpsimd.dma_start(
                            out=wqb[hh % 2][:], in_=wq_t[hh, :, :]), 16)
                        dwp[hh % 2].mark(f"wq{hh}")
                    if 0 <= h + 1 < H:
                        rope_pool(h + 1, 0)
                        rope_pool(h + 1, 1)
                    if h >= 0:
                        dve.wait_mark(gpsimd, f"wev{h}m4")
                        prev = r3(wout, 512)[:, 0:4, 256:512]
                        cur = r3(wout, 512)[:, 1:5, 0:256]
                        alm_v = r3(alm, 256)[:, :, :]
                        alp_v = r3(alp, 256)[:, :, :]
                        gpsimd.tensor_tensor(
                            out=r3(bls1, 256)[:, :, :], in0=prev, in1=alm_v, op=ALU.mult)
                        gpsimd.tensor_tensor(
                            out=r3(bls2, 256)[:, :, :], in0=cur, in1=alp_v, op=ALU.mult)
                        pool.inc(gpsimd.tensor_tensor(
                            out=fin[:, h * OWN:(h + 1) * OWN],
                            in0=bls1[:], in1=bls2[:], op=ALU.add))
                        pool.mark(f"blend{h}")

            @block.tensor
            def _(tensor):
                dma_c.wait_mark(tensor, "p3c")

                def emit_q(hh, g, kc):
                    if kc == 0:
                        dwp[hh % 2].wait_mark(tensor, f"wq{hh}")
                        if hh >= 1:
                            dve.wait_mark(tensor, f"qev{hh - 1}g{g}")
                    pe.inc(tensor.matmul(
                        out=qps[:, g * 512:(g + 1) * 512],
                        lhsT=wqb[hh % 2][:, kc * 128:(kc + 1) * 128],
                        rhs=actT[:, kc * OWN + g * 512: kc * OWN + (g + 1) * 512],
                        start=(kc == 0), stop=(kc == DC - 1)))
                    if kc == DC - 1:
                        pe.mark(f"q{hh}g{g}")
                        if g == 1:
                            pe.mark(f"qmm{hh}")

                for g in range(2):
                    for kc in range(DC):
                        emit_q(0, g, kc)

                for h in range(H):
                    fillers = ([(g, kc) for g in range(2) for kc in range(DC)]
                               if h + 1 < H else [])
                    fpos = [0]

                    def fill(n):
                        while n > 0 and fpos[0] < len(fillers):
                            g, kc = fillers[fpos[0]]
                            emit_q(h + 1, g, kc)
                            fpos[0] += 1
                            n -= 1

                    pool.wait_mark(tensor, f"rope{h}g1")
                    for m in range(5):
                        qlo, qn = QCOLS[m]
                        koff = 256 * m
                        sg = sgb[m % 2]
                        for p in range(2):
                            if not (h == 0 and m == 0 and p == 0):
                                if p == 1:
                                    act.wait_mark(tensor, f"sg{h}m{m}p0")
                                else:
                                    ph, pm = prev_hm(h, m)
                                    act.wait_mark(tensor, f"sg{ph}m{pm}p1")
                            for i in (2 * p, 2 * p + 1):
                                pe.inc(tensor.matmul(
                                    out=scps[:, (i - 2 * p) * 512:(i - 2 * p) * 512 + qn],
                                    lhsT=kr[:, koff + i * 128: koff + (i + 1) * 128],
                                    rhs=qrb[h % 2][:, qlo:qlo + qn],
                                    start=True, stop=True))
                            pe.mark(f"scm{h}m{m}p{p}")
                            fill(10)
                            act.wait_mark(tensor, f"sg{h}m{m}p{p}")
                            if p == 0 and (5 * h + m >= 2):
                                pk = 5 * h + m - 2
                                dve.wait_mark(tensor, f"recip{pk // 5}m{pk % 5}")
                            for i in (2 * p, 2 * p + 1):
                                masked = (m == 4 and i >= 2)
                                pe.inc(tensor.matmul(
                                    out=bcps[:, (m % 2) * 512:(m % 2) * 512 + qn],
                                    lhsT=kmask[:, 128:256] if masked else kmask[:, 0:128],
                                    rhs=sg[:, i * 512: i * 512 + qn],
                                    start=(i == 0), stop=(i == 3)))
                        pe.mark(f"bc{h}m{m}")
                        if 5 * h + m >= 2:
                            pk = 5 * h + m - 2
                            dve.wait_mark(tensor, f"wev{pk // 5}m{pk % 5}")
                        for i in range(4):
                            tt = 2 * m + i
                            pe.inc(tensor.matmul(
                                out=avps[:, (m % 2) * 512:(m % 2) * 512 + qn],
                                lhsT=vtok[:, tt * 128:(tt + 1) * 128],
                                rhs=sg[:, i * 512: i * 512 + qn],
                                start=(i == 0), stop=(i == 3)))
                        pe.mark(f"av{h}m{m}")
                    fill(len(fillers))

            @block.scalar
            def _(scalar):
                scalar.activation(out=warm_b[:, 2:3], in_=rbsb[0:1, 0:1],
                                  func=AF.Sigmoid)
                for h in range(H):
                    for m in range(5):
                        qlo, qn = QCOLS[m]
                        sg = sgb[m % 2]
                        for p in range(2):
                            pe.wait_mark(scalar, f"scm{h}m{m}p{p}")
                            if p == 0 and not (h == 0 and m <= 1):
                                ph, pm = (h, m - 2) if m >= 2 else (h - 1, m + 3)
                                pe.wait_mark(scalar, f"av{ph}m{pm}")
                            act.inc(scalar.activation(
                                out=r3(sg, 512)[:, 2 * p:2 * p + 2, 0:qn],
                                in_=r3(scps, 512)[:, 0:2, 0:qn],
                                func=AF.Sigmoid, scale=SCALE))
                            act.mark(f"sg{h}m{m}p{p}")

            @block.vector
            def _(vector):
                dma_c.wait_mark(vector, "p3c")

                def emit_qev(hh, g):
                    pe.wait_mark(vector, f"q{hh}g{g}")
                    vector.tensor_copy(
                        out=qev[:, g * 512:(g + 1) * 512],
                        in_=qps[:, g * 512:(g + 1) * 512])
                    vector.tensor_copy(
                        out=qev2[0:64, g * 512:(g + 1) * 512],
                        in_=qps[64:128, g * 512:(g + 1) * 512])
                    dve.inc(vector.tensor_copy(
                        out=qev2[64:128, g * 512:(g + 1) * 512],
                        in_=qps[0:64, g * 512:(g + 1) * 512]))
                    dve.mark(f"qev{hh}g{g}")

                emit_qev(0, 0)
                emit_qev(0, 1)
                for h in range(H):
                    for m in range(5):
                        qlo, qn = QCOLS[m]
                        c0 = 256 if m == 0 else 0
                        if m == 1 and h + 1 < H:
                            emit_qev(h + 1, 0)
                        if m == 3 and h + 1 < H:
                            emit_qev(h + 1, 1)
                        pe.wait_mark(vector, f"bc{h}m{m}")
                        dve.inc(vector.reciprocal(
                            out=recip_sb[:, 0:qn],
                            in_=bcps[:, (m % 2) * 512:(m % 2) * 512 + qn]))
                        dve.mark(f"recip{h}m{m}")
                        pe.wait_mark(vector, f"av{h}m{m}")
                        if m == 0 and h >= 1:
                            pool.wait_mark(vector, f"blend{h - 1}")
                        dve.inc(vector.tensor_tensor(
                            out=wout[:, m * 512 + c0: m * 512 + c0 + qn],
                            in0=avps[:, (m % 2) * 512:(m % 2) * 512 + qn],
                            in1=recip_sb[:, 0:qn], op=ALU.mult))
                        dve.mark(f"wev{h}m{m}")


        with nc.sbuf_tensor("ao_tok", [128, NT_OWN * D], BF16) as ao_tok:
            stage4 = kr

            es5 = contextlib.ExitStack()
            with es5:
                wob = es5.enter_context(nc.sbuf_tensor("wob", [128, 2 * 1024], BF16))
                wops = es5.enter_context(nc.psum_tensor("wops", [128, 1024], F32))
                trp4 = es5.enter_context(nc.psum_tensor("trp4", [128, 4 * 1024], BF16))
                block = es5.enter_context(nc.Block("p4"))
                @block.sync
                def _(sync):
                    if DEBUG:
                        dve.wait_mark(sync, f"p4ev{DC - 1}g1")
                        dma_o.inc(sync.dma_start(out=d_ao[:], in_=ao_tok[:]), 16)

                @block.gpsimd
                def _(gpsimd):
                    for oc in range(DC):
                        for q4 in range(4):
                            k = oc * 4 + q4
                            if k >= 2:
                                pe.wait_mark(gpsimd, f"woq{k - 2}")
                            dwp[k % 2].inc(gpsimd.dma_start(
                                out=wob[:, (k % 2) * 1024:(k % 2 + 1) * 1024],
                                in_=wo_t[oc, :, q4 * 1024:(q4 + 1) * 1024]), 16)
                            dwp[k % 2].mark(f"wo{k}")

                @block.tensor
                def _(tensor):
                    for oc in range(DC):
                        for q4 in range(4):
                            k = oc * 4 + q4
                            dwp[k % 2].wait_mark(tensor, f"wo{k}")
                            if q4 == 0 and oc >= 1:
                                act.wait_mark(tensor, f"woev{oc - 1}g1")
                            for f8 in range(8):
                                fc = 8 * q4 + f8
                                for g in range(2):
                                    pe.inc(tensor.matmul(
                                        out=wops[:, g * 512:(g + 1) * 512],
                                        lhsT=wob[:, (k % 2) * 1024 + f8 * 128:
                                                  (k % 2) * 1024 + (f8 + 1) * 128],
                                        rhs=fin[:, fc * OWN + g * 512:
                                                fc * OWN + (g + 1) * 512],
                                        start=(fc == 0), stop=(fc == DC - 1)))
                            pe.mark(f"woq{k}")
                        pe.mark(f"womm{oc}")
                        for g in range(2):
                            act.wait_mark(tensor, f"woev{oc}g{g}")
                            if 2 * oc + g >= 1:
                                pk = 2 * oc + g - 1
                                dve.wait_mark(tensor, f"p4ev{pk // 2}g{pk % 2}")
                            for j in range(4):
                                pe.inc(tensor.transpose(
                                    out=trp4[:, j * 1024:j * 1024 + 128],
                                    in_=stage4[:, g * 512 + j * 128:
                                               g * 512 + (j + 1) * 128],
                                    identity=ident[:]))
                            pe.mark(f"wotr{oc}g{g}")

                @block.scalar
                def _(scalar):
                    scalar.activation(out=warm_b[:, 3:4], in_=rbsb[0:1, 0:1],
                                      func=AF.Sqrt)
                    for oc in range(DC):
                        pe.wait_mark(scalar, f"womm{oc}")
                        for g in range(2):
                            if oc >= 1:
                                pe.wait_mark(scalar, f"wotr{oc - 1}g{g}")
                            scalar.copy(
                                out=stage4[:, g * 512:(g + 1) * 512],
                                in_=wops[:, g * 512:(g + 1) * 512])
                            act.inc(scalar.drain())
                            act.mark(f"woev{oc}g{g}")

                @block.vector
                def _(vector):
                    for oc in range(DC):
                        for g in range(2):
                            pe.wait_mark(vector, f"wotr{oc}g{g}")
                            dve.inc(vector.tensor_copy(
                                out=r3(ao_tok, D)[:, 4 * g:4 * g + 4,
                                                  oc * 128:(oc + 1) * 128],
                                in_=r3(trp4, 1024)[:, :, 0:128]))
                            dve.mark(f"p4ev{oc}g{g}")

            es_fin.close()

            es6 = contextlib.ExitStack()
            with es6:
                xstg5 = es6.enter_context(nc.sbuf_tensor("xstg5", [128, D], F32))
                xm5 = es6.enter_context(nc.sbuf_tensor("xm5", [128, D], F32))
                scr5 = es6.enter_context(nc.sbuf_tensor("scr5", [128, D], F32))
                xn3s = es6.enter_context(nc.sbuf_tensor("xn3s", [128, D], BF16))
                trp5 = es6.enter_context(nc.psum_tensor("trp5", [128, 4 * 1024], BF16))
                block = es6.enter_context(nc.Block("p5"))
                @block.sync
                def _(sync):
                    for t in range(NT_OWN):
                        if t >= 1:
                            dve.wait_mark(sync, f"p5xm{t - 1}")
                        dxp[0].inc(sync.dma_start(
                            out=xstg5[:],
                            in_=x_halo[(t + 2) * 128:(t + 3) * 128, :]), 16)
                        dxp[0].mark(f"p5x{t}")
                    if t == NT_OWN - 1 and DEBUG:
                        dve.wait_mark(sync, f"p5ev{NT_OWN - 1}g7")
                        dma_o.inc(sync.dma_start(out=d_xn3t[:], in_=actT[:]), 16)

                @block.scalar
                def _(scalar):
                    for t in range(NT_OWN):
                        ao_v = ao_tok[:, t * D:(t + 1) * D]
                        if t >= 1:
                            dve.wait_mark(scalar, f"p5v3{t - 1}")
                        scalar.activation(
                            out=scr5[:], in_=ao_v, func=AF.Square, accum_out=ssq_b[:])
                        act.inc(scalar.drain())
                        act.mark(f"p5sq2{t}")
                        dve.wait_mark(scalar, f"p5var2{t}")
                        scalar.activation(
                            out=std_b[:], in_=var_b[:], func=AF.Sqrt)
                        act.inc(scalar.drain())
                        act.mark(f"p5std2{t}")
                        dve.wait_mark(scalar, f"p5xm{t}")
                        scalar.activation(
                            out=scr5[:], in_=xm5[:], func=AF.Square, accum_out=ssq_b[:])
                        act.inc(scalar.drain())
                        act.mark(f"p5sq3{t}")
                        dve.wait_mark(scalar, f"p5var3{t}")
                        scalar.activation(
                            out=std_b[:], in_=var_b[:], func=AF.Sqrt)
                        act.inc(scalar.drain())
                        act.mark(f"p5std3{t}")

                @block.vector
                def _(vector):
                    for t in range(NT_OWN):
                        ao_v = ao_tok[:, t * D:(t + 1) * D]
                        vector.reduce_sum(out=sum_b[:], in_=ao_v, axis=AX.X)
                        vector.drain()
                        vector.tensor_scalar(
                            out=m2c[:, t:t + 1], in0=sum_b[:], scalar1=-1.0 / D,
                            scalar2=None, op0=ALU.mult)
                        vector.drain()
                        act.wait_mark(vector, f"p5sq2{t}")
                        vector.tensor_scalar(
                            out=var_b[:], in0=ssq_b[:], scalar1=1.0 / D,
                            scalar2=EPS, op0=ALU.mult, op1=ALU.add)
                        vector.tensor_tensor(
                            out=tmp1_b[:], in0=m2c[:, t:t + 1], in1=m2c[:, t:t + 1],
                            op=ALU.mult)
                        vector.drain()
                        vector.tensor_tensor(
                            out=var_b[:], in0=var_b[:], in1=tmp1_b[:], op=ALU.subtract)
                        dve.inc(vector.drain())
                        dve.mark(f"p5var2{t}")
                        act.wait_mark(vector, f"p5std2{t}")
                        vector.reciprocal(out=r2c[:, t:t + 1], in_=std_b[:])
                        vector.drain()
                        vector.tensor_scalar(
                            out=xm5[:], in0=ao_v, scalar1=m2c[:, t:t + 1],
                            scalar2=r2c[:, t:t + 1], op0=ALU.add, op1=ALU.mult)
                        dxp[0].wait_mark(vector, f"p5x{t}")
                        dve.inc(vector.tensor_tensor(
                            out=xm5[:], in0=xm5[:], in1=xstg5[:], op=ALU.add))
                        dve.mark(f"p5xm{t}")
                        vector.reduce_sum(out=sum_b[:], in_=xm5[:], axis=AX.X)
                        vector.drain()
                        vector.tensor_scalar(
                            out=negm_b[:], in0=sum_b[:], scalar1=-1.0 / D,
                            scalar2=None, op0=ALU.mult)
                        vector.drain()
                        act.wait_mark(vector, f"p5sq3{t}")
                        vector.tensor_scalar(
                            out=var_b[:], in0=ssq_b[:], scalar1=1.0 / D,
                            scalar2=EPS, op0=ALU.mult, op1=ALU.add)
                        vector.tensor_tensor(
                            out=tmp1_b[:], in0=negm_b[:], in1=negm_b[:], op=ALU.mult)
                        vector.drain()
                        vector.tensor_tensor(
                            out=var_b[:], in0=var_b[:], in1=tmp1_b[:], op=ALU.subtract)
                        dve.inc(vector.drain())
                        dve.mark(f"p5var3{t}")
                        dve.mark(f"p5v3{t}")
                        act.wait_mark(vector, f"p5std3{t}")
                        vector.reciprocal(out=rstd_b[:], in_=std_b[:])
                        vector.drain()
                        if t >= 1:
                            pe.wait_mark(vector, f"p5tr{t - 1}")
                        dve.inc(vector.tensor_scalar(
                            out=xn3s[:], in0=xm5[:], scalar1=negm_b[:],
                            scalar2=rstd_b[:], op0=ALU.add, op1=ALU.mult))
                        dve.mark(f"p5app{t}")
                        for g in range(8):
                            pe.wait_mark(vector, f"p5tg{t}g{g}")
                            dve.inc(vector.tensor_copy(
                                out=r3(actT, OWN)[:, 4 * g:4 * g + 4,
                                                  t * 128:(t + 1) * 128],
                                in_=r3(trp5, 1024)[:, :, 0:128]))
                            dve.mark(f"p5ev{t}g{g}")

                @block.tensor
                def _(tensor):
                    for t in range(NT_OWN):
                        dve.wait_mark(tensor, f"p5app{t}")
                        for g in range(8):
                            if 8 * t + g >= 1:
                                pg = 8 * t + g - 1
                                dve.wait_mark(tensor, f"p5ev{pg // 8}g{pg % 8}")
                            for j in range(4):
                                kc = 4 * g + j
                                pe.inc(tensor.transpose(
                                    out=trp5[:, j * 1024:j * 1024 + 128],
                                    in_=xn3s[:, kc * 128:(kc + 1) * 128],
                                    identity=ident[:]))
                            pe.mark(f"p5tg{t}g{g}")
                        pe.mark(f"p5tr{t}")

            es7 = contextlib.ExitStack()
            with es7:
                wmask = es7.enter_context(nc.sbuf_tensor("wmask", [128, OWN], BF16))
                mskscr = es7.enter_context(nc.sbuf_tensor("mskscr", [128, OWN], BF16))
                a_sb = es7.enter_context(nc.sbuf_tensor("a_sb", [128, DC * 128], BF16))
                traw = es7.enter_context(nc.sbuf_tensor("traw", [128, OWN], F32))
                lgps = es7.enter_context(nc.psum_tensor("lgps", [8, 1], F32))
                ltps = es7.enter_context(nc.psum_tensor("ltps", [1, 8], F32))
                bc8ps = es7.enter_context(nc.psum_tensor("bc8ps", [128, 8], F32))
                tps = es7.enter_context(nc.psum_tensor("tps", [128, 1024], F32))
                block = es7.enter_context(nc.Block("p6"))
                @block.sync
                def _(sync):
                    dma_c.inc(sync.dma_start(out=wmask[:], in_=wmask_in[:]), 16)
                    dma_c.mark("p6c")
                    if DEBUG:
                        dve.wait_mark(sync, "router_partial")
                        dma_o.inc(sync.dma_start(out=d_rt[:], in_=router_sb[:]), 16)
                        dve.wait_mark(sync, "tsc")
                        dma_o.inc(sync.dma_start(out=d_qs[:], in_=qsum_sb[:]), 16)
                        dma_o.inc(sync.dma_start(out=d_rtg[:], in_=rtg_sb[:]), 16)
                        dma_o.inc(sync.dma_start(out=d_srt[:], in_=srt_sb[:]), 16)
                        dma_o.inc(sync.dma_start(out=d_tsc[:], in_=tsc_sb[:]), 16)

                @block.gpsimd
                def _(gpsimd):
                    dwm.inc(gpsimd.dma_start(out=a_sb[:], in_=a_t[:]), 16)
                    dwm.mark("a_sb")
                    dve.wait_mark(gpsimd, "router_partial")
                    dwm.inc(gpsimd.dma_start(out=cc_in[:], in_=router_sb[:]), 16)
                    dwm.wait_now(gpsimd)
                    ccs.inc(gpsimd.collective_compute(
                        "AllReduce", ALU.add,
                        replica_groups=[list(range(NCORE))],
                        ins=[cc_in.ap().opt()],
                        outs=[cc_out.ap().opt()]), 1)
                    ccs.wait_now(gpsimd)
                    dwm.inc(gpsimd.dma_start(out=qsum_sb[:], in_=cc_out[:]), 16)
                    dwm.mark("qsum")

                @block.tensor
                def _(tensor):
                    dwm.wait_mark(tensor, "qsum")
                    for kc in range(DC):
                        pe.inc(tensor.matmul(
                            out=lgps[:], lhsT=rwsb[:, kc * 8:(kc + 1) * 8],
                            rhs=qsum_sb[:, kc:kc + 1],
                            start=(kc == 0), stop=(kc == DC - 1)))
                    pe.mark("logits")
                    dve.wait_mark(tensor, "lg_sb")
                    pe.inc(tensor.matmul(
                        out=ltps[:], lhsT=logit_sb[:], rhs=eye8[:],
                        start=True, stop=True))
                    pe.mark("logitsT")
                    dve.wait_mark(tensor, "routing")
                    pe.inc(tensor.matmul(
                        out=bc8ps[:], lhsT=ones_f[:], rhs=rtg_sb[:],
                        start=True, stop=True))
                    pe.mark("bc8")
                    dwm.wait_mark(tensor, "a_sb")
                    for g in range(2):
                        for kc in range(DC):
                            pe.inc(tensor.matmul(
                                out=tps[:, g * 512:(g + 1) * 512],
                                lhsT=a_sb[:, kc * 128:(kc + 1) * 128],
                                rhs=actT[:, kc * OWN + g * 512: kc * OWN + (g + 1) * 512],
                                start=(kc == 0), stop=(kc == DC - 1)))
                    pe.mark("tmm")

                @block.vector
                def _(vector):
                    dma_c.wait_mark(vector, "p6c")
                    for kc in range(DC):
                        vector.scalar_tensor_tensor(
                            out=mskscr[:], in0=actT[:, kc * OWN:(kc + 1) * OWN],
                            scalar=0.0, in1=wmask[:], op0=ALU.add, op1=ALU.mult,
                            accum_out=router_sb[:, kc:kc + 1])
                    dve.inc(vector.drain(), DC)
                    dve.mark("router_partial")
                    pe.wait_mark(vector, "logits")
                    vector.tensor_scalar(
                        out=logit_sb[:], in0=lgps[:], scalar1=1.0 / NSEL,
                        scalar2=rbsb[:], op0=ALU.mult, op1=ALU.add)
                    dve.inc(vector.drain())
                    dve.mark("lg_sb")
                    pe.wait_mark(vector, "logitsT")
                    vector.tensor_copy(out=rtmp_sb[:], in_=ltps[:])
                    vector.drain()
                    vector.reduce_max(out=rmax_sb[:], in_=rtmp_sb[:], axis=AX.X)
                    vector.drain()
                    vector.tensor_scalar(
                        out=rmax_sb[:], in0=rmax_sb[:], scalar1=-1.0,
                        scalar2=None, op0=ALU.mult)
                    dve.inc(vector.drain())
                    dve.mark("sftm_pre")
                    act.wait_mark(vector, "sftm_exp")
                    vector.reduce_sum(out=rsum1_sb[:], in_=rtmp_sb[:], axis=AX.X)
                    vector.drain()
                    vector.reciprocal(out=rsum1_sb[:], in_=rsum1_sb[:])
                    vector.drain()
                    vector.tensor_scalar(
                        out=rtg_sb[:], in0=rtmp_sb[:], scalar1=rsum1_sb[:],
                        scalar2=None, op0=ALU.mult)
                    dve.inc(vector.drain())
                    dve.mark("routing")
                    pe.wait_mark(vector, "bc8")
                    vector.tensor_tensor(
                        out=bc8_sb[:], in0=bc8ps[:], in1=sel2[:], op=ALU.mult)
                    vector.drain()
                    vector.tensor_reduce(
                        out=srt_sb[:], in_=bc8_sb[:], axis=AX.X, op=ALU.add)
                    vector.drain()
                    pe.wait_mark(vector, "tmm")
                    vector.tensor_copy(out=traw[:], in_=tps[:])
                    vector.drain()
                    dve.inc(vector.tensor_scalar(
                        out=tsc_sb[:], in0=traw[:], scalar1=srt_sb[:],
                        scalar2=None, op0=ALU.mult))
                    dve.mark("tsc")

                @block.scalar
                def _(scalar):
                    scalar.activation(out=warm_b[:, 4:5], in_=rbsb[0:1, 0:1],
                                      func=AF.Exp)
                    dve.wait_mark(scalar, "sftm_pre")
                    scalar.activation(
                        out=rtmp_sb[:], in_=rtmp_sb[:], func=AF.Exp,
                        bias=rmax_sb[:])
                    act.inc(scalar.drain())
                    act.mark("sftm_exp")

            es8 = contextlib.ExitStack()
            with es8:
                ffh = es8.enter_context(nc.sbuf_tensor("ffh", [128, 4 * D], BF16))
                wfb = es8.enter_context(nc.sbuf_tensor("wfb", [128, 4 * 1024], BF16))
                b_sb = es8.enter_context(nc.sbuf_tensor("b_sb", [128, DC * 128], BF16))
                xb9 = es8.enter_context(nc.sbuf_tensor("xb9", [128, 1024], F32))
                s9a0 = es8.enter_context(nc.sbuf_tensor("s9a0", [128, 1024], F32))
                s9a1 = es8.enter_context(nc.sbuf_tensor("s9a1", [128, 1024], F32))
                s9b = es8.enter_context(nc.sbuf_tensor("s9b", [128, 1024], F32))
                junk9 = es8.enter_context(nc.sbuf_tensor("junk9", [128, 1024], BF16))
                st9 = es8.enter_context(nc.sbuf_tensor("st9", [128, 4], F32))
                ffps = es8.enter_context(nc.psum_tensor("ffps", [128, 1024], F32))
                trp8 = es8.enter_context(nc.psum_tensor("trp8", [128, 4 * 1024], BF16))
                s9a = [s9a0, s9a1]
                stage8 = kr

                for hf in range(2):
                    with nc.Block(f"p8h{hf}") as block:
                        @block.sync
                        def _(sync, hf=hf):
                            if DEBUG:
                                dve.wait_mark(sync, f"p8ev{hf}o{DC - 1}")
                                dma_o.inc(sync.dma_start(
                                    out=(d_ffh0 if hf == 0 else d_ffh1)[:],
                                    in_=ffh[:]), 16)

                        @block.gpsimd
                        def _(gpsimd, hf=hf):
                            if hf == 0:
                                dwm.inc(gpsimd.dma_start(out=b_sb[:], in_=b_t[:]), 16)
                                dwm.mark("b_sb")
                            for oc in range(DC):
                                for q4 in range(4):
                                    k = oc * 4 + q4
                                    if k >= 4:
                                        pe.wait_mark(gpsimd, f"f{hf}q{k - 4}")
                                    dwp[k % 2].inc(gpsimd.dma_start(
                                        out=wfb[:, (k % 4) * 1024:(k % 4 + 1) * 1024],
                                        in_=wf_t[oc, :, q4 * 1024:(q4 + 1) * 1024]), 16)
                                    dwp[k % 2].mark(f"wf{hf}o{oc}q{q4}")

                        @block.tensor
                        def _(tensor, hf=hf):
                            dwm.wait_mark(tensor, "b_sb")
                            for oc in range(DC):
                                fo = (oc % 2) * 512
                                for q4 in range(4):
                                    k = oc * 4 + q4
                                    dwp[k % 2].wait_mark(tensor, f"wf{hf}o{oc}q{q4}")
                                    if q4 == 0 and (hf * DC + oc >= 2):
                                        po = (hf * DC + oc - 2) % DC
                                        ph = (hf * DC + oc - 2) // DC
                                        act.wait_mark(tensor, f"ffev{ph}o{po}")
                                    for f8 in range(8):
                                        kc = 8 * q4 + f8
                                        pe.inc(tensor.matmul(
                                            out=ffps[:, fo:fo + 512],
                                            lhsT=wfb[:, (k % 4) * 1024 + f8 * 128:
                                                      (k % 4) * 1024 + (f8 + 1) * 128],
                                            rhs=actT[:, kc * OWN + hf * 512:
                                                     kc * OWN + hf * 512 + 512],
                                            start=(kc == 0), stop=False))
                                    pe.mark(f"f{hf}q{k}")
                                pe.inc(tensor.matmul(
                                    out=ffps[:, fo:fo + 512],
                                    lhsT=b_sb[:, oc * 128:(oc + 1) * 128],
                                    rhs=tsc_sb[:, hf * 512: hf * 512 + 512],
                                    start=False, stop=True))
                                pe.mark(f"ffmm{hf}o{oc}")
                                act.wait_mark(tensor, f"ffev{hf}o{oc}")
                                if hf + oc >= 1:
                                    po = oc - 1 if oc >= 1 else DC - 1
                                    ph = hf if oc >= 1 else hf - 1
                                    dve.wait_mark(tensor, f"p8ev{ph}o{po}")
                                for j in range(4):
                                    pe.inc(tensor.transpose(
                                        out=trp8[:, j * 1024:j * 1024 + 128],
                                        in_=stage8[:, j * 128:(j + 1) * 128],
                                        identity=ident[:]))
                                pe.mark(f"fftr{hf}o{oc}")

                        @block.scalar
                        def _(scalar, hf=hf):
                            if hf == 0:
                                scalar.activation(out=warm_b[:, 5:6],
                                                  in_=rbsb[0:1, 0:1], func=AF.Sqrt)
                            for oc in range(DC):
                                pe.wait_mark(scalar, f"ffmm{hf}o{oc}")
                                if hf + oc >= 1:
                                    po = oc - 1 if oc >= 1 else DC - 1
                                    ph = hf if oc >= 1 else hf - 1
                                    pe.wait_mark(scalar, f"fftr{ph}o{po}")
                                scalar.activation(
                                    out=stage8[:, 0:512],
                                    in_=ffps[:, (oc % 2) * 512:(oc % 2) * 512 + 512],
                                    func=AF.Identity, bias=biassb[:, oc:oc + 1])
                                act.inc(scalar.drain())
                                act.mark(f"ffev{hf}o{oc}")

                        @block.vector
                        def _(vector, hf=hf):
                            for oc in range(DC):
                                pe.wait_mark(vector, f"fftr{hf}o{oc}")
                                dve.inc(vector.tensor_copy(
                                    out=r3(ffh, D)[:, :, oc * 128:(oc + 1) * 128],
                                    in_=r3(trp8, 1024)[:, :, 0:128]))
                                dve.mark(f"p8ev{hf}o{oc}")

                    with nc.Block(f"p9h{hf}") as block:
                        @block.sync
                        def _(sync, hf=hf):
                            for k in range(17):
                                tl, cb = divmod(k, 4)
                                T = 4 * hf + tl
                                if k < 16:
                                    if k >= 1:
                                        dve.wait_mark(sync, f"p9xm{hf}k{k - 1}")
                                    dxp[0].inc(sync.dma_start(
                                        out=xb9[:],
                                        in_=x_halo[(T + 2) * 128:(T + 3) * 128,
                                                   cb * 1024:(cb + 1) * 1024]), 16)
                                    dxp[0].mark(f"p9x{hf}t{tl}c{cb}")
                                if k >= 1:
                                    kx = k - 1
                                    ptl, pcb = divmod(kx, 4)
                                    PT = 4 * hf + ptl
                                    dve.wait_mark(sync, f"p9f{hf}k{kx}")
                                    dop[kx % 2].inc(sync.dma_start(
                                        out=out[PT * 128:(PT + 1) * 128,
                                                pcb * 1024:(pcb + 1) * 1024],
                                        in_=s9a[kx % 2][:]), 16)
                                    dop[kx % 2].mark(f"p9o{hf}k{kx}")

                        @block.scalar
                        def _(scalar, hf=hf):
                            for tl in range(4):
                                ff_t3 = r3(ffh, 1024)[:, 4 * tl:4 * tl + 4, :]
                                if hf + tl >= 1:
                                    dve.wait_mark(scalar, f"p9st{hf}t{tl - 1}"
                                                  if tl >= 1 else f"p9st{hf - 1}t3")
                                for cb in range(4):
                                    scalar.activation(
                                        out=junk9[:], in_=ffh[:, (4 * tl + cb) * 1024:
                                                             (4 * tl + cb + 1) * 1024],
                                        func=AF.Square, accum_out=st9[:, cb:cb + 1])
                                act.inc(scalar.drain(), 4)
                                act.mark(f"p9sq{hf}t{tl}")
                                dve.wait_mark(scalar, f"p9var{hf}t{tl}")
                                scalar.activation(
                                    out=std_b[:], in_=var_b[:], func=AF.Sqrt)
                                act.inc(scalar.drain())
                                act.mark(f"p9std{hf}t{tl}")

                        @block.vector
                        def _(vector, hf=hf):
                            for tl in range(4):
                                T = 4 * hf + tl
                                ff_v = ffh[:, tl * D:(tl + 1) * D]
                                vector.reduce_sum(out=sum_b[:], in_=ff_v, axis=AX.X)
                                vector.drain()
                                vector.tensor_scalar(
                                    out=negm_b[:], in0=sum_b[:], scalar1=-1.0 / D,
                                    scalar2=None, op0=ALU.mult)
                                vector.drain()
                                act.wait_mark(vector, f"p9sq{hf}t{tl}")
                                vector.tensor_reduce(
                                    out=ssq_b[:], in_=st9[:], axis=AX.X, op=ALU.add)
                                vector.drain()
                                vector.tensor_scalar(
                                    out=var_b[:], in0=ssq_b[:], scalar1=1.0 / D,
                                    scalar2=EPS, op0=ALU.mult, op1=ALU.add)
                                vector.tensor_tensor(
                                    out=tmp1_b[:], in0=negm_b[:], in1=negm_b[:],
                                    op=ALU.mult)
                                vector.drain()
                                vector.tensor_tensor(
                                    out=var_b[:], in0=var_b[:], in1=tmp1_b[:],
                                    op=ALU.subtract)
                                dve.inc(vector.drain())
                                dve.mark(f"p9var{hf}t{tl}")
                                dve.mark(f"p9st{hf}t{tl}")
                                act.wait_mark(vector, f"p9std{hf}t{tl}")
                                vector.reciprocal(out=rstd_b[:], in_=std_b[:])
                                vector.drain()
                                for cb in range(4):
                                    kx = 4 * tl + cb
                                    sa = s9a[kx % 2]
                                    if kx >= 2:
                                        dop[kx % 2].wait_mark(vector, f"p9o{hf}k{kx - 2}")
                                    vector.tensor_scalar(
                                        out=sa[:],
                                        in0=ao_tok[:, T * D + cb * 1024:
                                                   T * D + (cb + 1) * 1024],
                                        scalar1=m2c[:, T:T + 1], scalar2=r2c[:, T:T + 1],
                                        op0=ALU.add, op1=ALU.mult)
                                    dxp[0].wait_mark(vector, f"p9x{hf}t{tl}c{cb}")
                                    dve.inc(vector.tensor_tensor(
                                        out=sa[:], in0=sa[:], in1=xb9[:], op=ALU.add))
                                    dve.mark(f"p9xm{hf}k{kx}")
                                    vector.tensor_scalar(
                                        out=s9b[:],
                                        in0=ffh[:, tl * D + cb * 1024:
                                                tl * D + (cb + 1) * 1024],
                                        scalar1=negm_b[:], scalar2=rstd_b[:],
                                        op0=ALU.add, op1=ALU.mult)
                                    vector.tensor_tensor(
                                        out=sa[:], in0=sa[:], in1=s9b[:], op=ALU.add)
                                    dve.inc(vector.drain())
                                    dve.mark(f"p9f{hf}k{kx}")


        with nc.Block("fin") as block:
            @block.sync
            def _(sync):
                dop[0].wait_now(sync)
                dop[1].wait_now(sync)
                if DEBUG:
                    dma_o.wait_now(sync)

    return nc



def _host_prep(L0_attn_x, L0_attn_Wq, L0_attn_Wk, L0_attn_Wv, L0_attn_Wo,
               L0_ffn_W, L0_ffn_b, L0_ffn_A, L0_ffn_B,
               L0_ffn_router_W, L0_ffn_router_b):
    f32 = np.float32
    bf16 = ml_dtypes.bfloat16

    def tile_w(W):
        return np.ascontiguousarray(
            W.reshape(DC, 128, DC, 128).transpose(0, 3, 2, 1).reshape(DC, 128, DC * 128))

    def tile_small(Wsm):
        return np.ascontiguousarray(
            Wsm.reshape(128, DC, 128).transpose(2, 1, 0).reshape(128, DC * 128))

    wq = tile_w(np.asarray(L0_attn_Wq, f32))
    wo = tile_w(np.asarray(L0_attn_Wo, f32))
    wf = tile_w(np.asarray(L0_ffn_W, f32))
    wk = tile_small(np.asarray(L0_attn_Wk, f32))
    wv = tile_small(np.asarray(L0_attn_Wv, f32))
    a_cat = np.asarray(L0_ffn_A, f32).reshape(128, D)
    a_tl = tile_small(a_cat)
    b_cat = np.asarray(L0_ffn_B, f32).transpose(1, 0, 2).reshape(D, 128)
    b_tl = np.ascontiguousarray(
        b_cat.reshape(DC, 128, 128).transpose(2, 0, 1).reshape(128, DC * 128))
    rw = np.ascontiguousarray(
        np.asarray(L0_ffn_router_W, f32).T.reshape(DC, 128, 8).transpose(1, 0, 2)
        .reshape(128, DC * 8))
    rb = np.ascontiguousarray(np.asarray(L0_ffn_router_b, f32).reshape(8, 1))
    bias = np.ascontiguousarray(np.asarray(L0_ffn_b, f32).reshape(DC, 128).T)

    inv_freq = (1.0 / 10000.0 ** (np.arange(0, DH, 2, dtype=f32) / DH)).astype(f32)
    t_pos = np.arange(S, dtype=f32)
    f = np.outer(t_pos, inv_freq).astype(f32)
    cosg = np.zeros((S, DH), f32)
    sing = np.zeros((S, DH), f32)
    cosg[:, 0::2] = np.cos(f)
    cosg[:, 1::2] = np.cos(f)
    sing[:, 0::2] = np.sin(f)
    sing[:, 1::2] = np.sin(f)

    alpha_blk = np.linspace(0.0, 1.0, 256, dtype=f32)

    xf = np.asarray(L0_attn_x, f32).reshape(S, D)
    xpad = np.pad(xf, ((HALO, HALO), (0, 0)))

    ident = np.eye(128, dtype=bf16)
    ones_r = np.ones((1, 128), dtype=bf16)
    eye8 = np.eye(8, dtype=f32)
    sel2 = np.zeros((128, 8), f32)
    for p in range(128):
        sel2[p, p // 16] = 2.0

    in_maps = []
    for c in range(NCORE):
        g0 = c * OWN
        xh = np.ascontiguousarray(xpad[g0:g0 + SH])
        pos = np.clip(np.arange(g0 - HALO, g0 - HALO + SH), 0, S - 1)
        cos_c = np.ascontiguousarray(cosg[pos].T)
        sin_c = cosg[pos].T * 0 + sing[pos].T
        sin_c[0:64, :] *= -1.0
        sin_c = np.ascontiguousarray(sin_c.astype(f32))
        al = np.tile(alpha_blk, 4).astype(f32)
        if c == 0:
            al[0:256] = 1.0
        alm = (np.float32(1.0) - al).astype(f32)
        al_t = np.ascontiguousarray(np.broadcast_to(al, (128, OWN))).astype(bf16)
        alm_t = np.ascontiguousarray(np.broadcast_to(alm, (128, OWN))).astype(bf16)
        km = np.ones((128, 256), np.float32)
        if c == NCORE - 1:
            km[:, 128:256] = 0.0
        km = km.astype(bf16)
        wm = np.zeros(OWN, np.float32)
        gidx = np.arange(g0, g0 + OWN)
        wm[(gidx >= 611) & (gidx <= S - 2)] = 1.0
        wm_t = np.ascontiguousarray(np.broadcast_to(wm, (128, OWN))).astype(bf16)
        in_maps.append({
            "x_halo": xh, "wq_t": wq, "wo_t": wo, "wf_t": wf,
            "wk_t": wk, "wv_t": wv, "a_t": a_tl, "b_t": b_tl,
            "rw_t": rw, "rb_in": rb, "bias_t": bias,
            "cos_in": cos_c, "sin_in": sin_c,
            "alpha_in": al_t, "alm1_in": alm_t,
            "kmask_in": km, "wmask_in": wm_t,
            "ident_in": ident, "ones_in": ones_r, "eye8_in": eye8,
            "sel2_in": sel2,
        })
    return in_maps


def kernel(L0_attn_x, L0_attn_Wq, L0_attn_Wk, L0_attn_Wv, L0_attn_Wo,
           L0_ffn_W, L0_ffn_b, L0_ffn_A, L0_ffn_B,
           L0_ffn_router_W, L0_ffn_router_b):
    in_maps = _host_prep(L0_attn_x, L0_attn_Wq, L0_attn_Wk, L0_attn_Wv,
                         L0_attn_Wo, L0_ffn_W, L0_ffn_b, L0_ffn_A, L0_ffn_B,
                         L0_ffn_router_W, L0_ffn_router_b)
    marks = {}
    build_program(marks, final_pass=False)
    nc = build_program(marks, final_pass=True)
    res = run_bass_kernel_spmd(nc, in_maps, core_ids=list(range(NCORE)))
    full = np.concatenate(
        [np.asarray(res.results[c]["out"], dtype=np.float32) for c in range(NCORE)],
        axis=0)
    return full.reshape(1, S, D)
